# version 11
# baseline (speedup 1.0000x reference)
import numpy as np
import concourse.bass as bass
import concourse.mybir as mybir
from concourse.bass_utils import run_bass_kernel_spmd

F32 = mybir.dt.float32
BF16 = mybir.dt.bfloat16
I32 = mybir.dt.int32
AF = mybir.ActivationFunctionType
ALU = mybir.AluOpType
AX = mybir.AxisListType

MAGIC = 1.5 * 2.0 ** 23
T, D = 4096, 1024
NT = T // 128
NCORES = 8
H0, H1 = 4096, 3072
NHEAD, HL = 16, 64
LN_EPS = 1e-5
ASC = 0.125

BUILD_UPTO = "full"


def set_shape(t, ncores=8):
    global T, NT, NCORES
    T = t
    NT = t // 128
    NCORES = ncores


class Buf:
    __slots__ = ("name", "writer", "readers", "sem", "fills")

    def __init__(self, name):
        self.name = name
        self.writer = None
        self.readers = {}
        self.sem = None
        self.fills = 0


class Prog:
    def __init__(self, nc):
        self.nc = nc
        self.eng = {"t": nc.tensor, "v": nc.vector, "s": nc.scalar,
                    "g": nc.gpsimd, "y": nc.sync}
        self.psem = {e: nc.alloc_semaphore(f"ps_{e}") for e in "tvsg"}
        self.pcnt = {e: 0 for e in "tvsg"}
        self.obs = {e: {} for e in "tvsgy"}
        self.bufs = {}
        self.nsem = 0

    def buf(self, n):
        if isinstance(n, Buf):
            return n
        b = self.bufs.get(n)
        if b is None:
            b = Buf(n)
            self.bufs[n] = b
        return b

    def _wait(self, e, sem, val):
        k = sem.num
        if self.obs[e].get(k, 0) >= val:
            return
        self.eng[e].wait_ge(sem, val)
        self.obs[e][k] = val

    def _gather(self, e, reads, writes):
        need = {}

        def add(evt):
            if evt is None:
                return
            sem, val = evt
            if need.get(sem.num, (None, 0))[1] < val:
                need[sem.num] = (sem, val)

        for b in reads:
            add(b.writer)
        for b in writes:
            add(b.writer)
            for evt in b.readers.values():
                add(evt)
        for sem, val in need.values():
            self._wait(e, sem, val)

    def op(self, e, fn, r=(), w=()):
        reads = [self.buf(x) for x in r]
        writes = [self.buf(x) for x in w]
        self._gather(e, reads, writes)
        ins = fn(self.eng[e])
        self.pcnt[e] += 1
        ins.then_inc(self.psem[e], 1)
        evt = (self.psem[e], self.pcnt[e])
        for b in writes:
            b.writer = evt
            b.readers = {}
        for b in reads:
            b.readers[("e", e)] = evt
        return ins

    def tt(self, e, out, a, b, alu, r=(), w=()):
        self.op(e, lambda E: E.tensor_tensor(out, a, b, alu), r, w)

    def ts(self, e, out, a, s1, s2, op0, op1=None, r=(), w=()):
        if s2 is None and op1 is None:
            self.op(e, lambda E: E.tensor_single_scalar(out, a, s1, op0), r, w)
        else:
            self.op(e, lambda E: E.tensor_scalar(out, a, s1, s2, op0, op1), r, w)

    def stt(self, e, out, a, s, b, op0, op1, r=(), w=()):
        self.op(e, lambda E: E.scalar_tensor_tensor(out, a, s, b, op0, op1), r, w)

    def cp(self, e, out, a, r=(), w=()):
        self.op(e, lambda E: E.tensor_copy(out, a), r, w)

    def recip(self, out, a, r=(), w=()):
        self.op("v", lambda E: E.reciprocal(out, a), r, w)

    def red(self, e, out, a, alu, r=(), w=(), absval=None):
        self.op(e, lambda E: E.tensor_reduce(out, a, AX.X, alu,
                apply_absolute_value=absval), r, w)

    def act(self, out, a, func, r=(), w=(), bias=0.0, scale=1.0, accum=None):
        if isinstance(bias, float) and bias != 0.0:
            bias = self.constc(bias)
            r = list(r) + ["cvals"]
        self.op("s", lambda E: E.activation(out, a, func, bias=bias, scale=scale,
                accum_out=accum), r, w)

    def mm(self, out, lhsT, rhs, start, stop, r=(), w=()):
        self.op("t", lambda E: E.matmul(out, lhsT, rhs, start=start, stop=stop), r, w)

    def mm_group(self, mms, r=(), w=()):
        reads = [self.buf(x) for x in r]
        writes = [self.buf(x) for x in w]
        self._gather("t", reads, writes)
        E = self.eng["t"]
        for (out, lhsT, rhs, start, stop) in mms:
            ins = E.matmul(out, lhsT, rhs, start=start, stop=stop)
        self.pcnt["t"] += 1
        ins.then_inc(self.psem["t"], 1)
        evt = (self.psem["t"], self.pcnt["t"])
        for b in writes:
            b.writer = evt
            b.readers = {}
        for b in reads:
            b.readers[("e", "t")] = evt
        return ins

    def stream_begin(self, w=()):
        self._stream_w = [self.buf(x) for x in w]
        self._gather("t", [], self._stream_w)
        self._stream_evt = None

    def stream_step(self, rbufs, mms):
        rb = [self.buf(x) for x in rbufs]
        self._gather("t", rb, [])
        E = self.eng["t"]
        ins = None
        for (out, lhsT, rhs, start, stop) in mms:
            ins = E.matmul(out, lhsT, rhs, start=start, stop=stop)
        self.pcnt["t"] += 1
        ins.then_inc(self.psem["t"], 1)
        evt = (self.psem["t"], self.pcnt["t"])
        for b in rb:
            b.readers[("e", "t")] = evt
        self._stream_evt = evt

    def stream_end(self):
        for b in self._stream_w:
            b.writer = self._stream_evt
            b.readers = {}

    def tr_group(self, trs, r=(), w=()):
        reads = [self.buf(x) for x in r]
        writes = [self.buf(x) for x in w]
        self._gather("t", reads, writes)
        E = self.eng["t"]
        for (out, a, idt) in trs:
            ins = E.transpose(out, a, idt)
        self.pcnt["t"] += 1
        ins.then_inc(self.psem["t"], 1)
        evt = (self.psem["t"], self.pcnt["t"])
        for b in writes:
            b.writer = evt
            b.readers = {}
        for b in reads:
            b.readers[("e", "t")] = evt
        return ins

    def trans(self, out, a, idt, r=(), w=()):
        self.op("t", lambda E: E.transpose(out, a, idt), r, w)

    def scan(self, e, out, d0, d1, init, r=(), w=()):
        self.op(e, lambda E: E.tensor_tensor_scan(out, d0, d1, init,
                ALU.mult, ALU.add), r, w)

    def mset(self, ap, val, w, r=()):
        self.op("g", lambda E: E.memset(ap, val), r, w)

    def afsel(self, ap, cmp, fill, pattern, chmul, r=(), w=()):
        self.op("g", lambda E: E.affine_select(out=ap, in_=ap, compare_op=cmp,
                fill=fill, base=0, pattern=pattern, channel_multiplier=chmul), r, w)

    def dma(self, e, out_ap, in_ap, out_buf, in_buf=None, slow=False):
        ob = self.buf(out_buf)
        rb = [self.buf(in_buf)] if in_buf is not None else []
        self._gather(e, rb, [ob])
        if ob.sem is None:
            self.nsem += 1
            ob.sem = self.nc.alloc_semaphore(f"dm{self.nsem}")
        ins = self.eng[e].dma_start(out=out_ap, in_=in_ap,
                                    allow_slow_non_contiguous=slow)
        ob.fills += 1
        ins.then_inc(ob.sem, 16)
        evt = (ob.sem, 16 * ob.fills)
        ob.writer = evt
        ob.readers = {}
        for b in rb:
            b.readers[("d", ob.name, ob.fills)] = evt
        return ins

    def final_wait(self, e, names):
        for n in names:
            b = self.buf(n)
            if b.writer is not None:
                self._wait(e, *b.writer)


class Arena:
    def __init__(self, prog, name, nbytes):
        self.p = prog
        self.t = prog.nc.alloc_sbuf_tensor(name, [128, nbytes], mybir.dt.uint8)
        self.nbytes = nbytes
        self.live = []
        self.gen = 0

    def alloc(self, name, lo, nbytes, dtype):
        hi = lo + nbytes
        assert hi <= self.nbytes, (name, lo, nbytes)
        self.gen += 1
        b = Buf(f"{name}#{self.gen}")
        keep = []
        for (l2, h2, b2) in self.live:
            if l2 < hi and lo < h2:
                if b2.writer is not None:
                    b.readers[("w", b2.name)] = b2.writer
                for k, evt in b2.readers.items():
                    b.readers[(b2.name,) + (k if isinstance(k, tuple) else (k,))] = evt
                if lo <= l2 and h2 <= hi:
                    continue
            keep.append((l2, h2, b2))
        keep.append((lo, hi, b))
        self.live = keep
        self.p.bufs[b.name] = b
        es = mybir.dt.size(dtype)
        ap = self.t[:, lo:hi].bitcast(dtype)
        return ap, b


def build_program(upto="full"):
    nc = bass.Bass("TRN2", target_bir_lowering=False, debug=False)
    p = Prog(nc)

    def din(name, shape, dt=F32):
        return nc.dram_tensor(name, shape, dt, kind="ExternalInput").ap()

    x_in = din("x", [T, D])
    wqkvT = din("wqkvT", [D, 3 * D])
    w_out = din("w_out", [D, D])
    w1T = din("w1T", [D, H0])
    b1v = din("b1v", [1, H0])
    w2T = din("w2T", [H0, D])
    b2v = din("b2v", [1, D])
    tdv = din("tdv", [1, D]); tfv = din("tfv", [1, D])
    wtkH = din("wtkH", [8 * 128, 1024])
    wtvH = din("wtvH", [8 * 128, 1024])
    wtrH = din("wtrH", [8 * 128, 1024])
    wckH = din("wckH", [32 * 128, 1024])
    wcv = din("wcv", [H0, D])
    wcr = din("wcr", [D, D])
    l1g = din("l1g", [1, D]); l1b = din("l1b", [1, D])
    l2g = din("l2g", [1, D]); l2b = din("l2b", [1, D])
    mxk = din("mxk", [1, D]); mxv = din("mxv", [1, D]); mxr = din("mxr", [1, D])
    cmk = din("cmk", [1, D]); cmr = din("cmr", [1, D])
    wgH = din("wgH", [24 * 128, 1024], I32)
    wuH = din("wuH", [24 * 128, 1024], I32)
    wdT = din("wdT", [H1, D], I32)
    out_t = nc.dram_tensor("o", [T, D], F32, kind="ExternalOutput").ap()

    x1_h = nc.dram_tensor("x1_h", [T, D], F32).ap()
    w1q_h = nc.dram_tensor("w1q_h", [D, H0], BF16).ap()
    acc_h = nc.dram_tensor("acc_h", [32, 128, T], F32).ap()
    w2q_h = nc.dram_tensor("w2q_h", [H0, D], BF16).ap()
    cc_in = nc.dram_tensor("cc_in", [1, 16], F32)
    cc_out = nc.dram_tensor("cc_out", [1, 16], F32)
    wtk_b = nc.dram_tensor("wtk_b", [8 * 128, 1024], BF16).ap()
    wtv_b = nc.dram_tensor("wtv_b", [8 * 128, 1024], BF16).ap()
    wtr_b = nc.dram_tensor("wtr_b", [8 * 128, 1024], BF16).ap()
    wck_b = nc.dram_tensor("wck_b", [32 * 128, 1024], BF16).ap()
    wcv_b = nc.dram_tensor("wcv_b", [H0, D], BF16).ap()
    wcr_b = nc.dram_tensor("wcr_b", [D, D], BF16).ap()
    wg_b = nc.dram_tensor("wg_b", [24 * 128, 1024], BF16).ap()
    wu_b = nc.dram_tensor("wu_b", [24 * 128, 1024], BF16).ap()
    wd_b = nc.dram_tensor("wd_b", [H1, D], BF16).ap()

    ident = nc.alloc_sbuf_tensor("ident", [128, 128], BF16)
    identf = nc.alloc_sbuf_tensor("identf", [128, 128], F32)
    ones_r = nc.alloc_sbuf_tensor("ones_r", [1, 128], F32)
    ones_cb = nc.alloc_sbuf_tensor("ones_cb", [128, 1], BF16)
    ones_rb1 = nc.alloc_sbuf_tensor("ones_rb1", [1, 128], BF16)
    sc0 = nc.alloc_sbuf_tensor("sc0", [128, NT], F32)
    sb0 = nc.alloc_sbuf_tensor("sb0", [128, NT], F32)
    qsc = nc.alloc_sbuf_tensor("qsc", [128, NT], F32)
    qsb = nc.alloc_sbuf_tensor("qsb", [128, NT], F32)
    qp = nc.alloc_sbuf_tensor("qp", [128, 12], F32)
    (QP_XHI, QP_XLO, QP_SMM1, QP_INVSH, QP_HHI, QP_HLO, QP_SMM2,
     QP_IW1, QP_IW2, QP_INVSX) = range(10)

    def qpc(i):
        return qp[:, i:i + 1]

    rowm = nc.alloc_sbuf_tensor("rowm", [128, 16], F32)
    row16 = nc.alloc_sbuf_tensor("row16", [1, 16], F32)
    roww = nc.alloc_sbuf_tensor("roww", [1, 16], F32)
    b1c = nc.alloc_sbuf_tensor("b1c", [128, 32], F32)
    b2b = nc.alloc_sbuf_tensor("b2b", [128, 1024], F32)
    chp = nc.alloc_sbuf_tensor("chp", [128, 8 * 16], F32)
    (NP_W, NP_EW, NP_U, NP_G1, NP_B1, NP_G2, NP_B2, NP_MK, NP_MV, NP_MR,
     NP_CK, NP_CR, NP_HB, NP_H2B, NP_EU) = range(15)

    def chcol(i, g):
        return chp[:, i * 8 + g:i * 8 + g + 1]

    wkvs = nc.alloc_sbuf_tensor("wkvs", [128, 32], F32)

    def ppe(g):
        return wkvs[:, g:g + 1]

    def aae(g):
        return wkvs[:, 8 + g:8 + g + 1]

    lnrow = nc.alloc_sbuf_tensor("lnrow", [128, 16], F32)

    PS0 = nc.alloc_psum_tensor("PS0", [128, 512], F32)
    PS1 = nc.alloc_psum_tensor("PS1", [128, 512], F32)
    PS2 = nc.alloc_psum_tensor("PS2", [128, 512], F32)
    PS3 = nc.alloc_psum_tensor("PS3", [128, 512], F32)
    PSY = nc.alloc_psum_tensor("PSY", [128, 512], F32)
    pS = nc.alloc_psum_tensor("pS", [64, 512], F32)
    ptrb = nc.alloc_psum_tensor("ptrb", [128, 1024], BF16)
    pb2 = nc.alloc_psum_tensor("pb2", [128, 512], F32)

    AR = Arena(p, "arena", 200 * 1024)

    cvals = nc.alloc_sbuf_tensor("cvals", [128, 8], F32)
    _creg = {}

    def constc(v):
        if v not in _creg:
            i = len(_creg)
            assert i < 8
            p.mset(cvals[:, i:i + 1], v, ["cvals"], r=["cvals"])
            _creg[v] = cvals[:, i:i + 1]
        return _creg[v]

    p.constc = constc

    p.mset(ident[:, :], 1.0, ["ident"])
    p.afsel(ident[:, :], ALU.is_equal, 0.0, [[1, 128]], -1, ["ident"], ["ident"])
    p.mset(identf[:, :], 1.0, ["identf"])
    p.afsel(identf[:, :], ALU.is_equal, 0.0, [[1, 128]], -1, ["identf"], ["identf"])
    p.mset(ones_r[:, :], 1.0, ["ones_r"])
    p.mset(ones_cb[:, :], 1.0, ["ones_cb"])
    p.mset(ones_rb1[:, :], 1.0, ["ones_rb1"])
    p.mset(wkvs[:, :], 0.0, ["wkvs"])
    for g in range(8):
        p.mset(ppe(g), -1e38, ["wkvs"], r=["wkvs"])

    for src, idx in [(tdv, NP_W), (tfv, NP_U), (l1g, NP_G1), (l1b, NP_B1),
                     (l2g, NP_G2), (l2b, NP_B2), (mxk, NP_MK), (mxv, NP_MV),
                     (mxr, NP_MR), (cmk, NP_CK), (cmr, NP_CR)]:
        p.dma("y", chp[:, idx * 8:(idx + 1) * 8],
              src.rearrange("one (g np) -> (one np) g", np=128), "chp", slow=True)
    p.act(chp[:, NP_EW * 8:(NP_EW + 1) * 8], chp[:, NP_W * 8:(NP_W + 1) * 8],
          AF.Exp, r=["chp"], w=["chp"])
    p.act(chp[:, NP_EU * 8:(NP_EU + 1) * 8], chp[:, NP_U * 8:(NP_U + 1) * 8],
          AF.Exp, r=["chp"], w=["chp"])
    p.mset(chp[:, NP_HB * 8:(NP_H2B + 1) * 8], 0.0, ["chp"], r=["chp"])
    p.dma("y", b1c[:, :], b1v.rearrange("one (c np) -> (one np) c", np=128), "b1c",
          slow=True)

    b2row, b2row_b = AR.alloc("b2row", 175 * 1024, 4096, F32)
    p.dma("y", b2row[0:1, :], b2v[0:1, :], b2row_b)
    for hh in range(2):
        p.op("t", lambda E, hh=hh: E.matmul(pb2[:, :], ones_r[0:1, :],
             b2row[0:1, hh * 512:(hh + 1) * 512], start=True, stop=True),
             r=[b2row_b, "ones_r"], w=["pb2"])
        p.cp("v", b2b[:, hh * 512:(hh + 1) * 512], pb2[:, :], r=["pb2"], w=["b2b"])

    def bcast_col(src_ap, dst_ap, rbufs, wbuf):
        p.op("t", lambda E: E.matmul(pb2[:, 0:1], ones_r[0:1, :], src_ap,
             start=True, stop=True), r=list(rbufs) + ["ones_r"], w=["pb2"])
        p.cp("v", dst_ap, pb2[:, 0:1], r=["pb2"], w=[wbuf])

    def part_max(src_ap, src_buf, dst_row_ap, dst_buf, tmpcol):
        p.trans(pb2[0:1, 0:128], src_ap, identf[:, :], r=[src_buf, "identf"], w=["pb2"])
        p.red("v", rowm[0:1, tmpcol:tmpcol + 1], pb2[0:1, 0:128], ALU.max,
              r=["pb2"], w=["rowm"])
        p.dma("g", dst_row_ap, rowm[0:1, tmpcol:tmpcol + 1], dst_buf, in_buf="rowm")

    xr = [AR.alloc(f"xrA{i}", i * 4096, 4096, F32) for i in range(4)]
    scrA, scrA_b = AR.alloc("scrA", 175 * 1024, 4096, F32)
    mnmx, mnmx_b = AR.alloc("mnmxA", 179 * 1024, 64, F32)
    p.mset(mnmx[:, 0:2], -1e38, [mnmx_b])

    wst = [AR.alloc(f"wstg{i}", 180 * 1024 + i * 4 * 1024, 4 * 1024, F32)
           for i in range(2)]
    wqo = [AR.alloc(f"wqo{i}", 188 * 1024 + i * 2 * 1024, 2 * 1024, BF16)
           for i in range(2)]
    wmx, wmx_b = AR.alloc("wmx", 192 * 1024, 64, F32)

    def quant_weight(wsrc, nchunks, X, hbm_dst, iw_slot, tag):
        iw_col = QP_IW1 if tag == "w1" else QP_IW2
        pieces = []
        nh = max(1, X // 1024)
        Xp = X // nh
        for ci in range(nchunks):
            for hswp in range(nh):
                pieces.append((ci, hswp * Xp, Xp))
        p.mset(wmx[:, 0:1], 0.0, [wmx_b])
        for pi, (ci, c0, Xp_) in enumerate(pieces):
            a, b = wst[pi % 2]
            p.dma("y", a[:, 0:Xp_], wsrc[ci * 128:(ci + 1) * 128, c0:c0 + Xp_], b)
            p.red("v", wmx[:, 1:2], a[:, 0:Xp_], ALU.max, r=[b], w=[wmx_b], absval=True)
            p.tt("v", wmx[:, 0:1], wmx[:, 0:1], wmx[:, 1:2], ALU.max, r=[wmx_b], w=[wmx_b])
        part_max(wmx[:, 0:1], wmx_b, roww[0:1, iw_slot:iw_slot + 1], f"sw{tag}", 12)
        p.ts("v", roww[0:1, 4 + iw_slot:5 + iw_slot], roww[0:1, iw_slot:iw_slot + 1],
             1.0 / 127.0, 1e-8, ALU.mult, ALU.max, r=[f"sw{tag}", "roww"], w=["roww"])
        p.recip(roww[0:1, 6:7], roww[0:1, 4 + iw_slot:5 + iw_slot], r=["roww"], w=["roww"])
        bcast_col(roww[0:1, 6:7], qpc(iw_col), ["roww"], "qp")
        for pi, (ci, c0, Xp_) in enumerate(pieces):
            a, b = wst[pi % 2]
            qa, qb = wqo[pi % 2]
            p.dma("y", a[:, 0:Xp_], wsrc[ci * 128:(ci + 1) * 128, c0:c0 + Xp_], b)
            p.ts("v", a[:, 0:Xp_], a[:, 0:Xp_], qpc(iw_col), MAGIC, ALU.mult, ALU.add,
                 r=[b, "qp"], w=[b])
            p.ts("v", a[:, 0:Xp_], a[:, 0:Xp_], MAGIC, 127.0, ALU.subtract, ALU.min,
                 r=[b], w=[b])
            p.ts("v", qa[:, 0:Xp_], a[:, 0:Xp_], -127.0, None, ALU.max, r=[b], w=[qb])
            p.dma("g", hbm_dst[ci * 128:(ci + 1) * 128, c0:c0 + Xp_], qa[:, 0:Xp_],
                  f"{tag}h", in_buf=qb)


    wqv, wqv_b = AR.alloc("wqv", 16 * 1024, 48 * 1024, BF16)
    wou, wou_b = AR.alloc("wou", 64 * 1024, 16 * 1024, BF16)
    tri4, tri4_b = AR.alloc("tri4", 80 * 1024, 1024, BF16)
    S_f, S_f_b = AR.alloc("S_f", 81 * 1024, 4096, F32)
    S_b_, S_b_b = AR.alloc("S_b", 85 * 1024, 2048, BF16)
    carry_f, carry_f_b = AR.alloc("carry_f", 87 * 1024, 4096, F32)
    carry_bb, carry_b_b = AR.alloc("carry_bb", 91 * 1024, 2048, BF16)
    def _cset(base, tag):
        ns = {}
        off = [("xnb", 0, 2048, BF16), ("xnf", 2048, 2048, BF16),
               ("eqs", 4096, 4096, F32), ("keb", 8192, 2048, BF16),
               ("lnk", 10240, 4096, F32), ("scC", 14336, 4096, F32),
               ("qsbf", 18432, 2048, BF16), ("qsf", 20480, 4096, BF16),
               ("kef", 24576, 4096, BF16), ("vtm", 28672, 2048, BF16),
               ("atb", 30720, 1024, BF16), ("atb2", 31744, 1024, BF16),
               ("ytm", 32768, 2048, BF16), ("yf", 34816, 2048, BF16),
               ("s16", 36864, 128, F32), ("x1p", 37376, 4096, F32)]
        for nm, o, nb, dt in off:
            ns[nm] = AR.alloc(f"{nm}{tag}", base + o, nb, dt)
        return ns

    csets = [_cset(93 * 1024, "0"), _cset(134 * 1024, "1")]

    p.mset(tri4[:, :], 1.0, [tri4_b])
    for i in range(4):
        p.afsel(tri4[:, i * 128:(i + 1) * 128], ALU.is_ge, 0.0, [[1, 128]], -1,
                [tri4_b], [tri4_b])
    p.mset(S_f[:, :], 0.0, [S_f_b])
    p.mset(S_b_[:, :], 0.0, [S_b_b])
    p.mset(carry_f[:, :], 0.0, [carry_f_b])
    p.mset(carry_bb[:, :], 0.0, [carry_b_b])

    for dc in range(8):
        p.dma("g", wqv[:, dc * 3072:(dc + 1) * 3072],
              wqkvT[dc * 128:(dc + 1) * 128, :], wqv_b)
        p.dma("g", wou[:, dc * 1024:(dc + 1) * 1024],
              w_out[dc * 128:(dc + 1) * 128, :], wou_b)
    for dst, srcw, nm in [(wtk_b, wtkH, "wtkb"), (wtv_b, wtvH, "wtvb"),
                          (wtr_b, wtrH, "wtrb"), (wck_b, wckH, "wckb"),
                          (wcv_b, wcv, "wcvb"), (wcr_b, wcr, "wcrb"),
                          (wg_b, wgH, "wgb"), (wu_b, wuH, "wub"),
                          (wd_b, wdT, "wdb")]:
        p.dma("g", dst, srcw, nm)

    def wq_ap(dc, lo, n):
        return wqv[:, dc * 3072 + lo:dc * 3072 + lo + n]

    for j in range(NT):
        cs = csets[j % 2]
        (xnb, xnb_b) = cs["xnb"]; (xnf, xnf_b) = cs["xnf"]
        (eqs, eqs_b) = cs["eqs"]; (keb, keb_b) = cs["keb"]
        (lnk, lnk_b) = cs["lnk"]; (scC, scC_b) = cs["scC"]
        (qsbf, qsbf_b) = cs["qsbf"]; (qsf, qsf_b) = cs["qsf"]
        (kef, kef_b) = cs["kef"]; (vtm, vtm_b) = cs["vtm"]
        (atb, atb_b) = cs["atb"]; (atb2, atb2_b) = cs["atb2"]
        (ytm, ytm_b) = cs["ytm"]; (yf, yf_b) = cs["yf"]
        (s16, s16_b) = cs["s16"]; (x1p, x1p_b) = cs["x1p"]
        xa, xb = xr[j % 4]
        p.dma("g", xa, x_in[j * 128:(j + 1) * 128, :], xb)
        p.red("v", rowm[:, 0:1], xa, ALU.add, r=[xb], w=["rowm"])
        p.act(scrA, xa, AF.Square, r=[xb], w=[scrA_b, "rowm"], accum=rowm[:, 1:2])
        p.red("v", rowm[:, 2:3], xa, ALU.max, r=[xb], w=["rowm"])
        p.red("v", rowm[:, 3:4], xa, ALU.min, r=[xb], w=["rowm"])
        p.ts("v", rowm[:, 4:5], rowm[:, 0:1], 1.0 / D, None, ALU.mult, r=["rowm"], w=["rowm"])
        p.ts("v", rowm[:, 5:6], rowm[:, 1:2], 1.0 / D, None, ALU.mult, r=["rowm"], w=["rowm"])
        p.tt("v", rowm[:, 6:7], rowm[:, 4:5], rowm[:, 4:5], ALU.mult, r=["rowm"], w=["rowm"])
        p.tt("v", rowm[:, 6:7], rowm[:, 5:6], rowm[:, 6:7], ALU.subtract, r=["rowm"], w=["rowm"])
        p.act(rowm[:, 7:8], rowm[:, 6:7], AF.Ln, r=["rowm"], w=["rowm"], bias=LN_EPS)
        p.act(sc0[:, j:j + 1], rowm[:, 7:8], AF.Exp, r=["rowm"], w=["sc0"], scale=-0.5)
        p.tt("v", sb0[:, j:j + 1], rowm[:, 4:5], sc0[:, j:j + 1], ALU.mult,
             r=["rowm", "sc0"], w=["sb0"])
        p.ts("v", sb0[:, j:j + 1], sb0[:, j:j + 1], -1.0, None, ALU.mult, r=["sb0"], w=["sb0"])
        p.tt("v", rowm[:, 8:9], rowm[:, 3:4], rowm[:, 4:5], ALU.subtract, r=["rowm"], w=["rowm"])
        p.tt("v", rowm[:, 8:9], rowm[:, 8:9], sc0[:, j:j + 1], ALU.mult,
             r=["rowm", "sc0"], w=["rowm"])
        p.ts("v", rowm[:, 8:9], rowm[:, 8:9], -1.0, None, ALU.mult, r=["rowm"], w=["rowm"])
        p.tt("v", mnmx[:, 0:1], mnmx[:, 0:1], rowm[:, 8:9], ALU.max,
             r=["rowm", mnmx_b], w=[mnmx_b])
        p.tt("v", rowm[:, 9:10], rowm[:, 2:3], rowm[:, 4:5], ALU.subtract, r=["rowm"], w=["rowm"])
        p.tt("v", rowm[:, 9:10], rowm[:, 9:10], sc0[:, j:j + 1], ALU.mult,
             r=["rowm", "sc0"], w=["rowm"])
        p.tt("v", mnmx[:, 1:2], mnmx[:, 1:2], rowm[:, 9:10], ALU.max,
             r=["rowm", mnmx_b], w=[mnmx_b])

        p.act(xnb, xa, AF.Identity, r=[xb, "sc0", "sb0"], w=[xnb_b],
              bias=sb0[:, j:j + 1], scale=sc0[:, j:j + 1])
        p.tr_group([(ptrb[:, dc * 128:(dc + 1) * 128],
                     xnb[:, dc * 128:(dc + 1) * 128], ident[:, :])
                    for dc in range(8)], r=[xnb_b, "ident"], w=["ptrb"])
        p.cp("v", xnf, ptrb[:, :], r=["ptrb"], w=[xnf_b])
        pcs_banks = [(PS0, "PS0"), (PS1, "PS1"), (PS2, "PS2"), (PS3, "PS3"),
                     (PS0, "PS0"), (PS1, "PS1")]
        for pc in range(6):
            ps, psn = pcs_banks[pc]
            p.mm_group([(ps[:, :], xnf[:, dc * 128:(dc + 1) * 128],
                         wq_ap(dc, pc * 512, 512), dc == 0, dc == 7)
                        for dc in range(8)], r=[xnf_b, wqv_b], w=[psn])
            if pc == 0:
                p.act(eqs[:, 0:512], ps[:, :], AF.Exp, r=[psn], w=[eqs_b], scale=ASC)
            elif pc == 1:
                p.act(eqs[:, 512:1024], ps[:, :], AF.Exp, r=[psn], w=[eqs_b], scale=ASC)
            elif pc in (2, 3):
                o = (pc - 2) * 512
                p.act(scC[:, o:o + 512], ps[:, :], AF.Exp, r=[psn], w=[scC_b], scale=ASC)
                p.ts("v", keb[:, o:o + 512], scC[:, o:o + 512], 1e-6, None, ALU.add,
                     r=[scC_b], w=[keb_b])
            else:
                o = (pc - 4) * 512
                p.cp("v", vtm[:, o:o + 512], ps[:, :], r=[psn], w=[vtm_b])
        for hh, (ps, psn) in enumerate([(PS2, "PS2"), (PS3, "PS3")]):
            mms = [(ps[:, :], tri4[:, 0:128], keb[:, hh * 512:(hh + 1) * 512],
                    True, j == 0)]
            if j > 0:
                mms.append((ps[:, :], ones_rb1[0:1, :],
                            carry_bb[0:1, hh * 512:(hh + 1) * 512], False, True))
            p.mm_group(mms, r=[tri4_b, keb_b, "ones_rb1", carry_b_b], w=[psn])
            p.act(lnk[:, hh * 512:(hh + 1) * 512], ps[:, :], AF.Ln, r=[psn], w=[lnk_b])
        if j < NT - 1:
            for hh in range(2):
                p.op("t", lambda E, hh=hh: E.matmul(pb2[0:1, :], ones_cb[:, 0:1],
                     keb[:, hh * 512:(hh + 1) * 512], start=True, stop=True),
                     r=["ones_cb", keb_b], w=["pb2"])
                p.tt("v", carry_f[0:1, hh * 512:(hh + 1) * 512],
                     carry_f[0:1, hh * 512:(hh + 1) * 512], pb2[0:1, :], ALU.add,
                     r=["pb2", carry_f_b], w=[carry_f_b])
            p.cp("v", carry_bb[0:1, :], carry_f[0:1, :], r=[carry_f_b], w=[carry_b_b])
        p.red("v", s16[:, 0:16], eqs[:, :].rearrange("np (h l) -> np h l", l=64),
              ALU.add, r=[eqs_b], w=[s16_b])
        p.act(s16[:, 16:32], s16[:, 0:16], AF.Ln, r=[s16_b], w=[s16_b])
        p.tt("v", lnk[:, :].rearrange("np (h l) -> np h l", l=64),
             lnk[:, :].rearrange("np (h l) -> np h l", l=64),
             s16[:, 16:32].unsqueeze(2).broadcast_to([128, 16, 64]),
             ALU.add, r=[lnk_b, s16_b], w=[lnk_b])
        p.act(scC, lnk[:, :], AF.Exp, r=[lnk_b], w=[scC_b], scale=-1.0)
        p.tt("v", qsbf, eqs, scC, ALU.mult, r=[eqs_b, scC_b], w=[qsbf_b])
        for rnd in range(2):
            p.tr_group([(ptrb[0:64, hi * 128:(hi + 1) * 128],
                         qsbf[:, (rnd * 8 + hi) * 64:(rnd * 8 + hi + 1) * 64],
                         ident[:, :]) for hi in range(8)],
                       r=[qsbf_b, "ident"], w=["ptrb"])
            p.cp("v", qsf[0:64, rnd * 1024:(rnd + 1) * 1024], ptrb[0:64, :],
                 r=["ptrb"], w=[qsf_b])
            p.tr_group([(ptrb[0:64, hi * 128:(hi + 1) * 128],
                         keb[:, (rnd * 8 + hi) * 64:(rnd * 8 + hi + 1) * 64],
                         ident[:, :]) for hi in range(8)],
                       r=[keb_b, "ident"], w=["ptrb"])
            p.cp("v", kef[0:64, rnd * 1024:(rnd + 1) * 1024], ptrb[0:64, :],
                 r=["ptrb"], w=[kef_b])

        def hslice(buf_ap, h):
            return buf_ap[0:64, h * 128:(h + 1) * 128]

        for quad in range(4):
            ps, psn = (PS2, "PS2") if quad % 2 == 0 else (PS3, "PS3")
            ab, abn = (atb, atb_b) if quad % 2 == 0 else (atb2, atb2_b)
            p.mm_group([(ps[:, hi * 128:(hi + 1) * 128], hslice(kef, quad * 4 + hi),
                         hslice(qsf, quad * 4 + hi), True, True)
                        for hi in range(4)], r=[kef_b, qsf_b], w=[psn])
            p.tt("v", ab[:, :], ps[:, :], tri4[:, :], ALU.mult, r=[psn, tri4_b], w=[abn])
        for half in range(2):
            mms = []
            for qd in (half * 2, half * 2 + 1):
                ab, abn = (atb, atb_b) if qd % 2 == 0 else (atb2, atb2_b)
                for hi in range(4):
                    h = qd * 4 + hi
                    mms.append((PSY[:, (h % 8) * 64:(h % 8) * 64 + 64],
                                ab[:, hi * 128:(hi + 1) * 128],
                                vtm[:, h * 64:(h + 1) * 64], True, j == 0))
                    if j > 0:
                        mms.append((PSY[:, (h % 8) * 64:(h % 8) * 64 + 64],
                                    hslice(qsf, h),
                                    S_b_[0:64, h * 64:(h + 1) * 64], False, True))
            p.mm_group(mms, r=[atb_b, atb2_b, vtm_b, qsf_b, S_b_b], w=["PSY"])
            p.cp("v", ytm[:, half * 512:(half + 1) * 512], PSY[:, :],
                 r=["PSY"], w=[ytm_b])
        for half in range(2):
            p.mm_group([(pS[0:64, hi * 64:(hi + 1) * 64],
                         keb[:, (half * 8 + hi) * 64:(half * 8 + hi + 1) * 64],
                         vtm[:, (half * 8 + hi) * 64:(half * 8 + hi + 1) * 64],
                         True, True) for hi in range(8)],
                       r=[keb_b, vtm_b], w=["pS"])
            p.tt("v", S_f[0:64, half * 512:(half + 1) * 512],
                 S_f[0:64, half * 512:(half + 1) * 512], pS[0:64, :], ALU.add,
                 r=["pS", S_f_b], w=[S_f_b])
        p.cp("v", S_b_[0:64, :], S_f[0:64, :], r=[S_f_b], w=[S_b_b])
        p.tr_group([(ptrb[:, dc * 128:(dc + 1) * 128],
                     ytm[:, dc * 128:(dc + 1) * 128], ident[:, :])
                    for dc in range(8)], r=[ytm_b, "ident"], w=["ptrb"])
        p.cp("v", yf, ptrb[:, :], r=["ptrb"], w=[yf_b])
        for hh, (ps, psn) in enumerate([(PS0, "PS0"), (PS1, "PS1")]):
            p.mm_group([(ps[:, :], yf[:, dc * 128:(dc + 1) * 128],
                         wou[:, dc * 1024 + hh * 512:dc * 1024 + hh * 512 + 512],
                         dc == 0, dc == 7) for dc in range(8)],
                       r=[yf_b, wou_b], w=[psn])
            p.tt("v", x1p[:, hh * 512:(hh + 1) * 512], xa[:, hh * 512:(hh + 1) * 512],
                 ps[:, :], ALU.add, r=[xb, psn], w=[x1p_b])
        p.dma("g", (out_t if upto == "attn" else x1_h)[j * 128:(j + 1) * 128, :],
              x1p, "x1hC", in_buf=x1p_b)
        if j == min(8, NT - 2):
            quant_weight(w1T, 8, H0, w1q_h, 0, "w1")
        if j == min(18, NT - 1):
            quant_weight(w2T, 32, D, w2q_h, 1, "w2")

    part_max(mnmx[:, 0:1], mnmx_b, row16[0:1, 0:1], "r16a", 10)
    part_max(mnmx[:, 1:2], mnmx_b, row16[0:1, 1:2], "r16b", 11)
    p.dma("g", cc_in.ap()[0:1, 0:2], row16[0:1, 0:2], "ccinA", in_buf="r16b")
    p.buf("ccinA").readers[("xa",)] = p.buf("r16a").writer
    p.op("g", lambda E: E.collective_compute("AllReduce", ALU.max,
         replica_groups=[list(range(NCORES))], ins=[cc_in.ap()[0:1, 0:2]],
         outs=[cc_out.ap()[0:1, 0:2]]), r=["ccinA"], w=["ccoutA"])
    p.dma("g", row16[0:1, 2:4], cc_out.ap()[0:1, 0:2], "ccresA", in_buf="ccoutA")

    def derive_act_qparams(nmin_ap, gmax_ap, res_buf, sw_ap, hi_c, lo_c, smm_c,
                           inv_c):
        r, rb = row16, "row16"
        p.tt("v", r[0:1, 4:5], gmax_ap, nmin_ap, ALU.add, r=[res_buf, rb], w=[rb])
        p.ts("v", r[0:1, 4:5], r[0:1, 4:5], 1.0 / 255.0, 1e-8, ALU.mult, ALU.max,
             r=[rb], w=[rb])
        p.recip(r[0:1, 5:6], r[0:1, 4:5], r=[rb], w=[rb])
        p.tt("v", r[0:1, 6:7], nmin_ap, r[0:1, 5:6], ALU.mult, r=[rb], w=[rb])
        p.ts("v", r[0:1, 6:7], r[0:1, 6:7], MAGIC, MAGIC, ALU.add, ALU.subtract,
             r=[rb], w=[rb])
        p.ts("v", r[0:1, 6:7], r[0:1, 6:7], 128.0, 127.0, ALU.subtract, ALU.min,
             r=[rb], w=[rb])
        p.ts("v", r[0:1, 6:7], r[0:1, 6:7], -128.0, None, ALU.max, r=[rb], w=[rb])
        p.ts("v", r[0:1, 7:8], r[0:1, 6:7], -1.0, 127.0, ALU.mult, ALU.add, r=[rb], w=[rb])
        p.ts("v", r[0:1, 8:9], r[0:1, 6:7], -1.0, -128.0, ALU.mult, ALU.add, r=[rb], w=[rb])
        p.tt("v", r[0:1, 9:10], r[0:1, 4:5], sw_ap, ALU.mult, r=[rb, "roww"], w=[rb])
        bcast_col(r[0:1, 7:8], qpc(hi_c), [rb], "qp")
        bcast_col(r[0:1, 8:9], qpc(lo_c), [rb], "qp")
        bcast_col(r[0:1, 9:10], qpc(smm_c), [rb], "qp")
        bcast_col(r[0:1, 5:6], qpc(inv_c), [rb], "qp")

    derive_act_qparams(row16[0:1, 2:3], row16[0:1, 3:4], "ccresA",
                       roww[0:1, 4:5], QP_XHI, QP_XLO, QP_SMM1, QP_INVSX)
    p.ts("v", qsc[:, :], sc0[:, :], qpc(QP_INVSX), None, ALU.mult,
         r=["sc0", "qp"], w=["qsc"])
    p.ts("v", qsb[:, :], sb0[:, :], qpc(QP_INVSX), None, ALU.mult,
         r=["sb0", "qp"], w=["qsb"])

    def emit_xq_tile(j, jj, xa, xb, scr, scr_b, xqtm, xqtm_b, xqf3, xqf_b):
        p.act(scr, xa, AF.Identity, r=[xb, "qsc", "qsb"], w=[scr_b],
              bias=qsb[:, j:j + 1], scale=qsc[:, j:j + 1])
        p.ts("v", scr, scr, MAGIC, MAGIC, ALU.add, ALU.subtract, r=[scr_b], w=[scr_b])
        p.ts("v", scr, scr, qpc(QP_XHI), None, ALU.min, r=[scr_b, "qp"], w=[scr_b])
        p.ts("v", xqtm, scr, qpc(QP_XLO), None, ALU.max, r=[scr_b, "qp"], w=[xqtm_b])
        p.tr_group([(ptrb[:, dc * 128:(dc + 1) * 128],
                     xqtm[:, dc * 128:(dc + 1) * 128], ident[:, :])
                    for dc in range(8)], r=[xqtm_b, "ident"], w=["ptrb"])
        p.cp("v", xqf3[:, :, jj * 128:(jj + 1) * 128],
             ptrb[:, :].rearrange("np (dc tc) -> np dc tc", tc=128),
             r=["ptrb"], w=[xqf_b])

    w1r, w1r_b = AR.alloc("w1r", 40 * 1024, 64 * 1024, BF16)
    xqfBs = [AR.alloc(f"xqfB{i}", 104 * 1024 + i * 8 * 1024, 8 * 1024, BF16)
             for i in range(2)]
    scrBs = [AR.alloc(f"scrB{i}", 120 * 1024 + i * 4096, 4096, F32) for i in range(2)]
    xqtmBs = [AR.alloc(f"xqtmB{i}", 128 * 1024 + i * 2048, 2048, BF16) for i in range(2)]
    accst = [AR.alloc(f"accst{i}", 132 * 1024 + 64 + i * 2048, 2048, F32)
             for i in range(2)]
    hmn, hmn_b = AR.alloc("hmn", 132 * 1024, 64, F32)
    p.mset(hmn[:, 0:2], -1e38, [hmn_b])
    for dc in range(8):
        p.dma("y", w1r[:, dc * 4096:(dc + 1) * 4096],
              w1q_h[dc * 128:(dc + 1) * 128, :], w1r_b, in_buf="w1h")

    for ch in range(T // 512):
        xqfB, xqfB_b = xqfBs[ch % 2]
        xqfB3 = xqfB.rearrange("np (dc tc) -> np dc tc", tc=512)
        for jj in range(4):
            j = ch * 4 + jj
            xa, xb = xr[j % 4]
            p.dma("g", xa, x_in[j * 128:(j + 1) * 128, :], xb)
            scrB, scrB_b = scrBs[jj % 2]
            xqtmB, xqtmB_b = xqtmBs[jj % 2]
            emit_xq_tile(j, jj, xa, xb, scrB, scrB_b, xqtmB, xqtmB_b, xqfB3, xqfB_b)
        for hg in range(32):
            ps = PS0 if hg % 2 == 0 else PS1
            psn = "PS0" if hg % 2 == 0 else "PS1"
            p.mm_group([(ps[:, :],
                         w1r[:, dc * 4096 + hg * 128:dc * 4096 + (hg + 1) * 128],
                         xqfB[:, dc * 512:(dc + 1) * 512], dc == 0, dc == 7)
                        for dc in range(8)], r=[w1r_b, xqfB_b], w=[psn])
            ac, ac_b = accst[hg % 2]
            p.cp("v", ac, ps[:, :], r=[psn], w=[ac_b])
            p.dma("g", acc_h[hg, :, ch * 512:(ch + 1) * 512], ac, "acch", in_buf=ac_b)
            rb = (hg % 2) * 8
            p.red("v", rowm[:, rb + 0:rb + 1], ac, ALU.max, r=[ac_b], w=["rowm"])
            p.red("v", rowm[:, rb + 1:rb + 2], ac, ALU.min, r=[ac_b], w=["rowm"])
            e2 = "v"
            p.act(rowm[:, rb + 2:rb + 3], rowm[:, rb + 0:rb + 1], AF.Gelu,
                  r=["rowm", "qp", "b1c"], w=["rowm"],
                  bias=b1c[:, hg:hg + 1], scale=qpc(QP_SMM1))
            p.act(rowm[:, rb + 3:rb + 4], rowm[:, rb + 1:rb + 2], AF.Gelu,
                  r=["rowm", "qp", "b1c"], w=["rowm"],
                  bias=b1c[:, hg:hg + 1], scale=qpc(QP_SMM1))
            p.act(rowm[:, rb + 4:rb + 5], rowm[:, rb + 0:rb + 1], AF.Identity,
                  r=["rowm", "qp", "b1c"], w=["rowm"],
                  bias=b1c[:, hg:hg + 1], scale=qpc(QP_SMM1))
            p.act(rowm[:, rb + 5:rb + 6], rowm[:, rb + 1:rb + 2], AF.Identity,
                  r=["rowm", "qp", "b1c"], w=["rowm"],
                  bias=b1c[:, hg:hg + 1], scale=qpc(QP_SMM1))
            p.ts(e2, rowm[:, rb + 4:rb + 5], rowm[:, rb + 4:rb + 5],
                 -0.751791534894327, None, ALU.is_ge, r=["rowm"], w=["rowm"])
            p.ts(e2, rowm[:, rb + 5:rb + 6], rowm[:, rb + 5:rb + 6],
                 -0.751791534894327, None, ALU.is_le, r=["rowm"], w=["rowm"])
            p.tt(e2, rowm[:, rb + 4:rb + 5], rowm[:, rb + 4:rb + 5],
                 rowm[:, rb + 5:rb + 6], ALU.mult, r=["rowm"], w=["rowm"])
            p.tt(e2, rowm[:, rb + 5:rb + 6], rowm[:, rb + 4:rb + 5],
                 p.constc(-0.16997120747990369), ALU.mult,
                 r=["rowm", "cvals"], w=["rowm"])
            p.ts(e2, rowm[:, rb + 4:rb + 5], rowm[:, rb + 4:rb + 5],
                 -65536.0, 65536.0, ALU.mult, ALU.add, r=["rowm"], w=["rowm"])
            p.tt(e2, rowm[:, rb + 4:rb + 5], rowm[:, rb + 4:rb + 5],
                 rowm[:, rb + 5:rb + 6], ALU.add, r=["rowm"], w=["rowm"])
            p.tt(e2, hmn[:, 1:2], hmn[:, 1:2], rowm[:, rb + 2:rb + 3], ALU.max,
                 r=["rowm", hmn_b], w=[hmn_b])
            p.tt(e2, hmn[:, 1:2], hmn[:, 1:2], rowm[:, rb + 3:rb + 4], ALU.max,
                 r=["rowm", hmn_b], w=[hmn_b])
            p.tt(e2, rowm[:, rb + 3:rb + 4], rowm[:, rb + 3:rb + 4],
                 rowm[:, rb + 2:rb + 3], ALU.min, r=["rowm"], w=["rowm"])
            p.tt(e2, rowm[:, rb + 3:rb + 4], rowm[:, rb + 3:rb + 4],
                 rowm[:, rb + 4:rb + 5], ALU.min, r=["rowm"], w=["rowm"])
            p.ts(e2, rowm[:, rb + 3:rb + 4], rowm[:, rb + 3:rb + 4], -1.0, None,
                 ALU.mult, r=["rowm"], w=["rowm"])
            p.tt(e2, hmn[:, 0:1], hmn[:, 0:1], rowm[:, rb + 3:rb + 4], ALU.max,
                 r=["rowm", hmn_b], w=[hmn_b])

    part_max(hmn[:, 0:1], hmn_b, row16[0:1, 0:1], "r16c", 10)
    part_max(hmn[:, 1:2], hmn_b, row16[0:1, 1:2], "r16d", 11)
    p.dma("g", cc_in.ap()[0:1, 4:6], row16[0:1, 0:2], "ccinB", in_buf="r16d")
    p.buf("ccinB").readers[("xc",)] = p.buf("r16c").writer
    p.op("g", lambda E: E.collective_compute("AllReduce", ALU.max,
         replica_groups=[list(range(NCORES))], ins=[cc_in.ap()[0:1, 4:6]],
         outs=[cc_out.ap()[0:1, 4:6]]), r=["ccinB"], w=["ccoutB"])
    p.dma("g", row16[0:1, 2:4], cc_out.ap()[0:1, 4:6], "ccresB", in_buf="ccoutB")
    derive_act_qparams(row16[0:1, 2:3], row16[0:1, 3:4], "ccresB",
                       roww[0:1, 5:6], QP_HHI, QP_HLO, QP_SMM2, QP_INVSH)

    if upto == "attn":
        p.final_wait("y", ["x1hC"])
        return nc

    hqDs = [AR.alloc(f"hqD{i}", 16 * 1024 + i * 32 * 1024, 32 * 1024, BF16)
            for i in range(2)]
    accld = [AR.alloc(f"accld{i}", 80 * 1024 + i * 2048, 2048, F32) for i in range(4)]
    x1r = [AR.alloc(f"x1rD{i}", 88 * 1024 + i * 4096, 4096, F32) for i in range(2)]
    scrDs = [AR.alloc(f"scrD{i}", 96 * 1024 + i * 4096, 4096, F32) for i in range(2)]
    w2s = [AR.alloc(f"w2s{i}", 104 * 1024 + i * 2 * 1024, 2 * 1024, BF16)
           for i in range(4)]
    dst_l0 = out_t if upto == "l0" else x1_h

    for ch in range(T // 512):
        hqD, hqD_b = hqDs[ch % 2]
        for hg in range(32):
            ac, ac_b = accld[hg % 4]
            p.dma("g", ac, acc_h[hg, :, ch * 512:(ch + 1) * 512], ac_b, in_buf="acch")
            scrD, scrD_b = scrDs[hg % 2]
            sD = scrD[:, 0:512]
            p.act(sD, ac, AF.Gelu, r=[ac_b, "qp", "b1c"], w=[scrD_b],
                  bias=b1c[:, hg:hg + 1], scale=qpc(QP_SMM1))
            p.ts("v", sD, sD, qpc(QP_INVSH), MAGIC, ALU.mult, ALU.add,
                 r=[scrD_b, "qp"], w=[scrD_b])
            p.ts("v", sD, sD, MAGIC, qpc(QP_HHI), ALU.subtract, ALU.min,
                 r=[scrD_b, "qp"], w=[scrD_b])
            p.ts("v", hqD[:, hg * 512:(hg + 1) * 512], sD, qpc(QP_HLO), None,
                 ALU.max, r=[scrD_b, "qp"], w=[hqD_b])
        for jp in range(2):
            for jj2 in range(2):
                j = ch * 4 + jp * 2 + jj2
                p.dma("g", x1r[jj2][0], x1_h[j * 128:(j + 1) * 128, :], x1r[jj2][1],
                      in_buf="x1hC")
            banks = [(PS2, "PS2"), (PS3, "PS3"), (PSY, "PSY"), (pb2, "pb2")]
            p.stream_begin(w=[b[1] for b in banks])
            for hg in range(32):
                ws, wsb = w2s[hg % 4]
                p.dma("y", ws, w2q_h[hg * 128:(hg + 1) * 128, :], wsb, in_buf="w2h")
                mms = []
                for jj2 in range(2):
                    jjj = jp * 2 + jj2
                    lhs = hqD[:, hg * 512 + jjj * 128:hg * 512 + (jjj + 1) * 128]
                    mms.append((banks[jj2 * 2][0][:, :], lhs, ws[:, 0:512],
                                hg == 0, hg == 31))
                    mms.append((banks[jj2 * 2 + 1][0][:, :], lhs, ws[:, 512:1024],
                                hg == 0, hg == 31))
                p.stream_step([hqD_b, wsb], mms)
            p.stream_end()
            for jj2 in range(2):
                j = ch * 4 + jp * 2 + jj2
                scrD, scrD_b = scrDs[jj2 % 2]
                for hh in range(2):
                    ps, psn = banks[jj2 * 2 + hh]
                    p.stt("v", scrD[:, hh * 512:(hh + 1) * 512], ps[:, :], qpc(QP_SMM2),
                          b2b[:, hh * 512:(hh + 1) * 512], ALU.mult, ALU.add,
                          r=[psn, "qp", "b2b"], w=[scrD_b])
                    p.tt("v", scrD[:, hh * 512:(hh + 1) * 512],
                         scrD[:, hh * 512:(hh + 1) * 512],
                         x1r[jj2][0][:, hh * 512:(hh + 1) * 512], ALU.add,
                         r=[scrD_b, x1r[jj2][1]], w=[scrD_b])
                p.dma("g", dst_l0[j * 128:(j + 1) * 128, :], scrD, "x1hD", in_buf=scrD_b)

    if upto == "l0":
        p.final_wait("y", ["x1hD"])
        return nc

    x1L = [AR.alloc(f"x1L{i}", i * 4096, 4096, F32) for i in range(4)]
    xn1 = [AR.alloc(f"xn1L{i}", 16 * 1024 + i * 4096, 4096, F32) for i in range(4)]
    xnf1, xnf1_b = AR.alloc("xnf1", 32 * 1024, 8 * 1024, BF16)
    xnf13 = xnf1.rearrange("np (dc tc) -> np dc tc", tc=512)
    hnL, hnL_b = AR.alloc("hnL", 40 * 1024, 4096, F32)
    hfm, hfm_b = AR.alloc("hfm", 44 * 1024, 8 * 2056, F32)
    HS = 514
    dscL, dscL_b = AR.alloc("dscL", 61 * 1024, 2048, F32)
    xkf, xkf_b = AR.alloc("xkf", 63 * 1024, 8 * 1024, BF16)
    xvf, xvf_b = AR.alloc("xvf", 71 * 1024, 8 * 1024, BF16)
    xrf, xrf_b = AR.alloc("xrf", 79 * 1024, 8 * 1024, BF16)
    rfm, rfm_b = AR.alloc("rfm", 87 * 1024, 8 * 1024, BF16)
    ybuf, ybuf_b = AR.alloc("ybuf", 95 * 1024, 8 * 1024, BF16)
    kmS, kmS_b = AR.alloc("kmS", 103 * 1024, 2048, F32)
    KpS, KpS_b = AR.alloc("KpS", 105 * 1024, 2048, F32)
    PbS, PbS_b = AR.alloc("PbS", 107 * 1024, 2056, F32)
    lnS, lnS_b = AR.alloc("lnS", 109 * 1024 + 64, 2056, F32)
    alS, alS_b = AR.alloc("alS", 111 * 1024 + 128, 2048, F32)
    bvS, bvS_b = AR.alloc("bvS", 113 * 1024 + 128, 2048, F32)
    aaS, aaS_b = AR.alloc("aaS", 115 * 1024 + 128, 2056, F32)
    vsb, vsb_b = AR.alloc("vsb", 117 * 1024 + 192, 2048, F32)
    ryf, ryf_b = AR.alloc("ryf", 120 * 1024, 8 * 1024, BF16)
    wsl = [AR.alloc(f"wslL{i}", 174 * 1024 + i * 4096, 4096, BF16) for i in range(4)]

    for ch in range(T // 512):
        for jj in range(4):
            j = ch * 4 + jj
            xa, xab = x1L[jj]
            p.dma("g", xa, x1_h[j * 128:(j + 1) * 128, :], xab, in_buf="x1hD")
            p.red("v", rowm[:, 0:1], xa, ALU.add, r=[xab], w=["rowm"])
            p.act(hnL, xa, AF.Square, r=[xab], w=[hnL_b, "rowm"], accum=rowm[:, 1:2])
            p.ts("v", rowm[:, 4:5], rowm[:, 0:1], 1.0 / D, None, ALU.mult, r=["rowm"], w=["rowm"])
            p.ts("v", rowm[:, 5:6], rowm[:, 1:2], 1.0 / D, None, ALU.mult, r=["rowm"], w=["rowm"])
            p.tt("v", rowm[:, 6:7], rowm[:, 4:5], rowm[:, 4:5], ALU.mult, r=["rowm"], w=["rowm"])
            p.tt("v", rowm[:, 6:7], rowm[:, 5:6], rowm[:, 6:7], ALU.subtract, r=["rowm"], w=["rowm"])
            p.act(rowm[:, 7:8], rowm[:, 6:7], AF.Ln, r=["rowm"], w=["rowm"], bias=LN_EPS)
            p.act(rowm[:, 7:8], rowm[:, 7:8], AF.Exp, r=["rowm"], w=["rowm"], scale=-0.5)
            p.tt("v", rowm[:, 8:9], rowm[:, 4:5], rowm[:, 7:8], ALU.mult, r=["rowm"], w=["rowm"])
            p.ts("v", rowm[:, 8:9], rowm[:, 8:9], -1.0, None, ALU.mult, r=["rowm"], w=["rowm"])
            p.act(xn1[jj][0], xa, AF.Identity, r=[xab, "rowm"], w=[xn1[jj][1]],
                  bias=rowm[:, 8:9], scale=rowm[:, 7:8])
            xna, xnab = xn1[jj]
            p.red("v", rowm[:, 0:1], xna, ALU.add, r=[xnab], w=["rowm"])
            p.act(hnL, xna, AF.Square, r=[xnab], w=[hnL_b, "rowm"], accum=rowm[:, 1:2])
            p.ts("v", rowm[:, 4:5], rowm[:, 0:1], 1.0 / D, None, ALU.mult, r=["rowm"], w=["rowm"])
            p.ts("v", rowm[:, 5:6], rowm[:, 1:2], 1.0 / D, None, ALU.mult, r=["rowm"], w=["rowm"])
            p.tt("v", rowm[:, 6:7], rowm[:, 4:5], rowm[:, 4:5], ALU.mult, r=["rowm"], w=["rowm"])
            p.tt("v", rowm[:, 6:7], rowm[:, 5:6], rowm[:, 6:7], ALU.subtract, r=["rowm"], w=["rowm"])
            p.act(rowm[:, 7:8], rowm[:, 6:7], AF.Ln, r=["rowm"], w=["rowm"], bias=LN_EPS)
            p.act(rowm[:, 7:8], rowm[:, 7:8], AF.Exp, r=["rowm"], w=["rowm"], scale=-0.5)
            p.tt("v", rowm[:, 8:9], rowm[:, 4:5], rowm[:, 7:8], ALU.mult, r=["rowm"], w=["rowm"])
            p.ts("v", rowm[:, 8:9], rowm[:, 8:9], -1.0, None, ALU.mult, r=["rowm"], w=["rowm"])
            p.act(hnL, xna, AF.Identity, r=[xnab, "rowm"], w=[hnL_b],
                  bias=rowm[:, 8:9], scale=rowm[:, 7:8])
            for half in range(2):
                ps, psn = (PS3, "PS3") if half == 0 else (PSY, "PSY")
                p.tr_group([(ps[:, q4 * 128:(q4 + 1) * 128],
                             hnL[:, (half * 4 + q4) * 128:(half * 4 + q4 + 1) * 128],
                             identf[:, :]) for q4 in range(4)],
                           r=[hnL_b, "identf"], w=[psn])
                for q4 in range(4):
                    dc = half * 4 + q4
                    p.act(hfm[:, dc * HS + 1 + jj * 128:dc * HS + 1 + (jj + 1) * 128],
                          ps[:, q4 * 128:(q4 + 1) * 128], AF.Identity,
                          r=[psn, "chp"], w=[hfm_b],
                          bias=chcol(NP_B1, dc), scale=chcol(NP_G1, dc))
            for half in range(2):
                ps, psn = (PS3, "PS3") if half == 0 else (PSY, "PSY")
                p.tr_group([(ps[:, q4 * 128:(q4 + 1) * 128],
                             xna[:, (half * 4 + q4) * 128:(half * 4 + q4 + 1) * 128],
                             identf[:, :]) for q4 in range(4)],
                           r=[xnab, "identf"], w=[psn])
                for q4 in range(4):
                    dc = half * 4 + q4
                    p.cp("v", xnf13[:, dc, jj * 128:(jj + 1) * 128],
                         ps[:, q4 * 128:(q4 + 1) * 128], r=[psn], w=[xnf1_b])
        for g in range(8):
            p.cp("v", hfm[:, g * HS:g * HS + 1], chcol(NP_HB, g), r=["chp"], w=[hfm_b])
        for g in range(8):
            h_g = hfm[:, g * HS + 1:g * HS + 513]
            hp_g = hfm[:, g * HS:g * HS + 512]
            p.tt("v", dscL, h_g, hp_g, ALU.subtract, r=[hfm_b], w=[dscL_b])
            p.ts("v", kmS, dscL, chcol(NP_MK, g), None, ALU.mult,
                 r=[dscL_b, "chp"], w=[kmS_b])
            p.tt("g", xkf[:, g * 512:(g + 1) * 512], kmS, hp_g, ALU.add,
                 r=[kmS_b, hfm_b], w=[xkf_b])
            p.stt("v", xvf[:, g * 512:(g + 1) * 512], dscL, chcol(NP_MV, g), hp_g,
                  ALU.mult, ALU.add, r=[dscL_b, "chp", hfm_b], w=[xvf_b])
            p.ts("v", KpS, dscL, chcol(NP_MR, g), None, ALU.mult,
                 r=[dscL_b, "chp"], w=[KpS_b])
            p.tt("g", xrf[:, g * 512:(g + 1) * 512], KpS, hp_g, ALU.add,
                 r=[KpS_b, hfm_b], w=[xrf_b])
        for g in range(8):
            p.cp("v", chcol(NP_HB, g), hfm[:, g * HS + 512:g * HS + 513],
                 r=[hfm_b], w=["chp"])
        for g in range(8):
            for wi, (wsrc, wsrcb, psj, psnj, xf, xfb) in enumerate([
                    (wtk_b, "wtkb", PS0, "PS0", xkf, xkf_b),
                    (wtv_b, "wtvb", PS1, "PS1", xvf, xvf_b),
                    (wtr_b, "wtrb", PS2, "PS2", xrf, xrf_b)]):
                ws, wsb = wsl[(g * 3 + wi) % 4]
                p.dma("y", ws[:, 0:1024], wsrc[g * 128:(g + 1) * 128, :], wsb,
                      in_buf=wsrcb)
                p.mm_group([(psj[:, :], ws[:, dc * 128:(dc + 1) * 128],
                             xf[:, dc * 512:(dc + 1) * 512], dc == 0, dc == 7)
                            for dc in range(8)], r=[wsb, xfb], w=[psnj])
            p.act(rfm[:, g * 512:(g + 1) * 512], PS2[:, :], AF.Identity,
                  r=["PS2"], w=[rfm_b])
            p.cp("v", vsb, PS1[:, :], r=["PS1"], w=[vsb_b])
            p.red("v", rowm[:, 0:1], PS0[:, :], ALU.max, r=["PS0"], w=["rowm"])
            p.tt("v", rowm[:, 0:1], rowm[:, 0:1], ppe(g), ALU.max, r=["rowm", "wkvs"], w=["rowm"])
            p.ts("v", rowm[:, 1:2], rowm[:, 0:1], -1.0, None, ALU.mult, r=["rowm"], w=["rowm"])
            p.act(kmS, PS0[:, :], AF.Identity, r=["PS0", "rowm"], w=[kmS_b],
                  bias=rowm[:, 1:2])
            p.act(KpS, kmS, AF.Exp, r=[kmS_b], w=[KpS_b])
            p.ts("v", bvS, KpS, chcol(NP_EW, g), None, ALU.mult, r=[KpS_b, "chp"], w=[bvS_b])
            p.act(PbS[:, 0:1], ppe(g), AF.Exp, r=["wkvs", "rowm"], w=[PbS_b],
                  bias=rowm[:, 1:2])
            p.scan("v", PbS[:, 1:513], chcol(NP_EW, g).broadcast_to([128, 512]),
                   bvS, PbS[:, 0:1], r=[bvS_b, "chp", PbS_b], w=[PbS_b])
            p.act(lnS[:, 0:513], PbS[:, 0:513], AF.Ln, r=[PbS_b], w=[lnS_b],
                  bias=1e-38)
            p.tt("v", ppe(g), lnS[:, 512:513], rowm[:, 0:1], ALU.add,
                 r=[lnS_b, "rowm"], w=["wkvs"])
            p.tt("v", dscL, lnS[:, 0:512], kmS, ALU.subtract, r=[lnS_b, kmS_b], w=[dscL_b])
            p.ts("v", alS, dscL, 0.0, None, ALU.min, r=[dscL_b], w=[alS_b])
            p.act(alS, alS, AF.Exp, r=[alS_b], w=[alS_b])
            p.ts("v", bvS, dscL, -1.0, 0.0, ALU.mult, ALU.min, r=[dscL_b], w=[bvS_b])
            p.act(bvS, bvS, AF.Exp, r=[bvS_b], w=[bvS_b])
            p.tt("v", bvS, bvS, vsb, ALU.mult, r=[bvS_b, vsb_b], w=[bvS_b])
            p.cp("v", aaS[:, 0:1], aae(g), r=["wkvs"], w=[aaS_b])
            p.scan("v", aaS[:, 1:513], alS, bvS, aaS[:, 0:1], r=[alS_b, bvS_b, aaS_b],
                   w=[aaS_b])
            p.cp("v", aae(g), aaS[:, 512:513], r=[aaS_b], w=["wkvs"])
            p.stt("v", kmS, KpS, chcol(NP_EU, g), vsb, ALU.mult, ALU.mult,
                  r=[KpS_b, "chp", vsb_b], w=[kmS_b])
            p.tt("v", alS, PbS[:, 0:512], aaS[:, 0:512], ALU.mult,
                 r=[PbS_b, aaS_b], w=[alS_b])
            p.tt("v", alS, alS, kmS, ALU.add, r=[alS_b, kmS_b], w=[alS_b])
            p.stt("v", bvS, KpS, chcol(NP_EU, g), PbS[:, 0:512], ALU.mult, ALU.add,
                  r=[KpS_b, "chp", PbS_b], w=[bvS_b])
            p.act(bvS, bvS, AF.Ln, r=[bvS_b], w=[bvS_b])
            p.act(bvS, bvS, AF.Exp, r=[bvS_b], w=[bvS_b], scale=-1.0)
            p.tt("v", ybuf[:, g * 512:(g + 1) * 512], alS, bvS, ALU.mult,
                 r=[alS_b, bvS_b], w=[ybuf_b])
        for g in range(8):
            p.act(dscL, rfm[:, g * 512:(g + 1) * 512], AF.Sigmoid, r=[rfm_b], w=[dscL_b])
            p.tt("v", ryf[:, g * 512:(g + 1) * 512], dscL, ybuf[:, g * 512:(g + 1) * 512],
                 ALU.mult, r=[dscL_b, ybuf_b], w=[ryf_b])
        for jj in range(4):
            p.tr_group([(ptrb[:, g * 128:(g + 1) * 128],
                         ryf[:, g * 512 + jj * 128:g * 512 + (jj + 1) * 128],
                         ident[:, :]) for g in range(8)],
                       r=[ryf_b, "ident"], w=["ptrb"])
            p.tt("v", xn1[jj][0], xn1[jj][0], ptrb[:, :], ALU.add,
                 r=["ptrb", xn1[jj][1]], w=[xn1[jj][1]])
            p.red("v", lnrow[:, jj:jj + 1], xn1[jj][0], ALU.add, r=[xn1[jj][1]], w=["lnrow"])
            p.act(hnL, xn1[jj][0], AF.Square, r=[xn1[jj][1]], w=[hnL_b, "lnrow"],
                  accum=lnrow[:, 4 + jj:5 + jj])
        hs, hs_b = AR.alloc(f"hs{ch}", 95 * 1024, 24 * 1024, BF16)
        for hg in range(24):
            if hg % 2 == 0:
                hg2 = hg // 2
                wsg, wsgb = wsl[hg2 % 2]
                wsu, wsub = wsl[2 + hg2 % 2]
                p.dma("y", wsg[:, 0:2048].rearrange("np (two h) -> np two h", two=2),
                      wg_b[hg2 * 256:(hg2 + 1) * 256, :].rearrange(
                          "(two np) h -> np two h", np=128), wsgb, in_buf="wgb")
                p.dma("y", wsu[:, 0:2048].rearrange("np (two h) -> np two h", two=2),
                      wu_b[hg2 * 256:(hg2 + 1) * 256, :].rearrange(
                          "(two np) h -> np two h", np=128), wsub, in_buf="wub")
            wo = (hg % 2) * 1024
            psg, psgn = (PS0, "PS0") if hg % 2 == 0 else (PS2, "PS2")
            psu, psun = (PS1, "PS1") if hg % 2 == 0 else (PS3, "PS3")
            p.mm_group([(psg[:, :], wsg[:, wo + dc * 128:wo + (dc + 1) * 128],
                         xnf1[:, dc * 512:(dc + 1) * 512], dc == 0, dc == 7)
                        for dc in range(8)], r=[wsgb, xnf1_b], w=[psgn])
            p.mm_group([(psu[:, :], wsu[:, wo + dc * 128:wo + (dc + 1) * 128],
                         xnf1[:, dc * 512:(dc + 1) * 512], dc == 0, dc == 7)
                        for dc in range(8)], r=[wsub, xnf1_b], w=[psun])
            p.act(hnL[:, 0:512], psg[:, :], AF.Sigmoid, r=[psgn], w=[hnL_b],
                  scale=1.0 / 127.0)
            p.tt("v", hnL[:, 512:1024], hnL[:, 0:512], psg[:, :], ALU.mult,
                 r=[hnL_b, psgn], w=[hnL_b])
            p.tt("v", hs[:, hg * 512:(hg + 1) * 512], hnL[:, 512:1024], psu[:, :],
                 ALU.mult, r=[hnL_b, psun], w=[hs_b])
        kk, kk_b = AR.alloc(f"kk{ch}", 61 * 1024, 32 * 1024, BF16)
        ckf, ckf_b = AR.alloc(f"ckf{ch}", 126 * 1024, 8 * 1024, BF16)
        crf, crf_b = AR.alloc(f"crf{ch}", 150 * 1024, 8 * 1024, BF16)
        d3s = [AR.alloc(f"d3{ch}_{i}", 158 * 1024 + i * 4096, 4096, F32)
               for i in range(4)]
        dall = [(PS0, "PS0"), (PS1, "PS1"), (PS2, "PS2"), (PS3, "PS3")]
        for jp in range(2):
            jjs = (2 * jp, 2 * jp + 1)
            p.stream_begin(w=["PS0", "PS1", "PS2", "PS3"])
            for hg in range(24):
                if hg % 2 == 0:
                    hg2 = hg // 2
                    ws, wsb = wsl[hg2 % 4]
                    p.dma("y", ws[:, 0:2048].rearrange("np (two h) -> np two h", two=2),
                          wd_b[hg2 * 256:(hg2 + 1) * 256, :].rearrange(
                              "(two np) h -> np two h", np=128), wsb, in_buf="wdb")
                wo = (hg % 2) * 1024
                mms = []
                for ji, jj in enumerate(jjs):
                    lhs = hs[:, hg * 512 + jj * 128:hg * 512 + (jj + 1) * 128]
                    for hh in range(2):
                        mms.append((dall[ji * 2 + hh][0][:, :], lhs,
                                    ws[:, wo + hh * 512:wo + hh * 512 + 512],
                                    hg == 0, hg == 23))
                p.stream_step([hs_b, wsb], mms)
            p.stream_end()
            for ji, jj in enumerate(jjs):
                d3, d3_b = d3s[jp * 2 + ji]
                for hh in range(2):
                    ps, psn = dall[ji * 2 + hh]
                    p.ts("v", d3[:, hh * 512:(hh + 1) * 512], ps[:, :], 127.0 ** -3,
                         None, ALU.mult, r=[psn], w=[d3_b])
        u3s = [AR.alloc(f"u3{ch}_{i}", 120 * 1024 + i * 4096, 4096, F32)
               for i in range(2)]
        for jj in range(4):
            p.ts("v", rowm[:, 4:5], lnrow[:, jj:jj + 1], 1.0 / D, None, ALU.mult,
                 r=["lnrow"], w=["rowm"])
            p.ts("v", rowm[:, 5:6], lnrow[:, 4 + jj:5 + jj], 1.0 / D, None, ALU.mult,
                 r=["lnrow"], w=["rowm"])
            p.tt("v", rowm[:, 6:7], rowm[:, 4:5], rowm[:, 4:5], ALU.mult, r=["rowm"], w=["rowm"])
            p.tt("v", rowm[:, 6:7], rowm[:, 5:6], rowm[:, 6:7], ALU.subtract, r=["rowm"], w=["rowm"])
            p.act(rowm[:, 7:8], rowm[:, 6:7], AF.Ln, r=["rowm"], w=["rowm"], bias=LN_EPS)
            p.act(rowm[:, 7:8], rowm[:, 7:8], AF.Exp, r=["rowm"], w=["rowm"], scale=-0.5)
            p.tt("v", rowm[:, 8:9], rowm[:, 4:5], rowm[:, 7:8], ALU.mult, r=["rowm"], w=["rowm"])
            p.ts("v", rowm[:, 8:9], rowm[:, 8:9], -1.0, None, ALU.mult, r=["rowm"], w=["rowm"])
            p.act(hnL, xn1[jj][0], AF.Identity, r=[xn1[jj][1], "rowm"], w=[hnL_b],
                  bias=rowm[:, 8:9], scale=rowm[:, 7:8])
            for half in range(2):
                ps, psn = (PS3, "PS3") if half == 0 else (PSY, "PSY")
                p.tr_group([(ps[:, q4 * 128:(q4 + 1) * 128],
                             hnL[:, (half * 4 + q4) * 128:(half * 4 + q4 + 1) * 128],
                             identf[:, :]) for q4 in range(4)],
                           r=[hnL_b, "identf"], w=[psn])
                for q4 in range(4):
                    dc = half * 4 + q4
                    p.act(hfm[:, dc * HS + 1 + jj * 128:dc * HS + 1 + (jj + 1) * 128],
                          ps[:, q4 * 128:(q4 + 1) * 128], AF.Identity,
                          r=[psn, "chp"], w=[hfm_b],
                          bias=chcol(NP_B2, dc), scale=chcol(NP_G2, dc))
        for g in range(8):
            p.cp("v", hfm[:, g * HS:g * HS + 1], chcol(NP_H2B, g), r=["chp"], w=[hfm_b])
        for g in range(8):
            h_g = hfm[:, g * HS + 1:g * HS + 513]
            hp_g = hfm[:, g * HS:g * HS + 512]
            p.tt("v", hnL[:, 0:512], h_g, hp_g, ALU.subtract, r=[hfm_b], w=[hnL_b])
            p.ts("v", hnL[:, 512:1024], hnL[:, 0:512], chcol(NP_CK, g), None,
                 ALU.mult, r=[hnL_b, "chp"], w=[hnL_b])
            p.tt("g", ckf[:, g * 512:(g + 1) * 512], hnL[:, 512:1024], hp_g, ALU.add,
                 r=[hnL_b, hfm_b], w=[ckf_b])
            p.stt("v", crf[:, g * 512:(g + 1) * 512], hnL[:, 0:512], chcol(NP_CR, g),
                  hp_g, ALU.mult, ALU.add, r=[hnL_b, "chp", hfm_b], w=[crf_b])
        for g in range(8):
            p.cp("v", chcol(NP_H2B, g), hfm[:, g * HS + 512:g * HS + 513],
                 r=[hfm_b], w=["chp"])
        for hg in range(32):
            if hg % 2 == 0:
                hg2 = hg // 2
                ws, wsb = wsl[hg2 % 4]
                p.dma("y", ws[:, 0:2048].rearrange("np (two h) -> np two h", two=2),
                      wck_b[hg2 * 256:(hg2 + 1) * 256, :].rearrange(
                          "(two np) h -> np two h", np=128), wsb, in_buf="wckb")
            wo = (hg % 2) * 1024
            ps, psn = (PS0, "PS0") if hg % 2 == 0 else (PS1, "PS1")
            p.mm_group([(ps[:, :], ws[:, wo + dc * 128:wo + (dc + 1) * 128],
                         ckf[:, dc * 512:(dc + 1) * 512], dc == 0, dc == 7)
                        for dc in range(8)], r=[wsb, ckf_b], w=[psn])
            p.ts("v", hnL[:, 0:512], ps[:, :], 0.0, None, ALU.max, r=[psn], w=[hnL_b])
            p.tt("v", kk[:, hg * 512:(hg + 1) * 512], hnL[:, 0:512], hnL[:, 0:512],
                 ALU.mult, r=[hnL_b], w=[kk_b])
        for jp in range(2):
            jjs = (2 * jp, 2 * jp + 1)
            cvbank = {jjs[0]: [(PS0, "PS0"), (PS1, "PS1")],
                      jjs[1]: [(PS2, "PS2"), (PS3, "PS3")]}
            p.stream_begin(w=["PS0", "PS1", "PS2", "PS3"])
            for hg in range(32):
                if hg % 2 == 0:
                    hg2 = hg // 2
                    ws, wsb = wsl[hg2 % 4]
                    p.dma("y", ws[:, 0:2048].rearrange("np (two h) -> np two h", two=2),
                          wcv_b[hg2 * 256:(hg2 + 1) * 256, :].rearrange(
                              "(two np) h -> np two h", np=128), wsb, in_buf="wcvb")
                wo = (hg % 2) * 1024
                mms = []
                for jj in jjs:
                    lhs = kk[:, hg * 512 + jj * 128:hg * 512 + (jj + 1) * 128]
                    mms.append((cvbank[jj][0][0][:, :], lhs, ws[:, wo:wo + 512],
                                hg == 0, hg == 31))
                    mms.append((cvbank[jj][1][0][:, :], lhs, ws[:, wo + 512:wo + 1024],
                                hg == 0, hg == 31))
                p.stream_step([kk_b, wsb], mms)
            p.stream_end()
            for ji, jj in enumerate(jjs):
                j = ch * 4 + jj
                u3, u3_b = u3s[ji]
                p.stream_begin(w=["PSY", "pb2"])
                for dc in range(8):
                    ws, wsb = wsl[dc % 4]
                    p.dma("y", ws[:, 0:1024], wcr_b[dc * 128:(dc + 1) * 128, :], wsb,
                          in_buf="wcrb")
                    lhs = crf[:, dc * 512 + jj * 128:dc * 512 + (jj + 1) * 128]
                    p.stream_step([crf_b, wsb],
                                  [(PSY[:, :], lhs, ws[:, 0:512], dc == 0, dc == 7),
                                   (pb2[:, :], lhs, ws[:, 512:1024], dc == 0, dc == 7)])
                p.stream_end()
                for hh, psn in enumerate(["PSY", "pb2"]):
                    ps = PSY if hh == 0 else pb2
                    o = hh * 512
                    p.act(hnL[:, o:o + 512], ps[:, :], AF.Exp, r=[psn], w=[hnL_b],
                          scale=-1.0)
                    p.act(hnL[:, o:o + 512], hnL[:, o:o + 512], AF.Ln, r=[hnL_b],
                          w=[hnL_b], bias=1.0)
                    p.act(hnL[:, o:o + 512], hnL[:, o:o + 512], AF.Exp, r=[hnL_b],
                          w=[hnL_b], scale=-1.0)
                for hh in range(2):
                    ps, psn = cvbank[jj][hh]
                    o = hh * 512
                    p.tt("v", u3[:, o:o + 512], hnL[:, o:o + 512], ps[:, :], ALU.mult,
                         r=[hnL_b, psn], w=[u3_b])
                    p.tt("v", u3[:, o:o + 512], u3[:, o:o + 512],
                         xn1[jj][0][:, o:o + 512], ALU.add,
                         r=[u3_b, xn1[jj][1]], w=[u3_b])
                    p.tt("v", u3[:, o:o + 512], u3[:, o:o + 512],
                         x1L[jj][0][:, o:o + 512], ALU.add,
                         r=[u3_b, x1L[jj][1]], w=[u3_b])
            for ji, jj in enumerate(jjs):
                j = ch * 4 + jj
                u3, u3_b = u3s[ji]
                d3l, d3l_b = d3s[jp * 2 + ji]
                p.tt("v", u3, u3, d3l, ALU.add, r=[u3_b, d3l_b], w=[u3_b])
                p.dma("g", out_t[j * 128:(j + 1) * 128, :], u3, "out", in_buf=u3_b)

    p.final_wait("y", ["out"])
    return nc


def kernel(**inputs):
    f32 = np.float32

    def c(a, dt=np.float32):
        return np.ascontiguousarray(np.asarray(a), dtype=dt)

    x = c(inputs["L0_attn_x"])
    wqkvT = c(inputs["L0_attn_w"].T)
    w_out = c(inputs["L0_attn_w_out"])
    w1T = c(inputs["L0_ffn_W1"].T)
    b1v = c(inputs["L0_ffn_b1"]).reshape(1, H0)
    w2T = c(inputs["L0_ffn_W2"].T)
    b2v = c(inputs["L0_ffn_b2"]).reshape(1, D)
    tdv = c(inputs["L1_attn_time_decay"]).reshape(1, D)
    tfv = c(inputs["L1_attn_time_first"]).reshape(1, D)

    def fmH(w, ngrp):
        din, dout = w.shape
        ndc = din // 128
        r = w.reshape(ndc, 128, ngrp, 128)
        return np.ascontiguousarray(r.transpose(2, 1, 0, 3).reshape(ngrp * 128, ndc * 128))

    wtkH = fmH(c(inputs["L1_attn_w_tk"]), 8)
    wtvH = fmH(c(inputs["L1_attn_w_tv"]), 8)
    wtrH = fmH(c(inputs["L1_attn_w_tr"]), 8)
    wckH = fmH(c(inputs["L1_attn_w_ck"]), 32)
    wcv = c(inputs["L1_attn_w_cv"])
    wcr = c(inputs["L1_attn_w_cr"])
    l1g = c(inputs["L1_attn_ln1_g"]).reshape(1, D)
    l1b = c(inputs["L1_attn_ln1_b"]).reshape(1, D)
    l2g = c(inputs["L1_attn_ln2_g"]).reshape(1, D)
    l2b = c(inputs["L1_attn_ln2_b"]).reshape(1, D)
    mxk = c(inputs["L1_attn_mix_k"]).reshape(1, D)
    mxv = c(inputs["L1_attn_mix_v"]).reshape(1, D)
    mxr = c(inputs["L1_attn_mix_r"]).reshape(1, D)
    cmk = c(inputs["L1_attn_cmix_k"]).reshape(1, D)
    cmr = c(inputs["L1_attn_cmix_r"]).reshape(1, D)
    wgH = fmH(np.ascontiguousarray(inputs["L1_ffn_w_gate"].T), 24).astype(np.int32)
    wuH = fmH(np.ascontiguousarray(inputs["L1_ffn_w_up"].T), 24).astype(np.int32)
    wdT = c(inputs["L1_ffn_w_down"].T, np.int32)

    nc = build_program(BUILD_UPTO)
    shared = dict(wqkvT=wqkvT, w_out=w_out, w1T=w1T, b1v=b1v, w2T=w2T, b2v=b2v,
                  tdv=tdv, tfv=tfv, wtkH=wtkH, wtvH=wtvH, wtrH=wtrH, wckH=wckH,
                  wcv=wcv, wcr=wcr, l1g=l1g, l1b=l1b, l2g=l2g, l2b=l2b,
                  mxk=mxk, mxv=mxv, mxr=mxr, cmk=cmk, cmr=cmr,
                  wgH=wgH, wuH=wuH, wdT=wdT)
    in_maps = [dict(shared, x=np.ascontiguousarray(x[b])) for b in range(x.shape[0])]
    res = run_bass_kernel_spmd(nc, in_maps, core_ids=list(range(x.shape[0])))
    return np.stack([res.results[b]["o"] for b in range(x.shape[0])], axis=0)


if __name__ == "__main__":
    import reference
    ins = reference.setup_inputs()
    ins = {k: np.asarray(v) for k, v in ins.items()}
    out = kernel(**ins)
    print(out.shape, out.dtype)


# version 13
# speedup vs baseline: 1.0291x; 1.0291x over previous
import numpy as np
import concourse.bass as bass
import concourse.mybir as mybir
from concourse.bass_utils import run_bass_kernel_spmd

F32 = mybir.dt.float32
BF16 = mybir.dt.bfloat16
I32 = mybir.dt.int32
AF = mybir.ActivationFunctionType
ALU = mybir.AluOpType
AX = mybir.AxisListType

MAGIC = 1.5 * 2.0 ** 23
T, D = 4096, 1024
NT = T // 128
NCORES = 8
H0, H1 = 4096, 3072
NHEAD, HL = 16, 64
LN_EPS = 1e-5
ASC = 0.125

BUILD_UPTO = "full"


def set_shape(t, ncores=8):
    global T, NT, NCORES
    T = t
    NT = t // 128
    NCORES = ncores


class Buf:
    __slots__ = ("name", "writer", "readers", "sem", "fills")

    def __init__(self, name):
        self.name = name
        self.writer = None
        self.readers = {}
        self.sem = None
        self.fills = 0


class Prog:
    def __init__(self, nc):
        self.nc = nc
        self.eng = {"t": nc.tensor, "v": nc.vector, "s": nc.scalar,
                    "g": nc.gpsimd, "y": nc.sync}
        self.psem = {e: nc.alloc_semaphore(f"ps_{e}") for e in "tvsg"}
        self.pcnt = {e: 0 for e in "tvsg"}
        self.obs = {e: {} for e in "tvsgy"}
        self.bufs = {}
        self.nsem = 0

    def buf(self, n):
        if isinstance(n, Buf):
            return n
        b = self.bufs.get(n)
        if b is None:
            b = Buf(n)
            self.bufs[n] = b
        return b

    def _wait(self, e, sem, val, clock=None):
        k = sem.num
        if self.obs[e].get(k, 0) >= val:
            return
        self.eng[e].wait_ge(sem, val)
        obs = self.obs[e]
        obs[k] = max(obs.get(k, 0), val)
        if clock:
            for k2, v2 in clock.items():
                if obs.get(k2, 0) < v2:
                    obs[k2] = v2

    def _gather(self, e, reads, writes):
        need = {}

        def add(evt):
            if evt is None:
                return
            sem, val = evt[0], evt[1]
            clk = evt[2] if len(evt) > 2 else None
            cur = need.get(sem.num)
            if cur is None or cur[1] < val:
                need[sem.num] = (sem, val, clk)

        for b in reads:
            add(b.writer)
        for b in writes:
            add(b.writer)
            for evt in b.readers.values():
                add(evt)
        for sem, val, clk in need.values():
            self._wait(e, sem, val, clk)

    def op(self, e, fn, r=(), w=()):
        reads = [self.buf(x) for x in r]
        writes = [self.buf(x) for x in w]
        self._gather(e, reads, writes)
        ins = fn(self.eng[e])
        self.pcnt[e] += 1
        ins.then_inc(self.psem[e], 1)
        evt = (self.psem[e], self.pcnt[e], dict(self.obs[e]))
        for b in writes:
            b.writer = evt
            b.readers = {}
        for b in reads:
            b.readers[("e", e)] = evt
        return ins

    def tt(self, e, out, a, b, alu, r=(), w=()):
        self.op(e, lambda E: E.tensor_tensor(out, a, b, alu), r, w)

    def ts(self, e, out, a, s1, s2, op0, op1=None, r=(), w=()):
        if s2 is None and op1 is None:
            self.op(e, lambda E: E.tensor_single_scalar(out, a, s1, op0), r, w)
        else:
            self.op(e, lambda E: E.tensor_scalar(out, a, s1, s2, op0, op1), r, w)

    def stt(self, e, out, a, s, b, op0, op1, r=(), w=()):
        self.op(e, lambda E: E.scalar_tensor_tensor(out, a, s, b, op0, op1), r, w)

    def cp(self, e, out, a, r=(), w=()):
        self.op(e, lambda E: E.tensor_copy(out, a), r, w)

    def recip(self, out, a, r=(), w=()):
        self.op("v", lambda E: E.reciprocal(out, a), r, w)

    def red(self, e, out, a, alu, r=(), w=(), absval=None):
        self.op(e, lambda E: E.tensor_reduce(out, a, AX.X, alu,
                apply_absolute_value=absval), r, w)

    def act(self, out, a, func, r=(), w=(), bias=0.0, scale=1.0, accum=None):
        if isinstance(bias, float) and bias != 0.0:
            bias = self.constc(bias)
            r = list(r) + ["cvals"]
        self.op("s", lambda E: E.activation(out, a, func, bias=bias, scale=scale,
                accum_out=accum), r, w)

    def mm(self, out, lhsT, rhs, start, stop, r=(), w=()):
        self.op("t", lambda E: E.matmul(out, lhsT, rhs, start=start, stop=stop), r, w)

    def mm_group(self, mms, r=(), w=()):
        reads = [self.buf(x) for x in r]
        writes = [self.buf(x) for x in w]
        self._gather("t", reads, writes)
        E = self.eng["t"]
        for (out, lhsT, rhs, start, stop) in mms:
            ins = E.matmul(out, lhsT, rhs, start=start, stop=stop)
        self.pcnt["t"] += 1
        ins.then_inc(self.psem["t"], 1)
        evt = (self.psem["t"], self.pcnt["t"], dict(self.obs["t"]))
        for b in writes:
            b.writer = evt
            b.readers = {}
        for b in reads:
            b.readers[("e", "t")] = evt
        return ins

    def stream_begin(self, w=()):
        self._stream_w = [self.buf(x) for x in w]
        self._gather("t", [], self._stream_w)
        self._stream_evt = None

    def stream_step(self, rbufs, mms):
        rb = [self.buf(x) for x in rbufs]
        self._gather("t", rb, [])
        E = self.eng["t"]
        ins = None
        for (out, lhsT, rhs, start, stop) in mms:
            ins = E.matmul(out, lhsT, rhs, start=start, stop=stop)
        self.pcnt["t"] += 1
        ins.then_inc(self.psem["t"], 1)
        evt = (self.psem["t"], self.pcnt["t"], dict(self.obs["t"]))
        for b in rb:
            b.readers[("e", "t")] = evt
        self._stream_evt = evt

    def stream_end(self):
        for b in self._stream_w:
            b.writer = self._stream_evt
            b.readers = {}

    def tr_group(self, trs, r=(), w=()):
        reads = [self.buf(x) for x in r]
        writes = [self.buf(x) for x in w]
        self._gather("t", reads, writes)
        E = self.eng["t"]
        for (out, a, idt) in trs:
            ins = E.transpose(out, a, idt)
        self.pcnt["t"] += 1
        ins.then_inc(self.psem["t"], 1)
        evt = (self.psem["t"], self.pcnt["t"], dict(self.obs["t"]))
        for b in writes:
            b.writer = evt
            b.readers = {}
        for b in reads:
            b.readers[("e", "t")] = evt
        return ins

    def trans(self, out, a, idt, r=(), w=()):
        self.op("t", lambda E: E.transpose(out, a, idt), r, w)

    def scan(self, e, out, d0, d1, init, r=(), w=()):
        self.op(e, lambda E: E.tensor_tensor_scan(out, d0, d1, init,
                ALU.mult, ALU.add), r, w)

    def mset(self, ap, val, w, r=()):
        self.op("g", lambda E: E.memset(ap, val), r, w)

    def afsel(self, ap, cmp, fill, pattern, chmul, r=(), w=()):
        self.op("g", lambda E: E.affine_select(out=ap, in_=ap, compare_op=cmp,
                fill=fill, base=0, pattern=pattern, channel_multiplier=chmul), r, w)

    def dma(self, e, out_ap, in_ap, out_buf, in_buf=None, slow=False):
        ob = self.buf(out_buf)
        rb = [self.buf(in_buf)] if in_buf is not None else []
        self._gather(e, rb, [ob])
        if ob.sem is None:
            self.nsem += 1
            ob.sem = self.nc.alloc_semaphore(f"dm{self.nsem}")
        ins = self.eng[e].dma_start(out=out_ap, in_=in_ap,
                                    allow_slow_non_contiguous=slow)
        ob.fills += 1
        ins.then_inc(ob.sem, 16)
        evt = (ob.sem, 16 * ob.fills, dict(self.obs[e]))
        ob.writer = evt
        ob.readers = {}
        for b in rb:
            b.readers[("d", ob.name, ob.fills)] = evt
        return ins

    def final_wait(self, e, names):
        for n in names:
            b = self.buf(n)
            if b.writer is not None:
                self._wait(e, b.writer[0], b.writer[1])


class Arena:
    def __init__(self, prog, name, nbytes):
        self.p = prog
        self.t = prog.nc.alloc_sbuf_tensor(name, [128, nbytes], mybir.dt.uint8)
        self.nbytes = nbytes
        self.live = []
        self.gen = 0

    def alloc(self, name, lo, nbytes, dtype):
        hi = lo + nbytes
        assert hi <= self.nbytes, (name, lo, nbytes)
        self.gen += 1
        b = Buf(f"{name}#{self.gen}")
        keep = []
        for (l2, h2, b2) in self.live:
            if l2 < hi and lo < h2:
                if b2.writer is not None:
                    b.readers[("w", b2.name)] = b2.writer
                for k, evt in b2.readers.items():
                    b.readers[(b2.name,) + (k if isinstance(k, tuple) else (k,))] = evt
                if lo <= l2 and h2 <= hi:
                    continue
            keep.append((l2, h2, b2))
        keep.append((lo, hi, b))
        self.live = keep
        self.p.bufs[b.name] = b
        es = mybir.dt.size(dtype)
        ap = self.t[:, lo:hi].bitcast(dtype)
        return ap, b


def build_program(upto="full"):
    nc = bass.Bass("TRN2", target_bir_lowering=False, debug=False)
    p = Prog(nc)

    def din(name, shape, dt=F32):
        return nc.dram_tensor(name, shape, dt, kind="ExternalInput").ap()

    x_in = din("x", [T, D])
    wqkvT = din("wqkvT", [D, 3 * D])
    w_out = din("w_out", [D, D])
    w1T = din("w1T", [D, H0])
    b1v = din("b1v", [1, H0])
    w2T = din("w2T", [H0, D])
    b2v = din("b2v", [1, D])
    tdv = din("tdv", [1, D]); tfv = din("tfv", [1, D])
    wtkH = din("wtkH", [8 * 128, 1024])
    wtvH = din("wtvH", [8 * 128, 1024])
    wtrH = din("wtrH", [8 * 128, 1024])
    wckH = din("wckH", [32 * 128, 1024])
    wcv = din("wcv", [H0, D])
    wcr = din("wcr", [D, D])
    l1g = din("l1g", [1, D]); l1b = din("l1b", [1, D])
    l2g = din("l2g", [1, D]); l2b = din("l2b", [1, D])
    mxk = din("mxk", [1, D]); mxv = din("mxv", [1, D]); mxr = din("mxr", [1, D])
    cmk = din("cmk", [1, D]); cmr = din("cmr", [1, D])
    wgH = din("wgH", [24 * 128, 1024], I32)
    wuH = din("wuH", [24 * 128, 1024], I32)
    wdT = din("wdT", [H1, D], I32)
    out_t = nc.dram_tensor("o", [T, D], F32, kind="ExternalOutput").ap()

    x1_h = nc.dram_tensor("x1_h", [T, D], F32).ap()
    w1q_h = nc.dram_tensor("w1q_h", [D, H0], BF16).ap()
    acc_h = nc.dram_tensor("acc_h", [32, 128, T], F32).ap()
    w2q_h = nc.dram_tensor("w2q_h", [H0, D], BF16).ap()
    cc_in = nc.dram_tensor("cc_in", [1, 16], F32)
    cc_out = nc.dram_tensor("cc_out", [1, 16], F32)
    wtk_b = nc.dram_tensor("wtk_b", [8 * 128, 1024], BF16).ap()
    wtv_b = nc.dram_tensor("wtv_b", [8 * 128, 1024], BF16).ap()
    wtr_b = nc.dram_tensor("wtr_b", [8 * 128, 1024], BF16).ap()
    wck_b = nc.dram_tensor("wck_b", [32 * 128, 1024], BF16).ap()
    wcv_b = nc.dram_tensor("wcv_b", [H0, D], BF16).ap()
    wcr_b = nc.dram_tensor("wcr_b", [D, D], BF16).ap()
    wg_b = nc.dram_tensor("wg_b", [24 * 128, 1024], BF16).ap()
    wu_b = nc.dram_tensor("wu_b", [24 * 128, 1024], BF16).ap()
    wd_b = nc.dram_tensor("wd_b", [H1, D], BF16).ap()

    ident = nc.alloc_sbuf_tensor("ident", [128, 128], BF16)
    identf = nc.alloc_sbuf_tensor("identf", [128, 128], F32)
    ones_r = nc.alloc_sbuf_tensor("ones_r", [1, 128], F32)
    ones_cb = nc.alloc_sbuf_tensor("ones_cb", [128, 1], BF16)
    ones_rb1 = nc.alloc_sbuf_tensor("ones_rb1", [1, 128], BF16)
    sc0 = nc.alloc_sbuf_tensor("sc0", [128, NT], F32)
    sb0 = nc.alloc_sbuf_tensor("sb0", [128, NT], F32)
    qsc = nc.alloc_sbuf_tensor("qsc", [128, NT], F32)
    qsb = nc.alloc_sbuf_tensor("qsb", [128, NT], F32)
    qp = nc.alloc_sbuf_tensor("qp", [128, 12], F32)
    (QP_XHI, QP_XLO, QP_SMM1, QP_INVSH, QP_HHI, QP_HLO, QP_SMM2,
     QP_IW1, QP_IW2, QP_INVSX) = range(10)

    def qpc(i):
        return qp[:, i:i + 1]

    rowm = nc.alloc_sbuf_tensor("rowm", [128, 16], F32)
    row16 = nc.alloc_sbuf_tensor("row16", [1, 16], F32)
    roww = nc.alloc_sbuf_tensor("roww", [1, 16], F32)
    b1c = nc.alloc_sbuf_tensor("b1c", [128, 32], F32)
    b2b = nc.alloc_sbuf_tensor("b2b", [128, 1024], F32)
    chp = nc.alloc_sbuf_tensor("chp", [128, 8 * 16], F32)
    (NP_W, NP_EW, NP_U, NP_G1, NP_B1, NP_G2, NP_B2, NP_MK, NP_MV, NP_MR,
     NP_CK, NP_CR, NP_HB, NP_H2B, NP_EU) = range(15)

    def chcol(i, g):
        return chp[:, i * 8 + g:i * 8 + g + 1]

    wkvs = nc.alloc_sbuf_tensor("wkvs", [128, 32], F32)

    def ppe(g):
        return wkvs[:, g:g + 1]

    def aae(g):
        return wkvs[:, 8 + g:8 + g + 1]

    lnrow = nc.alloc_sbuf_tensor("lnrow", [128, 16], F32)

    PS0 = nc.alloc_psum_tensor("PS0", [128, 512], F32)
    PS1 = nc.alloc_psum_tensor("PS1", [128, 512], F32)
    PS2 = nc.alloc_psum_tensor("PS2", [128, 512], F32)
    PS3 = nc.alloc_psum_tensor("PS3", [128, 512], F32)
    PSY = nc.alloc_psum_tensor("PSY", [128, 512], F32)
    pS = nc.alloc_psum_tensor("pS", [64, 512], F32)
    ptrb = nc.alloc_psum_tensor("ptrb", [128, 1024], BF16)
    pb2 = nc.alloc_psum_tensor("pb2", [128, 512], F32)

    AR = Arena(p, "arena", 200 * 1024)

    cvals = nc.alloc_sbuf_tensor("cvals", [128, 8], F32)
    _creg = {}

    def constc(v):
        if v not in _creg:
            i = len(_creg)
            assert i < 8
            p.mset(cvals[:, i:i + 1], v, ["cvals"], r=["cvals"])
            _creg[v] = cvals[:, i:i + 1]
        return _creg[v]

    p.constc = constc

    p.mset(ident[:, :], 1.0, ["ident"])
    p.afsel(ident[:, :], ALU.is_equal, 0.0, [[1, 128]], -1, ["ident"], ["ident"])
    p.mset(identf[:, :], 1.0, ["identf"])
    p.afsel(identf[:, :], ALU.is_equal, 0.0, [[1, 128]], -1, ["identf"], ["identf"])
    p.mset(ones_r[:, :], 1.0, ["ones_r"])
    p.mset(ones_cb[:, :], 1.0, ["ones_cb"])
    p.mset(ones_rb1[:, :], 1.0, ["ones_rb1"])
    p.mset(wkvs[:, :], 0.0, ["wkvs"])
    for g in range(8):
        p.mset(ppe(g), -1e38, ["wkvs"], r=["wkvs"])

    for src, idx in [(tdv, NP_W), (tfv, NP_U), (l1g, NP_G1), (l1b, NP_B1),
                     (l2g, NP_G2), (l2b, NP_B2), (mxk, NP_MK), (mxv, NP_MV),
                     (mxr, NP_MR), (cmk, NP_CK), (cmr, NP_CR)]:
        p.dma("y", chp[:, idx * 8:(idx + 1) * 8],
              src.rearrange("one (g np) -> (one np) g", np=128), "chp", slow=True)
    p.act(chp[:, NP_EW * 8:(NP_EW + 1) * 8], chp[:, NP_W * 8:(NP_W + 1) * 8],
          AF.Exp, r=["chp"], w=["chp"])
    p.act(chp[:, NP_EU * 8:(NP_EU + 1) * 8], chp[:, NP_U * 8:(NP_U + 1) * 8],
          AF.Exp, r=["chp"], w=["chp"])
    p.mset(chp[:, NP_HB * 8:(NP_H2B + 1) * 8], 0.0, ["chp"], r=["chp"])
    p.dma("y", b1c[:, :], b1v.rearrange("one (c np) -> (one np) c", np=128), "b1c",
          slow=True)

    b2row, b2row_b = AR.alloc("b2row", 175 * 1024, 4096, F32)
    p.dma("y", b2row[0:1, :], b2v[0:1, :], b2row_b)
    for hh in range(2):
        p.op("t", lambda E, hh=hh: E.matmul(pb2[:, :], ones_r[0:1, :],
             b2row[0:1, hh * 512:(hh + 1) * 512], start=True, stop=True),
             r=[b2row_b, "ones_r"], w=["pb2"])
        p.cp("v", b2b[:, hh * 512:(hh + 1) * 512], pb2[:, :], r=["pb2"], w=["b2b"])

    def bcast_col(src_ap, dst_ap, rbufs, wbuf):
        p.op("t", lambda E: E.matmul(pb2[:, 0:1], ones_r[0:1, :], src_ap,
             start=True, stop=True), r=list(rbufs) + ["ones_r"], w=["pb2"])
        p.cp("v", dst_ap, pb2[:, 0:1], r=["pb2"], w=[wbuf])

    def part_max(src_ap, src_buf, dst_row_ap, dst_buf, tmpcol):
        p.trans(pb2[0:1, 0:128], src_ap, identf[:, :], r=[src_buf, "identf"], w=["pb2"])
        p.red("v", rowm[0:1, tmpcol:tmpcol + 1], pb2[0:1, 0:128], ALU.max,
              r=["pb2"], w=["rowm"])
        p.dma("g", dst_row_ap, rowm[0:1, tmpcol:tmpcol + 1], dst_buf, in_buf="rowm")

    xr = [AR.alloc(f"xrA{i}", i * 4096, 4096, F32) for i in range(4)]
    scrA, scrA_b = AR.alloc("scrA", 175 * 1024, 4096, F32)
    mnmx, mnmx_b = AR.alloc("mnmxA", 179 * 1024, 64, F32)
    p.mset(mnmx[:, 0:2], -1e38, [mnmx_b])

    wst = [AR.alloc(f"wstg{i}", 180 * 1024 + i * 4 * 1024, 4 * 1024, F32)
           for i in range(2)]
    wqo = [AR.alloc(f"wqo{i}", 188 * 1024 + i * 2 * 1024, 2 * 1024, BF16)
           for i in range(2)]
    wmx, wmx_b = AR.alloc("wmx", 192 * 1024, 64, F32)

    def quant_weight(wsrc, nchunks, X, hbm_dst, iw_slot, tag):
        iw_col = QP_IW1 if tag == "w1" else QP_IW2
        pieces = []
        nh = max(1, X // 1024)
        Xp = X // nh
        for ci in range(nchunks):
            for hswp in range(nh):
                pieces.append((ci, hswp * Xp, Xp))
        p.mset(wmx[:, 0:1], 0.0, [wmx_b])
        for pi, (ci, c0, Xp_) in enumerate(pieces):
            a, b = wst[pi % 2]
            p.dma("y", a[:, 0:Xp_], wsrc[ci * 128:(ci + 1) * 128, c0:c0 + Xp_], b)
            p.red("v", wmx[:, 1:2], a[:, 0:Xp_], ALU.max, r=[b], w=[wmx_b], absval=True)
            p.tt("v", wmx[:, 0:1], wmx[:, 0:1], wmx[:, 1:2], ALU.max, r=[wmx_b], w=[wmx_b])
        part_max(wmx[:, 0:1], wmx_b, roww[0:1, iw_slot:iw_slot + 1], f"sw{tag}", 12)
        p.ts("v", roww[0:1, 4 + iw_slot:5 + iw_slot], roww[0:1, iw_slot:iw_slot + 1],
             1.0 / 127.0, 1e-8, ALU.mult, ALU.max, r=[f"sw{tag}", "roww"], w=["roww"])
        p.recip(roww[0:1, 6:7], roww[0:1, 4 + iw_slot:5 + iw_slot], r=["roww"], w=["roww"])
        bcast_col(roww[0:1, 6:7], qpc(iw_col), ["roww"], "qp")
        for pi, (ci, c0, Xp_) in enumerate(pieces):
            a, b = wst[pi % 2]
            qa, qb = wqo[pi % 2]
            p.dma("y", a[:, 0:Xp_], wsrc[ci * 128:(ci + 1) * 128, c0:c0 + Xp_], b)
            p.ts("v", a[:, 0:Xp_], a[:, 0:Xp_], qpc(iw_col), MAGIC, ALU.mult, ALU.add,
                 r=[b, "qp"], w=[b])
            p.ts("v", a[:, 0:Xp_], a[:, 0:Xp_], MAGIC, 127.0, ALU.subtract, ALU.min,
                 r=[b], w=[b])
            p.ts("v", qa[:, 0:Xp_], a[:, 0:Xp_], -127.0, None, ALU.max, r=[b], w=[qb])
            p.dma("g", hbm_dst[ci * 128:(ci + 1) * 128, c0:c0 + Xp_], qa[:, 0:Xp_],
                  f"{tag}h", in_buf=qb)


    wqv, wqv_b = AR.alloc("wqv", 16 * 1024, 48 * 1024, BF16)
    wou, wou_b = AR.alloc("wou", 64 * 1024, 16 * 1024, BF16)
    tri4, tri4_b = AR.alloc("tri4", 80 * 1024, 1024, BF16)
    S_f, S_f_b = AR.alloc("S_f", 81 * 1024, 4096, F32)
    S_b_, S_b_b = AR.alloc("S_b", 85 * 1024, 2048, BF16)
    carry_f, carry_f_b = AR.alloc("carry_f", 87 * 1024, 4096, F32)
    carry_bb, carry_b_b = AR.alloc("carry_bb", 91 * 1024, 2048, BF16)
    def _cset(base, tag):
        ns = {}
        off = [("xnb", 0, 2048, BF16), ("xnf", 2048, 2048, BF16),
               ("eqs", 4096, 4096, F32), ("keb", 8192, 2048, BF16),
               ("lnk", 10240, 4096, F32), ("scC", 14336, 4096, F32),
               ("qsbf", 18432, 2048, BF16), ("qsf", 20480, 4096, BF16),
               ("kef", 24576, 4096, BF16), ("vtm", 28672, 2048, BF16),
               ("atb", 30720, 1024, BF16), ("atb2", 31744, 1024, BF16),
               ("ytm", 32768, 2048, BF16), ("yf", 34816, 2048, BF16),
               ("s16", 36864, 128, F32), ("x1p", 37376, 4096, F32)]
        for nm, o, nb, dt in off:
            ns[nm] = AR.alloc(f"{nm}{tag}", base + o, nb, dt)
        return ns

    csets = [_cset(93 * 1024, "0"), _cset(134 * 1024, "1")]

    p.mset(tri4[:, :], 1.0, [tri4_b])
    for i in range(4):
        p.afsel(tri4[:, i * 128:(i + 1) * 128], ALU.is_ge, 0.0, [[1, 128]], -1,
                [tri4_b], [tri4_b])
    p.mset(S_f[:, :], 0.0, [S_f_b])
    p.mset(S_b_[:, :], 0.0, [S_b_b])
    p.mset(carry_f[:, :], 0.0, [carry_f_b])
    p.mset(carry_bb[:, :], 0.0, [carry_b_b])

    for dc in range(8):
        p.dma("g", wqv[:, dc * 3072:(dc + 1) * 3072],
              wqkvT[dc * 128:(dc + 1) * 128, :], wqv_b)
        p.dma("g", wou[:, dc * 1024:(dc + 1) * 1024],
              w_out[dc * 128:(dc + 1) * 128, :], wou_b)
    for dst, srcw, nm in [(wtk_b, wtkH, "wtkb"), (wtv_b, wtvH, "wtvb"),
                          (wtr_b, wtrH, "wtrb"), (wck_b, wckH, "wckb"),
                          (wcv_b, wcv, "wcvb"), (wcr_b, wcr, "wcrb"),
                          (wg_b, wgH, "wgb"), (wu_b, wuH, "wub"),
                          (wd_b, wdT, "wdb")]:
        p.dma("g", dst, srcw, nm)

    def wq_ap(dc, lo, n):
        return wqv[:, dc * 3072 + lo:dc * 3072 + lo + n]

    for j in range(NT):
        cs = csets[j % 2]
        (xnb, xnb_b) = cs["xnb"]; (xnf, xnf_b) = cs["xnf"]
        (eqs, eqs_b) = cs["eqs"]; (keb, keb_b) = cs["keb"]
        (lnk, lnk_b) = cs["lnk"]; (scC, scC_b) = cs["scC"]
        (qsbf, qsbf_b) = cs["qsbf"]; (qsf, qsf_b) = cs["qsf"]
        (kef, kef_b) = cs["kef"]; (vtm, vtm_b) = cs["vtm"]
        (atb, atb_b) = cs["atb"]; (atb2, atb2_b) = cs["atb2"]
        (ytm, ytm_b) = cs["ytm"]; (yf, yf_b) = cs["yf"]
        (s16, s16_b) = cs["s16"]; (x1p, x1p_b) = cs["x1p"]
        xa, xb = xr[j % 4]
        p.dma("y", xa, x_in[j * 128:(j + 1) * 128, :], xb)
        p.red("v", rowm[:, 0:1], xa, ALU.add, r=[xb], w=["rowm"])
        p.act(scrA, xa, AF.Square, r=[xb], w=[scrA_b, "rowm"], accum=rowm[:, 1:2])
        p.red("v", rowm[:, 2:3], xa, ALU.max, r=[xb], w=["rowm"])
        p.red("v", rowm[:, 3:4], xa, ALU.min, r=[xb], w=["rowm"])
        p.ts("v", rowm[:, 4:5], rowm[:, 0:1], 1.0 / D, None, ALU.mult, r=["rowm"], w=["rowm"])
        p.ts("v", rowm[:, 5:6], rowm[:, 1:2], 1.0 / D, None, ALU.mult, r=["rowm"], w=["rowm"])
        p.tt("v", rowm[:, 6:7], rowm[:, 4:5], rowm[:, 4:5], ALU.mult, r=["rowm"], w=["rowm"])
        p.tt("v", rowm[:, 6:7], rowm[:, 5:6], rowm[:, 6:7], ALU.subtract, r=["rowm"], w=["rowm"])
        p.act(rowm[:, 7:8], rowm[:, 6:7], AF.Ln, r=["rowm"], w=["rowm"], bias=LN_EPS)
        p.act(sc0[:, j:j + 1], rowm[:, 7:8], AF.Exp, r=["rowm"], w=["sc0"], scale=-0.5)
        p.tt("v", sb0[:, j:j + 1], rowm[:, 4:5], sc0[:, j:j + 1], ALU.mult,
             r=["rowm", "sc0"], w=["sb0"])
        p.ts("v", sb0[:, j:j + 1], sb0[:, j:j + 1], -1.0, None, ALU.mult, r=["sb0"], w=["sb0"])
        p.tt("v", rowm[:, 8:9], rowm[:, 3:4], rowm[:, 4:5], ALU.subtract, r=["rowm"], w=["rowm"])
        p.tt("v", rowm[:, 8:9], rowm[:, 8:9], sc0[:, j:j + 1], ALU.mult,
             r=["rowm", "sc0"], w=["rowm"])
        p.ts("v", rowm[:, 8:9], rowm[:, 8:9], -1.0, None, ALU.mult, r=["rowm"], w=["rowm"])
        p.tt("v", mnmx[:, 0:1], mnmx[:, 0:1], rowm[:, 8:9], ALU.max,
             r=["rowm", mnmx_b], w=[mnmx_b])
        p.tt("v", rowm[:, 9:10], rowm[:, 2:3], rowm[:, 4:5], ALU.subtract, r=["rowm"], w=["rowm"])
        p.tt("v", rowm[:, 9:10], rowm[:, 9:10], sc0[:, j:j + 1], ALU.mult,
             r=["rowm", "sc0"], w=["rowm"])
        p.tt("v", mnmx[:, 1:2], mnmx[:, 1:2], rowm[:, 9:10], ALU.max,
             r=["rowm", mnmx_b], w=[mnmx_b])

        p.act(xnb, xa, AF.Identity, r=[xb, "sc0", "sb0"], w=[xnb_b],
              bias=sb0[:, j:j + 1], scale=sc0[:, j:j + 1])
        p.tr_group([(ptrb[:, dc * 128:(dc + 1) * 128],
                     xnb[:, dc * 128:(dc + 1) * 128], ident[:, :])
                    for dc in range(8)], r=[xnb_b, "ident"], w=["ptrb"])
        p.cp("v", xnf, ptrb[:, :], r=["ptrb"], w=[xnf_b])
        pcs_banks = [(PS0, "PS0"), (PS1, "PS1"), (PS2, "PS2"), (PS3, "PS3"),
                     (PS0, "PS0"), (PS1, "PS1")]
        for pc in range(6):
            ps, psn = pcs_banks[pc]
            p.mm_group([(ps[:, :], xnf[:, dc * 128:(dc + 1) * 128],
                         wq_ap(dc, pc * 512, 512), dc == 0, dc == 7)
                        for dc in range(8)], r=[xnf_b, wqv_b], w=[psn])
            if pc == 0:
                p.act(eqs[:, 0:512], ps[:, :], AF.Exp, r=[psn], w=[eqs_b], scale=ASC)
            elif pc == 1:
                p.act(eqs[:, 512:1024], ps[:, :], AF.Exp, r=[psn], w=[eqs_b], scale=ASC)
            elif pc in (2, 3):
                o = (pc - 2) * 512
                p.act(scC[:, o:o + 512], ps[:, :], AF.Exp, r=[psn], w=[scC_b], scale=ASC)
                p.ts("v", keb[:, o:o + 512], scC[:, o:o + 512], 1e-6, None, ALU.add,
                     r=[scC_b], w=[keb_b])
            else:
                o = (pc - 4) * 512
                p.cp("v", vtm[:, o:o + 512], ps[:, :], r=[psn], w=[vtm_b])
        for hh, (ps, psn) in enumerate([(PS2, "PS2"), (PS3, "PS3")]):
            mms = [(ps[:, :], tri4[:, 0:128], keb[:, hh * 512:(hh + 1) * 512],
                    True, j == 0)]
            if j > 0:
                mms.append((ps[:, :], ones_rb1[0:1, :],
                            carry_bb[0:1, hh * 512:(hh + 1) * 512], False, True))
            p.mm_group(mms, r=[tri4_b, keb_b, "ones_rb1", carry_b_b], w=[psn])
            p.act(lnk[:, hh * 512:(hh + 1) * 512], ps[:, :], AF.Ln, r=[psn], w=[lnk_b])
        if j < NT - 1:
            for hh in range(2):
                p.op("t", lambda E, hh=hh: E.matmul(pb2[0:1, :], ones_cb[:, 0:1],
                     keb[:, hh * 512:(hh + 1) * 512], start=True, stop=True),
                     r=["ones_cb", keb_b], w=["pb2"])
                p.tt("v", carry_f[0:1, hh * 512:(hh + 1) * 512],
                     carry_f[0:1, hh * 512:(hh + 1) * 512], pb2[0:1, :], ALU.add,
                     r=["pb2", carry_f_b], w=[carry_f_b])
            p.cp("v", carry_bb[0:1, :], carry_f[0:1, :], r=[carry_f_b], w=[carry_b_b])
        p.red("v", s16[:, 0:16], eqs[:, :].rearrange("np (h l) -> np h l", l=64),
              ALU.add, r=[eqs_b], w=[s16_b])
        p.act(s16[:, 16:32], s16[:, 0:16], AF.Ln, r=[s16_b], w=[s16_b])
        p.tt("v", lnk[:, :].rearrange("np (h l) -> np h l", l=64),
             lnk[:, :].rearrange("np (h l) -> np h l", l=64),
             s16[:, 16:32].unsqueeze(2).broadcast_to([128, 16, 64]),
             ALU.add, r=[lnk_b, s16_b], w=[lnk_b])
        p.act(scC, lnk[:, :], AF.Exp, r=[lnk_b], w=[scC_b], scale=-1.0)
        p.tt("v", qsbf, eqs, scC, ALU.mult, r=[eqs_b, scC_b], w=[qsbf_b])
        for rnd in range(2):
            p.tr_group([(ptrb[0:64, hi * 128:(hi + 1) * 128],
                         qsbf[:, (rnd * 8 + hi) * 64:(rnd * 8 + hi + 1) * 64],
                         ident[:, :]) for hi in range(8)],
                       r=[qsbf_b, "ident"], w=["ptrb"])
            p.cp("v", qsf[0:64, rnd * 1024:(rnd + 1) * 1024], ptrb[0:64, :],
                 r=["ptrb"], w=[qsf_b])
            p.tr_group([(ptrb[0:64, hi * 128:(hi + 1) * 128],
                         keb[:, (rnd * 8 + hi) * 64:(rnd * 8 + hi + 1) * 64],
                         ident[:, :]) for hi in range(8)],
                       r=[keb_b, "ident"], w=["ptrb"])
            p.cp("v", kef[0:64, rnd * 1024:(rnd + 1) * 1024], ptrb[0:64, :],
                 r=["ptrb"], w=[kef_b])

        def hslice(buf_ap, h):
            return buf_ap[0:64, h * 128:(h + 1) * 128]

        for quad in range(4):
            ps, psn = (PS2, "PS2") if quad % 2 == 0 else (PS3, "PS3")
            ab, abn = (atb, atb_b) if quad % 2 == 0 else (atb2, atb2_b)
            p.mm_group([(ps[:, hi * 128:(hi + 1) * 128], hslice(kef, quad * 4 + hi),
                         hslice(qsf, quad * 4 + hi), True, True)
                        for hi in range(4)], r=[kef_b, qsf_b], w=[psn])
            p.tt("v", ab[:, :], ps[:, :], tri4[:, :], ALU.mult, r=[psn, tri4_b], w=[abn])
        for half in range(2):
            mms = []
            for qd in (half * 2, half * 2 + 1):
                ab, abn = (atb, atb_b) if qd % 2 == 0 else (atb2, atb2_b)
                for hi in range(4):
                    h = qd * 4 + hi
                    mms.append((PSY[:, (h % 8) * 64:(h % 8) * 64 + 64],
                                ab[:, hi * 128:(hi + 1) * 128],
                                vtm[:, h * 64:(h + 1) * 64], True, j == 0))
                    if j > 0:
                        mms.append((PSY[:, (h % 8) * 64:(h % 8) * 64 + 64],
                                    hslice(qsf, h),
                                    S_b_[0:64, h * 64:(h + 1) * 64], False, True))
            p.mm_group(mms, r=[atb_b, atb2_b, vtm_b, qsf_b, S_b_b], w=["PSY"])
            p.cp("v", ytm[:, half * 512:(half + 1) * 512], PSY[:, :],
                 r=["PSY"], w=[ytm_b])
        for half in range(2):
            p.mm_group([(pS[0:64, hi * 64:(hi + 1) * 64],
                         keb[:, (half * 8 + hi) * 64:(half * 8 + hi + 1) * 64],
                         vtm[:, (half * 8 + hi) * 64:(half * 8 + hi + 1) * 64],
                         True, True) for hi in range(8)],
                       r=[keb_b, vtm_b], w=["pS"])
            p.tt("v", S_f[0:64, half * 512:(half + 1) * 512],
                 S_f[0:64, half * 512:(half + 1) * 512], pS[0:64, :], ALU.add,
                 r=["pS", S_f_b], w=[S_f_b])
        p.cp("v", S_b_[0:64, :], S_f[0:64, :], r=[S_f_b], w=[S_b_b])
        p.tr_group([(ptrb[:, dc * 128:(dc + 1) * 128],
                     ytm[:, dc * 128:(dc + 1) * 128], ident[:, :])
                    for dc in range(8)], r=[ytm_b, "ident"], w=["ptrb"])
        p.cp("v", yf, ptrb[:, :], r=["ptrb"], w=[yf_b])
        for hh, (ps, psn) in enumerate([(PS0, "PS0"), (PS1, "PS1")]):
            p.mm_group([(ps[:, :], yf[:, dc * 128:(dc + 1) * 128],
                         wou[:, dc * 1024 + hh * 512:dc * 1024 + hh * 512 + 512],
                         dc == 0, dc == 7) for dc in range(8)],
                       r=[yf_b, wou_b], w=[psn])
            p.tt("v", x1p[:, hh * 512:(hh + 1) * 512], xa[:, hh * 512:(hh + 1) * 512],
                 ps[:, :], ALU.add, r=[xb, psn], w=[x1p_b])
        p.dma("y", (out_t if upto == "attn" else x1_h)[j * 128:(j + 1) * 128, :],
              x1p, "x1hC", in_buf=x1p_b)
        if j == min(8, NT - 2):
            quant_weight(w1T, 8, H0, w1q_h, 0, "w1")
        if j == min(18, NT - 1):
            quant_weight(w2T, 32, D, w2q_h, 1, "w2")

    part_max(mnmx[:, 0:1], mnmx_b, row16[0:1, 0:1], "r16a", 10)
    part_max(mnmx[:, 1:2], mnmx_b, row16[0:1, 1:2], "r16b", 11)
    p.dma("g", cc_in.ap()[0:1, 0:2], row16[0:1, 0:2], "ccinA", in_buf="r16b")
    p.buf("ccinA").readers[("xa",)] = p.buf("r16a").writer
    p.op("g", lambda E: E.collective_compute("AllReduce", ALU.max,
         replica_groups=[list(range(NCORES))], ins=[cc_in.ap()[0:1, 0:2]],
         outs=[cc_out.ap()[0:1, 0:2]]), r=["ccinA"], w=["ccoutA"])
    p.dma("g", row16[0:1, 2:4], cc_out.ap()[0:1, 0:2], "ccresA", in_buf="ccoutA")

    def derive_act_qparams(nmin_ap, gmax_ap, res_buf, sw_ap, hi_c, lo_c, smm_c,
                           inv_c):
        r, rb = row16, "row16"
        p.tt("v", r[0:1, 4:5], gmax_ap, nmin_ap, ALU.add, r=[res_buf, rb], w=[rb])
        p.ts("v", r[0:1, 4:5], r[0:1, 4:5], 1.0 / 255.0, 1e-8, ALU.mult, ALU.max,
             r=[rb], w=[rb])
        p.recip(r[0:1, 5:6], r[0:1, 4:5], r=[rb], w=[rb])
        p.tt("v", r[0:1, 6:7], nmin_ap, r[0:1, 5:6], ALU.mult, r=[rb], w=[rb])
        p.ts("v", r[0:1, 6:7], r[0:1, 6:7], MAGIC, MAGIC, ALU.add, ALU.subtract,
             r=[rb], w=[rb])
        p.ts("v", r[0:1, 6:7], r[0:1, 6:7], 128.0, 127.0, ALU.subtract, ALU.min,
             r=[rb], w=[rb])
        p.ts("v", r[0:1, 6:7], r[0:1, 6:7], -128.0, None, ALU.max, r=[rb], w=[rb])
        p.ts("v", r[0:1, 7:8], r[0:1, 6:7], -1.0, 127.0, ALU.mult, ALU.add, r=[rb], w=[rb])
        p.ts("v", r[0:1, 8:9], r[0:1, 6:7], -1.0, -128.0, ALU.mult, ALU.add, r=[rb], w=[rb])
        p.tt("v", r[0:1, 9:10], r[0:1, 4:5], sw_ap, ALU.mult, r=[rb, "roww"], w=[rb])
        bcast_col(r[0:1, 7:8], qpc(hi_c), [rb], "qp")
        bcast_col(r[0:1, 8:9], qpc(lo_c), [rb], "qp")
        bcast_col(r[0:1, 9:10], qpc(smm_c), [rb], "qp")
        bcast_col(r[0:1, 5:6], qpc(inv_c), [rb], "qp")

    derive_act_qparams(row16[0:1, 2:3], row16[0:1, 3:4], "ccresA",
                       roww[0:1, 4:5], QP_XHI, QP_XLO, QP_SMM1, QP_INVSX)
    p.ts("v", qsc[:, :], sc0[:, :], qpc(QP_INVSX), None, ALU.mult,
         r=["sc0", "qp"], w=["qsc"])
    p.ts("v", qsb[:, :], sb0[:, :], qpc(QP_INVSX), None, ALU.mult,
         r=["sb0", "qp"], w=["qsb"])

    def emit_xq_tile(j, jj, xa, xb, scr, scr_b, xqtm, xqtm_b, xqf3, xqf_b):
        p.act(scr, xa, AF.Identity, r=[xb, "qsc", "qsb"], w=[scr_b],
              bias=qsb[:, j:j + 1], scale=qsc[:, j:j + 1])
        p.ts("v", scr, scr, MAGIC, MAGIC, ALU.add, ALU.subtract, r=[scr_b], w=[scr_b])
        p.ts("v", scr, scr, qpc(QP_XHI), None, ALU.min, r=[scr_b, "qp"], w=[scr_b])
        p.ts("v", xqtm, scr, qpc(QP_XLO), None, ALU.max, r=[scr_b, "qp"], w=[xqtm_b])
        p.tr_group([(ptrb[:, dc * 128:(dc + 1) * 128],
                     xqtm[:, dc * 128:(dc + 1) * 128], ident[:, :])
                    for dc in range(8)], r=[xqtm_b, "ident"], w=["ptrb"])
        p.cp("v", xqf3[:, :, jj * 128:(jj + 1) * 128],
             ptrb[:, :].rearrange("np (dc tc) -> np dc tc", tc=128),
             r=["ptrb"], w=[xqf_b])

    w1r, w1r_b = AR.alloc("w1r", 40 * 1024, 64 * 1024, BF16)
    xqfBs = [AR.alloc(f"xqfB{i}", 104 * 1024 + i * 8 * 1024, 8 * 1024, BF16)
             for i in range(2)]
    scrBs = [AR.alloc(f"scrB{i}", 120 * 1024 + i * 4096, 4096, F32) for i in range(2)]
    xqtmBs = [AR.alloc(f"xqtmB{i}", 128 * 1024 + i * 2048, 2048, BF16) for i in range(2)]
    accst = [AR.alloc(f"accst{i}", 132 * 1024 + 64 + i * 2048, 2048, F32)
             for i in range(2)]
    hmn, hmn_b = AR.alloc("hmn", 132 * 1024, 64, F32)
    p.mset(hmn[:, 0:2], -1e38, [hmn_b])
    for dc in range(8):
        p.dma("y", w1r[:, dc * 4096:(dc + 1) * 4096],
              w1q_h[dc * 128:(dc + 1) * 128, :], w1r_b, in_buf="w1h")

    for ch in range(T // 512):
        xqfB, xqfB_b = xqfBs[ch % 2]
        xqfB3 = xqfB.rearrange("np (dc tc) -> np dc tc", tc=512)
        for jj in range(4):
            j = ch * 4 + jj
            xa, xb = xr[j % 4]
            p.dma("g", xa, x_in[j * 128:(j + 1) * 128, :], xb)
            scrB, scrB_b = scrBs[jj % 2]
            xqtmB, xqtmB_b = xqtmBs[jj % 2]
            emit_xq_tile(j, jj, xa, xb, scrB, scrB_b, xqtmB, xqtmB_b, xqfB3, xqfB_b)
        for hg in range(32):
            ps = PS0 if hg % 2 == 0 else PS1
            psn = "PS0" if hg % 2 == 0 else "PS1"
            p.mm_group([(ps[:, :],
                         w1r[:, dc * 4096 + hg * 128:dc * 4096 + (hg + 1) * 128],
                         xqfB[:, dc * 512:(dc + 1) * 512], dc == 0, dc == 7)
                        for dc in range(8)], r=[w1r_b, xqfB_b], w=[psn])
            ac, ac_b = accst[hg % 2]
            p.cp("v", ac, ps[:, :], r=[psn], w=[ac_b])
            p.dma("y", acc_h[hg, :, ch * 512:(ch + 1) * 512], ac, "acch", in_buf=ac_b)
            rb = (hg % 2) * 8
            p.red("v", rowm[:, rb + 0:rb + 1], ac, ALU.max, r=[ac_b], w=["rowm"])
            p.red("v", rowm[:, rb + 1:rb + 2], ac, ALU.min, r=[ac_b], w=["rowm"])
            e2 = "v"
            p.act(rowm[:, rb + 2:rb + 3], rowm[:, rb + 0:rb + 1], AF.Gelu,
                  r=["rowm", "qp", "b1c"], w=["rowm"],
                  bias=b1c[:, hg:hg + 1], scale=qpc(QP_SMM1))
            p.act(rowm[:, rb + 3:rb + 4], rowm[:, rb + 1:rb + 2], AF.Gelu,
                  r=["rowm", "qp", "b1c"], w=["rowm"],
                  bias=b1c[:, hg:hg + 1], scale=qpc(QP_SMM1))
            p.act(rowm[:, rb + 4:rb + 5], rowm[:, rb + 0:rb + 1], AF.Identity,
                  r=["rowm", "qp", "b1c"], w=["rowm"],
                  bias=b1c[:, hg:hg + 1], scale=qpc(QP_SMM1))
            p.act(rowm[:, rb + 5:rb + 6], rowm[:, rb + 1:rb + 2], AF.Identity,
                  r=["rowm", "qp", "b1c"], w=["rowm"],
                  bias=b1c[:, hg:hg + 1], scale=qpc(QP_SMM1))
            p.ts(e2, rowm[:, rb + 4:rb + 5], rowm[:, rb + 4:rb + 5],
                 -0.751791534894327, None, ALU.is_ge, r=["rowm"], w=["rowm"])
            p.ts(e2, rowm[:, rb + 5:rb + 6], rowm[:, rb + 5:rb + 6],
                 -0.751791534894327, None, ALU.is_le, r=["rowm"], w=["rowm"])
            p.tt(e2, rowm[:, rb + 4:rb + 5], rowm[:, rb + 4:rb + 5],
                 rowm[:, rb + 5:rb + 6], ALU.mult, r=["rowm"], w=["rowm"])
            p.tt(e2, rowm[:, rb + 5:rb + 6], rowm[:, rb + 4:rb + 5],
                 p.constc(-0.16997120747990369), ALU.mult,
                 r=["rowm", "cvals"], w=["rowm"])
            p.ts(e2, rowm[:, rb + 4:rb + 5], rowm[:, rb + 4:rb + 5],
                 -65536.0, 65536.0, ALU.mult, ALU.add, r=["rowm"], w=["rowm"])
            p.tt(e2, rowm[:, rb + 4:rb + 5], rowm[:, rb + 4:rb + 5],
                 rowm[:, rb + 5:rb + 6], ALU.add, r=["rowm"], w=["rowm"])
            p.tt(e2, hmn[:, 1:2], hmn[:, 1:2], rowm[:, rb + 2:rb + 3], ALU.max,
                 r=["rowm", hmn_b], w=[hmn_b])
            p.tt(e2, hmn[:, 1:2], hmn[:, 1:2], rowm[:, rb + 3:rb + 4], ALU.max,
                 r=["rowm", hmn_b], w=[hmn_b])
            p.tt(e2, rowm[:, rb + 3:rb + 4], rowm[:, rb + 3:rb + 4],
                 rowm[:, rb + 2:rb + 3], ALU.min, r=["rowm"], w=["rowm"])
            p.tt(e2, rowm[:, rb + 3:rb + 4], rowm[:, rb + 3:rb + 4],
                 rowm[:, rb + 4:rb + 5], ALU.min, r=["rowm"], w=["rowm"])
            p.ts(e2, rowm[:, rb + 3:rb + 4], rowm[:, rb + 3:rb + 4], -1.0, None,
                 ALU.mult, r=["rowm"], w=["rowm"])
            p.tt(e2, hmn[:, 0:1], hmn[:, 0:1], rowm[:, rb + 3:rb + 4], ALU.max,
                 r=["rowm", hmn_b], w=[hmn_b])

    part_max(hmn[:, 0:1], hmn_b, row16[0:1, 0:1], "r16c", 10)
    part_max(hmn[:, 1:2], hmn_b, row16[0:1, 1:2], "r16d", 11)
    p.dma("g", cc_in.ap()[0:1, 4:6], row16[0:1, 0:2], "ccinB", in_buf="r16d")
    p.buf("ccinB").readers[("xc",)] = p.buf("r16c").writer
    p.op("g", lambda E: E.collective_compute("AllReduce", ALU.max,
         replica_groups=[list(range(NCORES))], ins=[cc_in.ap()[0:1, 4:6]],
         outs=[cc_out.ap()[0:1, 4:6]]), r=["ccinB"], w=["ccoutB"])
    p.dma("g", row16[0:1, 2:4], cc_out.ap()[0:1, 4:6], "ccresB", in_buf="ccoutB")
    derive_act_qparams(row16[0:1, 2:3], row16[0:1, 3:4], "ccresB",
                       roww[0:1, 5:6], QP_HHI, QP_HLO, QP_SMM2, QP_INVSH)

    if upto == "attn":
        p.final_wait("y", ["x1hC"])
        return nc

    hqDs = [AR.alloc(f"hqD{i}", 16 * 1024 + i * 32 * 1024, 32 * 1024, BF16)
            for i in range(2)]
    accld = [AR.alloc(f"accld{i}", 80 * 1024 + i * 2048, 2048, F32) for i in range(4)]
    x1r = [AR.alloc(f"x1rD{i}", 88 * 1024 + i * 4096, 4096, F32) for i in range(2)]
    scrDs = [AR.alloc(f"scrD{i}", 96 * 1024 + i * 4096, 4096, F32) for i in range(2)]
    w2s = [AR.alloc(f"w2s{i}", 104 * 1024 + i * 2 * 1024, 2 * 1024, BF16)
           for i in range(4)]
    dst_l0 = out_t if upto == "l0" else x1_h

    for ch in range(T // 512):
        hqD, hqD_b = hqDs[ch % 2]
        for hg in range(32):
            ac, ac_b = accld[hg % 4]
            p.dma("y", ac, acc_h[hg, :, ch * 512:(ch + 1) * 512], ac_b, in_buf="acch")
            scrD, scrD_b = scrDs[hg % 2]
            sD = scrD[:, 0:512]
            p.act(sD, ac, AF.Gelu, r=[ac_b, "qp", "b1c"], w=[scrD_b],
                  bias=b1c[:, hg:hg + 1], scale=qpc(QP_SMM1))
            p.ts("v", sD, sD, qpc(QP_INVSH), MAGIC, ALU.mult, ALU.add,
                 r=[scrD_b, "qp"], w=[scrD_b])
            p.ts("v", sD, sD, MAGIC, qpc(QP_HHI), ALU.subtract, ALU.min,
                 r=[scrD_b, "qp"], w=[scrD_b])
            p.ts("v", hqD[:, hg * 512:(hg + 1) * 512], sD, qpc(QP_HLO), None,
                 ALU.max, r=[scrD_b, "qp"], w=[hqD_b])
        for jp in range(2):
            for jj2 in range(2):
                j = ch * 4 + jp * 2 + jj2
                p.dma("y", x1r[jj2][0], x1_h[j * 128:(j + 1) * 128, :], x1r[jj2][1],
                      in_buf="x1hC")
            banks = [(PS2, "PS2"), (PS3, "PS3"), (PSY, "PSY"), (pb2, "pb2")]
            p.stream_begin(w=[b[1] for b in banks])
            for hg in range(32):
                ws, wsb = w2s[hg % 4]
                p.dma("y", ws, w2q_h[hg * 128:(hg + 1) * 128, :], wsb, in_buf="w2h")
                mms = []
                for jj2 in range(2):
                    jjj = jp * 2 + jj2
                    lhs = hqD[:, hg * 512 + jjj * 128:hg * 512 + (jjj + 1) * 128]
                    mms.append((banks[jj2 * 2][0][:, :], lhs, ws[:, 0:512],
                                hg == 0, hg == 31))
                    mms.append((banks[jj2 * 2 + 1][0][:, :], lhs, ws[:, 512:1024],
                                hg == 0, hg == 31))
                p.stream_step([hqD_b, wsb], mms)
            p.stream_end()
            for jj2 in range(2):
                j = ch * 4 + jp * 2 + jj2
                scrD, scrD_b = scrDs[jj2 % 2]
                for hh in range(2):
                    ps, psn = banks[jj2 * 2 + hh]
                    p.stt("v", scrD[:, hh * 512:(hh + 1) * 512], ps[:, :], qpc(QP_SMM2),
                          b2b[:, hh * 512:(hh + 1) * 512], ALU.mult, ALU.add,
                          r=[psn, "qp", "b2b"], w=[scrD_b])
                    p.tt("v", scrD[:, hh * 512:(hh + 1) * 512],
                         scrD[:, hh * 512:(hh + 1) * 512],
                         x1r[jj2][0][:, hh * 512:(hh + 1) * 512], ALU.add,
                         r=[scrD_b, x1r[jj2][1]], w=[scrD_b])
                p.dma("y", dst_l0[j * 128:(j + 1) * 128, :], scrD, "x1hD", in_buf=scrD_b)

    if upto == "l0":
        p.final_wait("y", ["x1hD"])
        return nc

    x1L = [AR.alloc(f"x1L{i}", i * 4096, 4096, F32) for i in range(4)]
    xn1 = [AR.alloc(f"xn1L{i}", 16 * 1024 + i * 4096, 4096, F32) for i in range(4)]
    xnf1, xnf1_b = AR.alloc("xnf1", 32 * 1024, 8 * 1024, BF16)
    xnf13 = xnf1.rearrange("np (dc tc) -> np dc tc", tc=512)
    hnL, hnL_b = AR.alloc("hnL", 40 * 1024, 4096, F32)
    hfm, hfm_b = AR.alloc("hfm", 44 * 1024, 8 * 2056, F32)
    HS = 514
    dscL, dscL_b = AR.alloc("dscL", 61 * 1024, 2048, F32)
    xkf, xkf_b = AR.alloc("xkf", 63 * 1024, 8 * 1024, BF16)
    xvf, xvf_b = AR.alloc("xvf", 71 * 1024, 8 * 1024, BF16)
    xrf, xrf_b = AR.alloc("xrf", 79 * 1024, 8 * 1024, BF16)
    rfm, rfm_b = AR.alloc("rfm", 87 * 1024, 8 * 1024, BF16)
    ybuf, ybuf_b = AR.alloc("ybuf", 95 * 1024, 8 * 1024, BF16)
    kmS, kmS_b = AR.alloc("kmS", 103 * 1024, 2048, F32)
    KpS, KpS_b = AR.alloc("KpS", 105 * 1024, 2048, F32)
    PbS, PbS_b = AR.alloc("PbS", 107 * 1024, 2056, F32)
    lnS, lnS_b = AR.alloc("lnS", 109 * 1024 + 64, 2056, F32)
    alS, alS_b = AR.alloc("alS", 111 * 1024 + 128, 2048, F32)
    bvS, bvS_b = AR.alloc("bvS", 113 * 1024 + 128, 2048, F32)
    aaS, aaS_b = AR.alloc("aaS", 115 * 1024 + 128, 2056, F32)
    vsb, vsb_b = AR.alloc("vsb", 117 * 1024 + 192, 2048, F32)
    ryf, ryf_b = AR.alloc("ryf", 120 * 1024, 8 * 1024, BF16)
    wsl = [AR.alloc(f"wslL{i}", 174 * 1024 + i * 4096, 4096, BF16) for i in range(4)]

    for ch in range(T // 512):
        for jj in range(4):
            j = ch * 4 + jj
            xa, xab = x1L[jj]
            p.dma("y", xa, x1_h[j * 128:(j + 1) * 128, :], xab, in_buf="x1hD")
            p.red("v", rowm[:, 0:1], xa, ALU.add, r=[xab], w=["rowm"])
            p.act(hnL, xa, AF.Square, r=[xab], w=[hnL_b, "rowm"], accum=rowm[:, 1:2])
            p.ts("v", rowm[:, 4:5], rowm[:, 0:1], 1.0 / D, None, ALU.mult, r=["rowm"], w=["rowm"])
            p.ts("v", rowm[:, 5:6], rowm[:, 1:2], 1.0 / D, None, ALU.mult, r=["rowm"], w=["rowm"])
            p.tt("v", rowm[:, 6:7], rowm[:, 4:5], rowm[:, 4:5], ALU.mult, r=["rowm"], w=["rowm"])
            p.tt("v", rowm[:, 6:7], rowm[:, 5:6], rowm[:, 6:7], ALU.subtract, r=["rowm"], w=["rowm"])
            p.act(rowm[:, 7:8], rowm[:, 6:7], AF.Ln, r=["rowm"], w=["rowm"], bias=LN_EPS)
            p.act(rowm[:, 7:8], rowm[:, 7:8], AF.Exp, r=["rowm"], w=["rowm"], scale=-0.5)
            p.tt("v", rowm[:, 8:9], rowm[:, 4:5], rowm[:, 7:8], ALU.mult, r=["rowm"], w=["rowm"])
            p.ts("v", rowm[:, 8:9], rowm[:, 8:9], -1.0, None, ALU.mult, r=["rowm"], w=["rowm"])
            p.act(xn1[jj][0], xa, AF.Identity, r=[xab, "rowm"], w=[xn1[jj][1]],
                  bias=rowm[:, 8:9], scale=rowm[:, 7:8])
            xna, xnab = xn1[jj]
            p.red("v", rowm[:, 0:1], xna, ALU.add, r=[xnab], w=["rowm"])
            p.act(hnL, xna, AF.Square, r=[xnab], w=[hnL_b, "rowm"], accum=rowm[:, 1:2])
            p.ts("v", rowm[:, 4:5], rowm[:, 0:1], 1.0 / D, None, ALU.mult, r=["rowm"], w=["rowm"])
            p.ts("v", rowm[:, 5:6], rowm[:, 1:2], 1.0 / D, None, ALU.mult, r=["rowm"], w=["rowm"])
            p.tt("v", rowm[:, 6:7], rowm[:, 4:5], rowm[:, 4:5], ALU.mult, r=["rowm"], w=["rowm"])
            p.tt("v", rowm[:, 6:7], rowm[:, 5:6], rowm[:, 6:7], ALU.subtract, r=["rowm"], w=["rowm"])
            p.act(rowm[:, 7:8], rowm[:, 6:7], AF.Ln, r=["rowm"], w=["rowm"], bias=LN_EPS)
            p.act(rowm[:, 7:8], rowm[:, 7:8], AF.Exp, r=["rowm"], w=["rowm"], scale=-0.5)
            p.tt("v", rowm[:, 8:9], rowm[:, 4:5], rowm[:, 7:8], ALU.mult, r=["rowm"], w=["rowm"])
            p.ts("v", rowm[:, 8:9], rowm[:, 8:9], -1.0, None, ALU.mult, r=["rowm"], w=["rowm"])
            p.act(hnL, xna, AF.Identity, r=[xnab, "rowm"], w=[hnL_b],
                  bias=rowm[:, 8:9], scale=rowm[:, 7:8])
            for half in range(2):
                ps, psn = (PS3, "PS3") if half == 0 else (PSY, "PSY")
                p.tr_group([(ps[:, q4 * 128:(q4 + 1) * 128],
                             hnL[:, (half * 4 + q4) * 128:(half * 4 + q4 + 1) * 128],
                             identf[:, :]) for q4 in range(4)],
                           r=[hnL_b, "identf"], w=[psn])
                for q4 in range(4):
                    dc = half * 4 + q4
                    p.act(hfm[:, dc * HS + 1 + jj * 128:dc * HS + 1 + (jj + 1) * 128],
                          ps[:, q4 * 128:(q4 + 1) * 128], AF.Identity,
                          r=[psn, "chp"], w=[hfm_b],
                          bias=chcol(NP_B1, dc), scale=chcol(NP_G1, dc))
            for half in range(2):
                ps, psn = (PS3, "PS3") if half == 0 else (PSY, "PSY")
                p.tr_group([(ps[:, q4 * 128:(q4 + 1) * 128],
                             xna[:, (half * 4 + q4) * 128:(half * 4 + q4 + 1) * 128],
                             identf[:, :]) for q4 in range(4)],
                           r=[xnab, "identf"], w=[psn])
                for q4 in range(4):
                    dc = half * 4 + q4
                    p.cp("v", xnf13[:, dc, jj * 128:(jj + 1) * 128],
                         ps[:, q4 * 128:(q4 + 1) * 128], r=[psn], w=[xnf1_b])
        for g in range(8):
            p.cp("v", hfm[:, g * HS:g * HS + 1], chcol(NP_HB, g), r=["chp"], w=[hfm_b])
        for g in range(8):
            h_g = hfm[:, g * HS + 1:g * HS + 513]
            hp_g = hfm[:, g * HS:g * HS + 512]
            p.tt("v", dscL, h_g, hp_g, ALU.subtract, r=[hfm_b], w=[dscL_b])
            p.ts("v", kmS, dscL, chcol(NP_MK, g), None, ALU.mult,
                 r=[dscL_b, "chp"], w=[kmS_b])
            p.tt("g", xkf[:, g * 512:(g + 1) * 512], kmS, hp_g, ALU.add,
                 r=[kmS_b, hfm_b], w=[xkf_b])
            p.stt("v", xvf[:, g * 512:(g + 1) * 512], dscL, chcol(NP_MV, g), hp_g,
                  ALU.mult, ALU.add, r=[dscL_b, "chp", hfm_b], w=[xvf_b])
            p.ts("v", KpS, dscL, chcol(NP_MR, g), None, ALU.mult,
                 r=[dscL_b, "chp"], w=[KpS_b])
            p.tt("g", xrf[:, g * 512:(g + 1) * 512], KpS, hp_g, ALU.add,
                 r=[KpS_b, hfm_b], w=[xrf_b])
        for g in range(8):
            p.cp("v", chcol(NP_HB, g), hfm[:, g * HS + 512:g * HS + 513],
                 r=[hfm_b], w=["chp"])
        for g in range(8):
            for wi, (wsrc, wsrcb, psj, psnj, xf, xfb) in enumerate([
                    (wtk_b, "wtkb", PS0, "PS0", xkf, xkf_b),
                    (wtv_b, "wtvb", PS1, "PS1", xvf, xvf_b),
                    (wtr_b, "wtrb", PS2, "PS2", xrf, xrf_b)]):
                ws, wsb = wsl[(g * 3 + wi) % 4]
                p.dma("y", ws[:, 0:1024], wsrc[g * 128:(g + 1) * 128, :], wsb,
                      in_buf=wsrcb)
                p.mm_group([(psj[:, :], ws[:, dc * 128:(dc + 1) * 128],
                             xf[:, dc * 512:(dc + 1) * 512], dc == 0, dc == 7)
                            for dc in range(8)], r=[wsb, xfb], w=[psnj])
            p.act(rfm[:, g * 512:(g + 1) * 512], PS2[:, :], AF.Identity,
                  r=["PS2"], w=[rfm_b])
            p.cp("v", vsb, PS1[:, :], r=["PS1"], w=[vsb_b])
            p.red("v", rowm[:, 0:1], PS0[:, :], ALU.max, r=["PS0"], w=["rowm"])
            p.tt("v", rowm[:, 0:1], rowm[:, 0:1], ppe(g), ALU.max, r=["rowm", "wkvs"], w=["rowm"])
            p.ts("v", rowm[:, 1:2], rowm[:, 0:1], -1.0, None, ALU.mult, r=["rowm"], w=["rowm"])
            p.act(kmS, PS0[:, :], AF.Identity, r=["PS0", "rowm"], w=[kmS_b],
                  bias=rowm[:, 1:2])
            p.act(KpS, kmS, AF.Exp, r=[kmS_b], w=[KpS_b])
            p.ts("v", bvS, KpS, chcol(NP_EW, g), None, ALU.mult, r=[KpS_b, "chp"], w=[bvS_b])
            p.act(PbS[:, 0:1], ppe(g), AF.Exp, r=["wkvs", "rowm"], w=[PbS_b],
                  bias=rowm[:, 1:2])
            p.scan("v", PbS[:, 1:513], chcol(NP_EW, g).broadcast_to([128, 512]),
                   bvS, PbS[:, 0:1], r=[bvS_b, "chp", PbS_b], w=[PbS_b])
            p.act(lnS[:, 0:513], PbS[:, 0:513], AF.Ln, r=[PbS_b], w=[lnS_b],
                  bias=1e-38)
            p.tt("v", ppe(g), lnS[:, 512:513], rowm[:, 0:1], ALU.add,
                 r=[lnS_b, "rowm"], w=["wkvs"])
            p.tt("v", dscL, lnS[:, 0:512], kmS, ALU.subtract, r=[lnS_b, kmS_b], w=[dscL_b])
            p.ts("v", alS, dscL, 0.0, None, ALU.min, r=[dscL_b], w=[alS_b])
            p.act(alS, alS, AF.Exp, r=[alS_b], w=[alS_b])
            p.ts("v", bvS, dscL, -1.0, 0.0, ALU.mult, ALU.min, r=[dscL_b], w=[bvS_b])
            p.act(bvS, bvS, AF.Exp, r=[bvS_b], w=[bvS_b])
            p.tt("v", bvS, bvS, vsb, ALU.mult, r=[bvS_b, vsb_b], w=[bvS_b])
            p.cp("v", aaS[:, 0:1], aae(g), r=["wkvs"], w=[aaS_b])
            p.scan("v", aaS[:, 1:513], alS, bvS, aaS[:, 0:1], r=[alS_b, bvS_b, aaS_b],
                   w=[aaS_b])
            p.cp("v", aae(g), aaS[:, 512:513], r=[aaS_b], w=["wkvs"])
            p.stt("v", kmS, KpS, chcol(NP_EU, g), vsb, ALU.mult, ALU.mult,
                  r=[KpS_b, "chp", vsb_b], w=[kmS_b])
            p.tt("v", alS, PbS[:, 0:512], aaS[:, 0:512], ALU.mult,
                 r=[PbS_b, aaS_b], w=[alS_b])
            p.tt("v", alS, alS, kmS, ALU.add, r=[alS_b, kmS_b], w=[alS_b])
            p.stt("v", bvS, KpS, chcol(NP_EU, g), PbS[:, 0:512], ALU.mult, ALU.add,
                  r=[KpS_b, "chp", PbS_b], w=[bvS_b])
            p.act(bvS, bvS, AF.Ln, r=[bvS_b], w=[bvS_b])
            p.act(bvS, bvS, AF.Exp, r=[bvS_b], w=[bvS_b], scale=-1.0)
            p.tt("v", ybuf[:, g * 512:(g + 1) * 512], alS, bvS, ALU.mult,
                 r=[alS_b, bvS_b], w=[ybuf_b])
        for g in range(8):
            p.act(dscL, rfm[:, g * 512:(g + 1) * 512], AF.Sigmoid, r=[rfm_b], w=[dscL_b])
            p.tt("v", ryf[:, g * 512:(g + 1) * 512], dscL, ybuf[:, g * 512:(g + 1) * 512],
                 ALU.mult, r=[dscL_b, ybuf_b], w=[ryf_b])
        for jj in range(4):
            p.tr_group([(ptrb[:, g * 128:(g + 1) * 128],
                         ryf[:, g * 512 + jj * 128:g * 512 + (jj + 1) * 128],
                         ident[:, :]) for g in range(8)],
                       r=[ryf_b, "ident"], w=["ptrb"])
            p.tt("v", xn1[jj][0], xn1[jj][0], ptrb[:, :], ALU.add,
                 r=["ptrb", xn1[jj][1]], w=[xn1[jj][1]])
            p.red("v", lnrow[:, jj:jj + 1], xn1[jj][0], ALU.add, r=[xn1[jj][1]], w=["lnrow"])
            p.act(hnL, xn1[jj][0], AF.Square, r=[xn1[jj][1]], w=[hnL_b, "lnrow"],
                  accum=lnrow[:, 4 + jj:5 + jj])
        hs, hs_b = AR.alloc(f"hs{ch}", 95 * 1024, 24 * 1024, BF16)
        for hg in range(24):
            if hg % 2 == 0:
                hg2 = hg // 2
                wsg, wsgb = wsl[hg2 % 2]
                wsu, wsub = wsl[2 + hg2 % 2]
                p.dma("y", wsg[:, 0:2048].rearrange("np (two h) -> np two h", two=2),
                      wg_b[hg2 * 256:(hg2 + 1) * 256, :].rearrange(
                          "(two np) h -> np two h", np=128), wsgb, in_buf="wgb")
                p.dma("y", wsu[:, 0:2048].rearrange("np (two h) -> np two h", two=2),
                      wu_b[hg2 * 256:(hg2 + 1) * 256, :].rearrange(
                          "(two np) h -> np two h", np=128), wsub, in_buf="wub")
            wo = (hg % 2) * 1024
            psg, psgn = (PS0, "PS0") if hg % 2 == 0 else (PS2, "PS2")
            psu, psun = (PS1, "PS1") if hg % 2 == 0 else (PS3, "PS3")
            p.mm_group([(psg[:, :], wsg[:, wo + dc * 128:wo + (dc + 1) * 128],
                         xnf1[:, dc * 512:(dc + 1) * 512], dc == 0, dc == 7)
                        for dc in range(8)], r=[wsgb, xnf1_b], w=[psgn])
            p.mm_group([(psu[:, :], wsu[:, wo + dc * 128:wo + (dc + 1) * 128],
                         xnf1[:, dc * 512:(dc + 1) * 512], dc == 0, dc == 7)
                        for dc in range(8)], r=[wsub, xnf1_b], w=[psun])
            p.act(hnL[:, 0:512], psg[:, :], AF.Sigmoid, r=[psgn], w=[hnL_b],
                  scale=1.0 / 127.0)
            p.tt("v", hnL[:, 512:1024], hnL[:, 0:512], psg[:, :], ALU.mult,
                 r=[hnL_b, psgn], w=[hnL_b])
            p.tt("v", hs[:, hg * 512:(hg + 1) * 512], hnL[:, 512:1024], psu[:, :],
                 ALU.mult, r=[hnL_b, psun], w=[hs_b])
        kk, kk_b = AR.alloc(f"kk{ch}", 61 * 1024, 32 * 1024, BF16)
        ckf, ckf_b = AR.alloc(f"ckf{ch}", 126 * 1024, 8 * 1024, BF16)
        crf, crf_b = AR.alloc(f"crf{ch}", 150 * 1024, 8 * 1024, BF16)
        d3s = [AR.alloc(f"d3{ch}_{i}", 158 * 1024 + i * 4096, 4096, F32)
               for i in range(4)]
        dall = [(PS0, "PS0"), (PS1, "PS1"), (PS2, "PS2"), (PS3, "PS3")]
        for jp in range(2):
            jjs = (2 * jp, 2 * jp + 1)
            p.stream_begin(w=["PS0", "PS1", "PS2", "PS3"])
            for hg in range(24):
                if hg % 2 == 0:
                    hg2 = hg // 2
                    ws, wsb = wsl[hg2 % 4]
                    p.dma("y", ws[:, 0:2048].rearrange("np (two h) -> np two h", two=2),
                          wd_b[hg2 * 256:(hg2 + 1) * 256, :].rearrange(
                              "(two np) h -> np two h", np=128), wsb, in_buf="wdb")
                wo = (hg % 2) * 1024
                mms = []
                for ji, jj in enumerate(jjs):
                    lhs = hs[:, hg * 512 + jj * 128:hg * 512 + (jj + 1) * 128]
                    for hh in range(2):
                        mms.append((dall[ji * 2 + hh][0][:, :], lhs,
                                    ws[:, wo + hh * 512:wo + hh * 512 + 512],
                                    hg == 0, hg == 23))
                p.stream_step([hs_b, wsb], mms)
            p.stream_end()
            for ji, jj in enumerate(jjs):
                d3, d3_b = d3s[jp * 2 + ji]
                for hh in range(2):
                    ps, psn = dall[ji * 2 + hh]
                    p.ts("v", d3[:, hh * 512:(hh + 1) * 512], ps[:, :], 127.0 ** -3,
                         None, ALU.mult, r=[psn], w=[d3_b])
        u3s = [AR.alloc(f"u3{ch}_{i}", 120 * 1024 + i * 4096, 4096, F32)
               for i in range(2)]
        for jj in range(4):
            p.ts("v", rowm[:, 4:5], lnrow[:, jj:jj + 1], 1.0 / D, None, ALU.mult,
                 r=["lnrow"], w=["rowm"])
            p.ts("v", rowm[:, 5:6], lnrow[:, 4 + jj:5 + jj], 1.0 / D, None, ALU.mult,
                 r=["lnrow"], w=["rowm"])
            p.tt("v", rowm[:, 6:7], rowm[:, 4:5], rowm[:, 4:5], ALU.mult, r=["rowm"], w=["rowm"])
            p.tt("v", rowm[:, 6:7], rowm[:, 5:6], rowm[:, 6:7], ALU.subtract, r=["rowm"], w=["rowm"])
            p.act(rowm[:, 7:8], rowm[:, 6:7], AF.Ln, r=["rowm"], w=["rowm"], bias=LN_EPS)
            p.act(rowm[:, 7:8], rowm[:, 7:8], AF.Exp, r=["rowm"], w=["rowm"], scale=-0.5)
            p.tt("v", rowm[:, 8:9], rowm[:, 4:5], rowm[:, 7:8], ALU.mult, r=["rowm"], w=["rowm"])
            p.ts("v", rowm[:, 8:9], rowm[:, 8:9], -1.0, None, ALU.mult, r=["rowm"], w=["rowm"])
            p.act(hnL, xn1[jj][0], AF.Identity, r=[xn1[jj][1], "rowm"], w=[hnL_b],
                  bias=rowm[:, 8:9], scale=rowm[:, 7:8])
            for half in range(2):
                ps, psn = (PS3, "PS3") if half == 0 else (PSY, "PSY")
                p.tr_group([(ps[:, q4 * 128:(q4 + 1) * 128],
                             hnL[:, (half * 4 + q4) * 128:(half * 4 + q4 + 1) * 128],
                             identf[:, :]) for q4 in range(4)],
                           r=[hnL_b, "identf"], w=[psn])
                for q4 in range(4):
                    dc = half * 4 + q4
                    p.act(hfm[:, dc * HS + 1 + jj * 128:dc * HS + 1 + (jj + 1) * 128],
                          ps[:, q4 * 128:(q4 + 1) * 128], AF.Identity,
                          r=[psn, "chp"], w=[hfm_b],
                          bias=chcol(NP_B2, dc), scale=chcol(NP_G2, dc))
        for g in range(8):
            p.cp("v", hfm[:, g * HS:g * HS + 1], chcol(NP_H2B, g), r=["chp"], w=[hfm_b])
        for g in range(8):
            h_g = hfm[:, g * HS + 1:g * HS + 513]
            hp_g = hfm[:, g * HS:g * HS + 512]
            p.tt("v", hnL[:, 0:512], h_g, hp_g, ALU.subtract, r=[hfm_b], w=[hnL_b])
            p.ts("v", hnL[:, 512:1024], hnL[:, 0:512], chcol(NP_CK, g), None,
                 ALU.mult, r=[hnL_b, "chp"], w=[hnL_b])
            p.tt("g", ckf[:, g * 512:(g + 1) * 512], hnL[:, 512:1024], hp_g, ALU.add,
                 r=[hnL_b, hfm_b], w=[ckf_b])
            p.stt("v", crf[:, g * 512:(g + 1) * 512], hnL[:, 0:512], chcol(NP_CR, g),
                  hp_g, ALU.mult, ALU.add, r=[hnL_b, "chp", hfm_b], w=[crf_b])
        for g in range(8):
            p.cp("v", chcol(NP_H2B, g), hfm[:, g * HS + 512:g * HS + 513],
                 r=[hfm_b], w=["chp"])
        for hg in range(32):
            if hg % 2 == 0:
                hg2 = hg // 2
                ws, wsb = wsl[hg2 % 4]
                p.dma("y", ws[:, 0:2048].rearrange("np (two h) -> np two h", two=2),
                      wck_b[hg2 * 256:(hg2 + 1) * 256, :].rearrange(
                          "(two np) h -> np two h", np=128), wsb, in_buf="wckb")
            wo = (hg % 2) * 1024
            ps, psn = (PS0, "PS0") if hg % 2 == 0 else (PS1, "PS1")
            p.mm_group([(ps[:, :], ws[:, wo + dc * 128:wo + (dc + 1) * 128],
                         ckf[:, dc * 512:(dc + 1) * 512], dc == 0, dc == 7)
                        for dc in range(8)], r=[wsb, ckf_b], w=[psn])
            p.ts("v", hnL[:, 0:512], ps[:, :], 0.0, None, ALU.max, r=[psn], w=[hnL_b])
            p.tt("v", kk[:, hg * 512:(hg + 1) * 512], hnL[:, 0:512], hnL[:, 0:512],
                 ALU.mult, r=[hnL_b], w=[kk_b])
        for jp in range(2):
            jjs = (2 * jp, 2 * jp + 1)
            cvbank = {jjs[0]: [(PS0, "PS0"), (PS1, "PS1")],
                      jjs[1]: [(PS2, "PS2"), (PS3, "PS3")]}
            p.stream_begin(w=["PS0", "PS1", "PS2", "PS3"])
            for hg in range(32):
                if hg % 2 == 0:
                    hg2 = hg // 2
                    ws, wsb = wsl[hg2 % 4]
                    p.dma("y", ws[:, 0:2048].rearrange("np (two h) -> np two h", two=2),
                          wcv_b[hg2 * 256:(hg2 + 1) * 256, :].rearrange(
                              "(two np) h -> np two h", np=128), wsb, in_buf="wcvb")
                wo = (hg % 2) * 1024
                mms = []
                for jj in jjs:
                    lhs = kk[:, hg * 512 + jj * 128:hg * 512 + (jj + 1) * 128]
                    mms.append((cvbank[jj][0][0][:, :], lhs, ws[:, wo:wo + 512],
                                hg == 0, hg == 31))
                    mms.append((cvbank[jj][1][0][:, :], lhs, ws[:, wo + 512:wo + 1024],
                                hg == 0, hg == 31))
                p.stream_step([kk_b, wsb], mms)
            p.stream_end()
            for ji, jj in enumerate(jjs):
                j = ch * 4 + jj
                u3, u3_b = u3s[ji]
                p.stream_begin(w=["PSY", "pb2"])
                for dc in range(8):
                    ws, wsb = wsl[dc % 4]
                    p.dma("y", ws[:, 0:1024], wcr_b[dc * 128:(dc + 1) * 128, :], wsb,
                          in_buf="wcrb")
                    lhs = crf[:, dc * 512 + jj * 128:dc * 512 + (jj + 1) * 128]
                    p.stream_step([crf_b, wsb],
                                  [(PSY[:, :], lhs, ws[:, 0:512], dc == 0, dc == 7),
                                   (pb2[:, :], lhs, ws[:, 512:1024], dc == 0, dc == 7)])
                p.stream_end()
                for hh, psn in enumerate(["PSY", "pb2"]):
                    ps = PSY if hh == 0 else pb2
                    o = hh * 512
                    p.act(hnL[:, o:o + 512], ps[:, :], AF.Exp, r=[psn], w=[hnL_b],
                          scale=-1.0)
                    p.act(hnL[:, o:o + 512], hnL[:, o:o + 512], AF.Ln, r=[hnL_b],
                          w=[hnL_b], bias=1.0)
                    p.act(hnL[:, o:o + 512], hnL[:, o:o + 512], AF.Exp, r=[hnL_b],
                          w=[hnL_b], scale=-1.0)
                for hh in range(2):
                    ps, psn = cvbank[jj][hh]
                    o = hh * 512
                    p.tt("v", u3[:, o:o + 512], hnL[:, o:o + 512], ps[:, :], ALU.mult,
                         r=[hnL_b, psn], w=[u3_b])
                    p.tt("v", u3[:, o:o + 512], u3[:, o:o + 512],
                         xn1[jj][0][:, o:o + 512], ALU.add,
                         r=[u3_b, xn1[jj][1]], w=[u3_b])
                    p.tt("v", u3[:, o:o + 512], u3[:, o:o + 512],
                         x1L[jj][0][:, o:o + 512], ALU.add,
                         r=[u3_b, x1L[jj][1]], w=[u3_b])
            for ji, jj in enumerate(jjs):
                j = ch * 4 + jj
                u3, u3_b = u3s[ji]
                d3l, d3l_b = d3s[jp * 2 + ji]
                p.tt("v", u3, u3, d3l, ALU.add, r=[u3_b, d3l_b], w=[u3_b])
                p.dma("y", out_t[j * 128:(j + 1) * 128, :], u3, "out", in_buf=u3_b)

    p.final_wait("y", ["out"])
    return nc


def kernel(**inputs):
    f32 = np.float32

    def c(a, dt=np.float32):
        return np.ascontiguousarray(np.asarray(a), dtype=dt)

    x = c(inputs["L0_attn_x"])
    wqkvT = c(inputs["L0_attn_w"].T)
    w_out = c(inputs["L0_attn_w_out"])
    w1T = c(inputs["L0_ffn_W1"].T)
    b1v = c(inputs["L0_ffn_b1"]).reshape(1, H0)
    w2T = c(inputs["L0_ffn_W2"].T)
    b2v = c(inputs["L0_ffn_b2"]).reshape(1, D)
    tdv = c(inputs["L1_attn_time_decay"]).reshape(1, D)
    tfv = c(inputs["L1_attn_time_first"]).reshape(1, D)

    def fmH(w, ngrp):
        din, dout = w.shape
        ndc = din // 128
        r = w.reshape(ndc, 128, ngrp, 128)
        return np.ascontiguousarray(r.transpose(2, 1, 0, 3).reshape(ngrp * 128, ndc * 128))

    wtkH = fmH(c(inputs["L1_attn_w_tk"]), 8)
    wtvH = fmH(c(inputs["L1_attn_w_tv"]), 8)
    wtrH = fmH(c(inputs["L1_attn_w_tr"]), 8)
    wckH = fmH(c(inputs["L1_attn_w_ck"]), 32)
    wcv = c(inputs["L1_attn_w_cv"])
    wcr = c(inputs["L1_attn_w_cr"])
    l1g = c(inputs["L1_attn_ln1_g"]).reshape(1, D)
    l1b = c(inputs["L1_attn_ln1_b"]).reshape(1, D)
    l2g = c(inputs["L1_attn_ln2_g"]).reshape(1, D)
    l2b = c(inputs["L1_attn_ln2_b"]).reshape(1, D)
    mxk = c(inputs["L1_attn_mix_k"]).reshape(1, D)
    mxv = c(inputs["L1_attn_mix_v"]).reshape(1, D)
    mxr = c(inputs["L1_attn_mix_r"]).reshape(1, D)
    cmk = c(inputs["L1_attn_cmix_k"]).reshape(1, D)
    cmr = c(inputs["L1_attn_cmix_r"]).reshape(1, D)
    wgH = fmH(np.ascontiguousarray(inputs["L1_ffn_w_gate"].T), 24).astype(np.int32)
    wuH = fmH(np.ascontiguousarray(inputs["L1_ffn_w_up"].T), 24).astype(np.int32)
    wdT = c(inputs["L1_ffn_w_down"].T, np.int32)

    nc = build_program(BUILD_UPTO)
    shared = dict(wqkvT=wqkvT, w_out=w_out, w1T=w1T, b1v=b1v, w2T=w2T, b2v=b2v,
                  tdv=tdv, tfv=tfv, wtkH=wtkH, wtvH=wtvH, wtrH=wtrH, wckH=wckH,
                  wcv=wcv, wcr=wcr, l1g=l1g, l1b=l1b, l2g=l2g, l2b=l2b,
                  mxk=mxk, mxv=mxv, mxr=mxr, cmk=cmk, cmr=cmr,
                  wgH=wgH, wuH=wuH, wdT=wdT)
    in_maps = [dict(shared, x=np.ascontiguousarray(x[b])) for b in range(x.shape[0])]
    res = run_bass_kernel_spmd(nc, in_maps, core_ids=list(range(x.shape[0])))
    return np.stack([res.results[b]["o"] for b in range(x.shape[0])], axis=0)


if __name__ == "__main__":
    import reference
    ins = reference.setup_inputs()
    ins = {k: np.asarray(v) for k, v in ins.items()}
    out = kernel(**ins)
    print(out.shape, out.dtype)


# version 14
# speedup vs baseline: 1.1395x; 1.1073x over previous
import numpy as np
import concourse.bass as bass
import concourse.mybir as mybir
from concourse.bass_utils import run_bass_kernel_spmd

F32 = mybir.dt.float32
BF16 = mybir.dt.bfloat16
I32 = mybir.dt.int32
AF = mybir.ActivationFunctionType
ALU = mybir.AluOpType
AX = mybir.AxisListType

MAGIC = 1.5 * 2.0 ** 23
T, D = 4096, 1024
NT = T // 128
NCORES = 8
H0, H1 = 4096, 3072
NHEAD, HL = 16, 64
LN_EPS = 1e-5
ASC = 0.125

BUILD_UPTO = "full"


def set_shape(t, ncores=8):
    global T, NT, NCORES
    T = t
    NT = t // 128
    NCORES = ncores


class Buf:
    __slots__ = ("name", "writer", "readers", "sem", "fills")

    def __init__(self, name):
        self.name = name
        self.writer = None
        self.readers = {}
        self.sem = None
        self.fills = 0


class Prog:
    def __init__(self, nc):
        self.nc = nc
        self.eng = {"t": nc.tensor, "v": nc.vector, "s": nc.scalar,
                    "g": nc.gpsimd, "y": nc.sync}
        self.psem = {e: nc.alloc_semaphore(f"ps_{e}") for e in "tvsg"}
        self.pcnt = {e: 0 for e in "tvsg"}
        self.obs = {e: {} for e in "tvsgy"}
        self.bufs = {}
        self.nsem = 0

    def buf(self, n):
        if isinstance(n, Buf):
            return n
        b = self.bufs.get(n)
        if b is None:
            b = Buf(n)
            self.bufs[n] = b
        return b

    def _wait(self, e, sem, val, clock=None):
        k = sem.num
        if self.obs[e].get(k, 0) >= val:
            return
        self.eng[e].wait_ge(sem, val)
        obs = self.obs[e]
        obs[k] = max(obs.get(k, 0), val)
        if clock:
            for k2, v2 in clock.items():
                if obs.get(k2, 0) < v2:
                    obs[k2] = v2

    def _gather(self, e, reads, writes):
        need = {}

        def add(evt):
            if evt is None:
                return
            sem, val = evt[0], evt[1]
            clk = evt[2] if len(evt) > 2 else None
            cur = need.get(sem.num)
            if cur is None or cur[1] < val:
                need[sem.num] = (sem, val, clk)

        for b in reads:
            add(b.writer)
        for b in writes:
            add(b.writer)
            for evt in b.readers.values():
                add(evt)
        for sem, val, clk in need.values():
            self._wait(e, sem, val, clk)

    def op(self, e, fn, r=(), w=()):
        reads = [self.buf(x) for x in r]
        writes = [self.buf(x) for x in w]
        self._gather(e, reads, writes)
        ins = fn(self.eng[e])
        self.pcnt[e] += 1
        ins.then_inc(self.psem[e], 1)
        evt = (self.psem[e], self.pcnt[e], dict(self.obs[e]))
        for b in writes:
            b.writer = evt
            b.readers = {}
        for b in reads:
            b.readers[("e", e)] = evt
        return ins

    def tt(self, e, out, a, b, alu, r=(), w=()):
        self.op(e, lambda E: E.tensor_tensor(out, a, b, alu), r, w)

    def ts(self, e, out, a, s1, s2, op0, op1=None, r=(), w=()):
        if s2 is None and op1 is None:
            self.op(e, lambda E: E.tensor_single_scalar(out, a, s1, op0), r, w)
        else:
            self.op(e, lambda E: E.tensor_scalar(out, a, s1, s2, op0, op1), r, w)

    def stt(self, e, out, a, s, b, op0, op1, r=(), w=()):
        self.op(e, lambda E: E.scalar_tensor_tensor(out, a, s, b, op0, op1), r, w)

    def cp(self, e, out, a, r=(), w=()):
        self.op(e, lambda E: E.tensor_copy(out, a), r, w)

    def recip(self, out, a, r=(), w=()):
        self.op("v", lambda E: E.reciprocal(out, a), r, w)

    def red(self, e, out, a, alu, r=(), w=(), absval=None):
        self.op(e, lambda E: E.tensor_reduce(out, a, AX.X, alu,
                apply_absolute_value=absval), r, w)

    def act(self, out, a, func, r=(), w=(), bias=0.0, scale=1.0, accum=None):
        if isinstance(bias, float) and bias != 0.0:
            bias = self.constc(bias)
            r = list(r) + ["cvals"]
        self.op("s", lambda E: E.activation(out, a, func, bias=bias, scale=scale,
                accum_out=accum), r, w)

    def mm(self, out, lhsT, rhs, start, stop, r=(), w=()):
        self.op("t", lambda E: E.matmul(out, lhsT, rhs, start=start, stop=stop), r, w)

    def mm_group(self, mms, r=(), w=()):
        reads = [self.buf(x) for x in r]
        writes = [self.buf(x) for x in w]
        self._gather("t", reads, writes)
        E = self.eng["t"]
        for (out, lhsT, rhs, start, stop) in mms:
            ins = E.matmul(out, lhsT, rhs, start=start, stop=stop)
        self.pcnt["t"] += 1
        ins.then_inc(self.psem["t"], 1)
        evt = (self.psem["t"], self.pcnt["t"], dict(self.obs["t"]))
        for b in writes:
            b.writer = evt
            b.readers = {}
        for b in reads:
            b.readers[("e", "t")] = evt
        return ins

    def stream_begin(self, w=()):
        self._stream_w = [self.buf(x) for x in w]
        self._gather("t", [], self._stream_w)
        self._stream_evt = None

    def stream_step(self, rbufs, mms):
        rb = [self.buf(x) for x in rbufs]
        self._gather("t", rb, [])
        E = self.eng["t"]
        ins = None
        for (out, lhsT, rhs, start, stop) in mms:
            ins = E.matmul(out, lhsT, rhs, start=start, stop=stop)
        self.pcnt["t"] += 1
        ins.then_inc(self.psem["t"], 1)
        evt = (self.psem["t"], self.pcnt["t"], dict(self.obs["t"]))
        for b in rb:
            b.readers[("e", "t")] = evt
        self._stream_evt = evt

    def stream_end(self):
        for b in self._stream_w:
            b.writer = self._stream_evt
            b.readers = {}

    def tr_group(self, trs, r=(), w=()):
        reads = [self.buf(x) for x in r]
        writes = [self.buf(x) for x in w]
        self._gather("t", reads, writes)
        E = self.eng["t"]
        for (out, a, idt) in trs:
            ins = E.transpose(out, a, idt)
        self.pcnt["t"] += 1
        ins.then_inc(self.psem["t"], 1)
        evt = (self.psem["t"], self.pcnt["t"], dict(self.obs["t"]))
        for b in writes:
            b.writer = evt
            b.readers = {}
        for b in reads:
            b.readers[("e", "t")] = evt
        return ins

    def trans(self, out, a, idt, r=(), w=()):
        self.op("t", lambda E: E.transpose(out, a, idt), r, w)

    def scan(self, e, out, d0, d1, init, r=(), w=()):
        self.op(e, lambda E: E.tensor_tensor_scan(out, d0, d1, init,
                ALU.mult, ALU.add), r, w)

    def mset(self, ap, val, w, r=()):
        self.op("g", lambda E: E.memset(ap, val), r, w)

    def afsel(self, ap, cmp, fill, pattern, chmul, r=(), w=()):
        self.op("g", lambda E: E.affine_select(out=ap, in_=ap, compare_op=cmp,
                fill=fill, base=0, pattern=pattern, channel_multiplier=chmul), r, w)

    def dma(self, e, out_ap, in_ap, out_buf, in_buf=None, slow=False):
        ob = self.buf(out_buf)
        rb = [self.buf(in_buf)] if in_buf is not None else []
        self._gather(e, rb, [ob])
        if ob.sem is None:
            self.nsem += 1
            ob.sem = self.nc.alloc_semaphore(f"dm{self.nsem}")
        ins = self.eng[e].dma_start(out=out_ap, in_=in_ap,
                                    allow_slow_non_contiguous=slow)
        ob.fills += 1
        ins.then_inc(ob.sem, 16)
        evt = (ob.sem, 16 * ob.fills, dict(self.obs[e]))
        ob.writer = evt
        ob.readers = {}
        for b in rb:
            b.readers[("d", ob.name, ob.fills)] = evt
        return ins

    def final_wait(self, e, names):
        for n in names:
            b = self.buf(n)
            if b.writer is not None:
                self._wait(e, b.writer[0], b.writer[1])


class Arena:
    def __init__(self, prog, name, nbytes):
        self.p = prog
        self.t = prog.nc.alloc_sbuf_tensor(name, [128, nbytes], mybir.dt.uint8)
        self.nbytes = nbytes
        self.live = []
        self.gen = 0

    def alloc(self, name, lo, nbytes, dtype):
        hi = lo + nbytes
        assert hi <= self.nbytes, (name, lo, nbytes)
        self.gen += 1
        b = Buf(f"{name}#{self.gen}")
        keep = []
        for (l2, h2, b2) in self.live:
            if l2 < hi and lo < h2:
                if b2.writer is not None:
                    b.readers[("w", b2.name)] = b2.writer
                for k, evt in b2.readers.items():
                    b.readers[(b2.name,) + (k if isinstance(k, tuple) else (k,))] = evt
                if lo <= l2 and h2 <= hi:
                    continue
            keep.append((l2, h2, b2))
        keep.append((lo, hi, b))
        self.live = keep
        self.p.bufs[b.name] = b
        es = mybir.dt.size(dtype)
        ap = self.t[:, lo:hi].bitcast(dtype)
        return ap, b


def build_program(upto="full"):
    nc = bass.Bass("TRN2", target_bir_lowering=False, debug=False)
    p = Prog(nc)

    def din(name, shape, dt=F32):
        return nc.dram_tensor(name, shape, dt, kind="ExternalInput").ap()

    x_in = din("x", [T, D])
    wqkvT = din("wqkvT", [D, 3 * D])
    w_out = din("w_out", [D, D])
    w1T = din("w1T", [D, H0])
    b1v = din("b1v", [1, H0])
    w2T = din("w2T", [H0, D])
    b2v = din("b2v", [1, D])
    tdv = din("tdv", [1, D]); tfv = din("tfv", [1, D])
    wtkH = din("wtkH", [8 * 128, 1024])
    wtvH = din("wtvH", [8 * 128, 1024])
    wtrH = din("wtrH", [8 * 128, 1024])
    wckH = din("wckH", [32 * 128, 1024])
    wcv = din("wcv", [H0, D])
    wcr = din("wcr", [D, D])
    l1g = din("l1g", [1, D]); l1b = din("l1b", [1, D])
    l2g = din("l2g", [1, D]); l2b = din("l2b", [1, D])
    mxk = din("mxk", [1, D]); mxv = din("mxv", [1, D]); mxr = din("mxr", [1, D])
    cmk = din("cmk", [1, D]); cmr = din("cmr", [1, D])
    wgH = din("wgH", [24 * 128, 1024], I32)
    wuH = din("wuH", [24 * 128, 1024], I32)
    wdT = din("wdT", [H1, D], I32)
    out_t = nc.dram_tensor("o", [T, D], F32, kind="ExternalOutput").ap()

    x1_h = nc.dram_tensor("x1_h", [T, D], F32).ap()
    w1q_h = nc.dram_tensor("w1q_h", [D, H0], BF16).ap()
    acc_h = nc.dram_tensor("acc_h", [32, 128, T], F32).ap()
    w2q_h = nc.dram_tensor("w2q_h", [H0, D], BF16).ap()
    cc_in = nc.dram_tensor("cc_in", [1, 16], F32)
    cc_out = nc.dram_tensor("cc_out", [1, 16], F32)
    wtk_b = nc.dram_tensor("wtk_b", [8 * 128, 1024], BF16).ap()
    wtv_b = nc.dram_tensor("wtv_b", [8 * 128, 1024], BF16).ap()
    wtr_b = nc.dram_tensor("wtr_b", [8 * 128, 1024], BF16).ap()
    wck_b = nc.dram_tensor("wck_b", [32 * 128, 1024], BF16).ap()
    wcv_b = nc.dram_tensor("wcv_b", [H0, D], BF16).ap()
    wcr_b = nc.dram_tensor("wcr_b", [D, D], BF16).ap()
    wg_b = nc.dram_tensor("wg_b", [24 * 128, 1024], BF16).ap()
    wu_b = nc.dram_tensor("wu_b", [24 * 128, 1024], BF16).ap()
    wd_b = nc.dram_tensor("wd_b", [H1, D], BF16).ap()

    ident = nc.alloc_sbuf_tensor("ident", [128, 128], BF16)
    identf = nc.alloc_sbuf_tensor("identf", [128, 128], F32)
    ones_r = nc.alloc_sbuf_tensor("ones_r", [1, 128], F32)
    ones_cb = nc.alloc_sbuf_tensor("ones_cb", [128, 1], BF16)
    ones_rb1 = nc.alloc_sbuf_tensor("ones_rb1", [1, 128], BF16)
    sc0 = nc.alloc_sbuf_tensor("sc0", [128, NT], F32)
    sb0 = nc.alloc_sbuf_tensor("sb0", [128, NT], F32)
    qsc = nc.alloc_sbuf_tensor("qsc", [128, NT], F32)
    qsb = nc.alloc_sbuf_tensor("qsb", [128, NT], F32)
    qp = nc.alloc_sbuf_tensor("qp", [128, 12], F32)
    (QP_XHI, QP_XLO, QP_SMM1, QP_INVSH, QP_HHI, QP_HLO, QP_SMM2,
     QP_IW1, QP_IW2, QP_INVSX) = range(10)

    def qpc(i):
        return qp[:, i:i + 1]

    rowm = nc.alloc_sbuf_tensor("rowm", [128, 16], F32)
    row16 = nc.alloc_sbuf_tensor("row16", [1, 16], F32)
    roww = nc.alloc_sbuf_tensor("roww", [1, 16], F32)
    b1c = nc.alloc_sbuf_tensor("b1c", [128, 32], F32)
    b2b = nc.alloc_sbuf_tensor("b2b", [128, 1024], F32)
    chp = nc.alloc_sbuf_tensor("chp", [128, 8 * 16], F32)
    (NP_W, NP_EW, NP_U, NP_G1, NP_B1, NP_G2, NP_B2, NP_MK, NP_MV, NP_MR,
     NP_CK, NP_CR, NP_HB, NP_H2B, NP_EU) = range(15)

    def chcol(i, g):
        return chp[:, i * 8 + g:i * 8 + g + 1]

    wkvs = nc.alloc_sbuf_tensor("wkvs", [128, 32], F32)

    def ppe(g):
        return wkvs[:, g:g + 1]

    def aae(g):
        return wkvs[:, 8 + g:8 + g + 1]

    lnrow = nc.alloc_sbuf_tensor("lnrow", [128, 16], F32)

    PS0 = nc.alloc_psum_tensor("PS0", [128, 512], F32)
    PS1 = nc.alloc_psum_tensor("PS1", [128, 512], F32)
    PS2 = nc.alloc_psum_tensor("PS2", [128, 512], F32)
    PS3 = nc.alloc_psum_tensor("PS3", [128, 512], F32)
    PSY = nc.alloc_psum_tensor("PSY", [128, 512], F32)
    pS = nc.alloc_psum_tensor("pS", [64, 512], F32)
    ptrb = nc.alloc_psum_tensor("ptrb", [128, 1024], BF16)
    pb2 = nc.alloc_psum_tensor("pb2", [128, 512], F32)

    AR = Arena(p, "arena", 200 * 1024)

    cvals = nc.alloc_sbuf_tensor("cvals", [128, 8], F32)
    _creg = {}

    def constc(v):
        if v not in _creg:
            i = len(_creg)
            assert i < 8
            p.mset(cvals[:, i:i + 1], v, ["cvals"], r=["cvals"])
            _creg[v] = cvals[:, i:i + 1]
        return _creg[v]

    p.constc = constc

    p.mset(ident[:, :], 1.0, ["ident"])
    p.afsel(ident[:, :], ALU.is_equal, 0.0, [[1, 128]], -1, ["ident"], ["ident"])
    p.mset(identf[:, :], 1.0, ["identf"])
    p.afsel(identf[:, :], ALU.is_equal, 0.0, [[1, 128]], -1, ["identf"], ["identf"])
    p.mset(ones_r[:, :], 1.0, ["ones_r"])
    p.mset(ones_cb[:, :], 1.0, ["ones_cb"])
    p.mset(ones_rb1[:, :], 1.0, ["ones_rb1"])
    p.mset(wkvs[:, :], 0.0, ["wkvs"])
    for g in range(8):
        p.mset(ppe(g), -1e38, ["wkvs"], r=["wkvs"])

    for src, idx in [(tdv, NP_W), (tfv, NP_U), (l1g, NP_G1), (l1b, NP_B1),
                     (l2g, NP_G2), (l2b, NP_B2), (mxk, NP_MK), (mxv, NP_MV),
                     (mxr, NP_MR), (cmk, NP_CK), (cmr, NP_CR)]:
        p.dma("y", chp[:, idx * 8:(idx + 1) * 8],
              src.rearrange("one (g np) -> (one np) g", np=128), "chp", slow=True)
    p.act(chp[:, NP_EW * 8:(NP_EW + 1) * 8], chp[:, NP_W * 8:(NP_W + 1) * 8],
          AF.Exp, r=["chp"], w=["chp"])
    p.act(chp[:, NP_EU * 8:(NP_EU + 1) * 8], chp[:, NP_U * 8:(NP_U + 1) * 8],
          AF.Exp, r=["chp"], w=["chp"])
    p.mset(chp[:, NP_HB * 8:(NP_H2B + 1) * 8], 0.0, ["chp"], r=["chp"])
    p.dma("y", b1c[:, :], b1v.rearrange("one (c np) -> (one np) c", np=128), "b1c",
          slow=True)

    b2row, b2row_b = AR.alloc("b2row", 175 * 1024, 4096, F32)
    p.dma("y", b2row[0:1, :], b2v[0:1, :], b2row_b)
    for hh in range(2):
        p.op("t", lambda E, hh=hh: E.matmul(pb2[:, :], ones_r[0:1, :],
             b2row[0:1, hh * 512:(hh + 1) * 512], start=True, stop=True),
             r=[b2row_b, "ones_r"], w=["pb2"])
        p.cp("v", b2b[:, hh * 512:(hh + 1) * 512], pb2[:, :], r=["pb2"], w=["b2b"])

    def bcast_col(src_ap, dst_ap, rbufs, wbuf):
        p.op("t", lambda E: E.matmul(pb2[:, 0:1], ones_r[0:1, :], src_ap,
             start=True, stop=True), r=list(rbufs) + ["ones_r"], w=["pb2"])
        p.cp("v", dst_ap, pb2[:, 0:1], r=["pb2"], w=[wbuf])

    def part_max(src_ap, src_buf, dst_row_ap, dst_buf, tmpcol):
        p.trans(pb2[0:1, 0:128], src_ap, identf[:, :], r=[src_buf, "identf"], w=["pb2"])
        p.red("v", rowm[0:1, tmpcol:tmpcol + 1], pb2[0:1, 0:128], ALU.max,
              r=["pb2"], w=["rowm"])
        p.dma("g", dst_row_ap, rowm[0:1, tmpcol:tmpcol + 1], dst_buf, in_buf="rowm")

    xr = [AR.alloc(f"xrA{i}", i * 4096, 4096, F32) for i in range(4)]
    scrA, scrA_b = AR.alloc("scrA", 175 * 1024, 4096, F32)
    mnmx, mnmx_b = AR.alloc("mnmxA", 179 * 1024, 64, F32)
    p.mset(mnmx[:, 0:2], -1e38, [mnmx_b])

    wst = [AR.alloc(f"wstg{i}", 180 * 1024 + i * 4 * 1024, 4 * 1024, F32)
           for i in range(2)]
    wqo = [AR.alloc(f"wqo{i}", 188 * 1024 + i * 2 * 1024, 2 * 1024, BF16)
           for i in range(2)]
    wmx, wmx_b = AR.alloc("wmx", 192 * 1024, 64, F32)

    def quant_weight(wsrc, nchunks, X, hbm_dst, iw_slot, tag):
        iw_col = QP_IW1 if tag == "w1" else QP_IW2
        pieces = []
        nh = max(1, X // 1024)
        Xp = X // nh
        for ci in range(nchunks):
            for hswp in range(nh):
                pieces.append((ci, hswp * Xp, Xp))
        p.mset(wmx[:, 0:1], 0.0, [wmx_b])
        for pi, (ci, c0, Xp_) in enumerate(pieces):
            a, b = wst[pi % 2]
            p.dma("y", a[:, 0:Xp_], wsrc[ci * 128:(ci + 1) * 128, c0:c0 + Xp_], b)
            p.red("v", wmx[:, 1:2], a[:, 0:Xp_], ALU.max, r=[b], w=[wmx_b], absval=True)
            p.tt("v", wmx[:, 0:1], wmx[:, 0:1], wmx[:, 1:2], ALU.max, r=[wmx_b], w=[wmx_b])
        part_max(wmx[:, 0:1], wmx_b, roww[0:1, iw_slot:iw_slot + 1], f"sw{tag}", 12)
        p.ts("v", roww[0:1, 4 + iw_slot:5 + iw_slot], roww[0:1, iw_slot:iw_slot + 1],
             1.0 / 127.0, 1e-8, ALU.mult, ALU.max, r=[f"sw{tag}", "roww"], w=["roww"])
        p.recip(roww[0:1, 6:7], roww[0:1, 4 + iw_slot:5 + iw_slot], r=["roww"], w=["roww"])
        bcast_col(roww[0:1, 6:7], qpc(iw_col), ["roww"], "qp")
        for pi, (ci, c0, Xp_) in enumerate(pieces):
            a, b = wst[pi % 2]
            qa, qb = wqo[pi % 2]
            p.dma("y", a[:, 0:Xp_], wsrc[ci * 128:(ci + 1) * 128, c0:c0 + Xp_], b)
            p.ts("v", a[:, 0:Xp_], a[:, 0:Xp_], qpc(iw_col), MAGIC, ALU.mult, ALU.add,
                 r=[b, "qp"], w=[b])
            p.ts("v", a[:, 0:Xp_], a[:, 0:Xp_], MAGIC, 127.0, ALU.subtract, ALU.min,
                 r=[b], w=[b])
            p.ts("v", qa[:, 0:Xp_], a[:, 0:Xp_], -127.0, None, ALU.max, r=[b], w=[qb])
            p.dma("g", hbm_dst[ci * 128:(ci + 1) * 128, c0:c0 + Xp_], qa[:, 0:Xp_],
                  f"{tag}h", in_buf=qb)


    wqv, wqv_b = AR.alloc("wqv", 16 * 1024, 48 * 1024, BF16)
    wou, wou_b = AR.alloc("wou", 64 * 1024, 16 * 1024, BF16)
    tri4, tri4_b = AR.alloc("tri4", 80 * 1024, 1024, BF16)
    S_f, S_f_b = AR.alloc("S_f", 81 * 1024, 4096, F32)
    S_b_, S_b_b = AR.alloc("S_b", 85 * 1024, 2048, BF16)
    carry_f, carry_f_b = AR.alloc("carry_f", 87 * 1024, 4096, F32)
    carry_bb, carry_b_b = AR.alloc("carry_bb", 91 * 1024, 2048, BF16)
    def _cset(base, tag):
        ns = {}
        off = [("xnb", 0, 2048, BF16), ("xnf", 2048, 2048, BF16),
               ("eqs", 4096, 4096, F32), ("keb", 8192, 2048, BF16),
               ("lnk", 10240, 4096, F32), ("scC", 14336, 4096, F32),
               ("qsbf", 18432, 2048, BF16), ("qsf", 20480, 4096, BF16),
               ("kef", 24576, 4096, BF16), ("vtm", 28672, 2048, BF16),
               ("atb", 30720, 1024, BF16), ("atb2", 31744, 1024, BF16),
               ("ytm", 32768, 2048, BF16), ("yf", 34816, 2048, BF16),
               ("s16", 36864, 128, F32), ("x1p", 37376, 4096, F32)]
        for nm, o, nb, dt in off:
            ns[nm] = AR.alloc(f"{nm}{tag}", base + o, nb, dt)
        return ns

    csets = [_cset(93 * 1024, "0"), _cset(134 * 1024, "1")]

    p.mset(tri4[:, :], 1.0, [tri4_b])
    for i in range(4):
        p.afsel(tri4[:, i * 128:(i + 1) * 128], ALU.is_ge, 0.0, [[1, 128]], -1,
                [tri4_b], [tri4_b])
    p.mset(S_f[:, :], 0.0, [S_f_b])
    p.mset(S_b_[:, :], 0.0, [S_b_b])
    p.mset(carry_f[:, :], 0.0, [carry_f_b])
    p.mset(carry_bb[:, :], 0.0, [carry_b_b])

    for dc in range(8):
        p.dma("g", wqv[:, dc * 3072:(dc + 1) * 3072],
              wqkvT[dc * 128:(dc + 1) * 128, :], wqv_b)
        p.dma("g", wou[:, dc * 1024:(dc + 1) * 1024],
              w_out[dc * 128:(dc + 1) * 128, :], wou_b)
    for dst, srcw, nm in [(wtk_b, wtkH, "wtkb"), (wtv_b, wtvH, "wtvb"),
                          (wtr_b, wtrH, "wtrb"), (wck_b, wckH, "wckb"),
                          (wcv_b, wcv, "wcvb"), (wcr_b, wcr, "wcrb"),
                          (wg_b, wgH, "wgb"), (wu_b, wuH, "wub"),
                          (wd_b, wdT, "wdb")]:
        p.dma("g", dst, srcw, nm)

    def wq_ap(dc, lo, n):
        return wqv[:, dc * 3072 + lo:dc * 3072 + lo + n]

    for j in range(NT):
        cs = csets[j % 2]
        (xnb, xnb_b) = cs["xnb"]; (xnf, xnf_b) = cs["xnf"]
        (eqs, eqs_b) = cs["eqs"]; (keb, keb_b) = cs["keb"]
        (lnk, lnk_b) = cs["lnk"]; (scC, scC_b) = cs["scC"]
        (qsbf, qsbf_b) = cs["qsbf"]; (qsf, qsf_b) = cs["qsf"]
        (kef, kef_b) = cs["kef"]; (vtm, vtm_b) = cs["vtm"]
        (atb, atb_b) = cs["atb"]; (atb2, atb2_b) = cs["atb2"]
        (ytm, ytm_b) = cs["ytm"]; (yf, yf_b) = cs["yf"]
        (s16, s16_b) = cs["s16"]; (x1p, x1p_b) = cs["x1p"]
        xa, xb = xr[j % 4]
        p.dma("y", xa, x_in[j * 128:(j + 1) * 128, :], xb)
        p.red("v", rowm[:, 0:1], xa, ALU.add, r=[xb], w=["rowm"])
        p.act(scrA, xa, AF.Square, r=[xb], w=[scrA_b, "rowm"], accum=rowm[:, 1:2])
        p.red("v", rowm[:, 2:3], xa, ALU.max, r=[xb], w=["rowm"])
        p.red("v", rowm[:, 3:4], xa, ALU.min, r=[xb], w=["rowm"])
        p.ts("v", rowm[:, 4:5], rowm[:, 0:1], 1.0 / D, None, ALU.mult, r=["rowm"], w=["rowm"])
        p.ts("v", rowm[:, 5:6], rowm[:, 1:2], 1.0 / D, None, ALU.mult, r=["rowm"], w=["rowm"])
        p.tt("v", rowm[:, 6:7], rowm[:, 4:5], rowm[:, 4:5], ALU.mult, r=["rowm"], w=["rowm"])
        p.tt("v", rowm[:, 6:7], rowm[:, 5:6], rowm[:, 6:7], ALU.subtract, r=["rowm"], w=["rowm"])
        p.act(rowm[:, 7:8], rowm[:, 6:7], AF.Ln, r=["rowm"], w=["rowm"], bias=LN_EPS)
        p.act(sc0[:, j:j + 1], rowm[:, 7:8], AF.Exp, r=["rowm"], w=["sc0"], scale=-0.5)
        p.tt("v", sb0[:, j:j + 1], rowm[:, 4:5], sc0[:, j:j + 1], ALU.mult,
             r=["rowm", "sc0"], w=["sb0"])
        p.ts("v", sb0[:, j:j + 1], sb0[:, j:j + 1], -1.0, None, ALU.mult, r=["sb0"], w=["sb0"])
        p.tt("v", rowm[:, 8:9], rowm[:, 3:4], rowm[:, 4:5], ALU.subtract, r=["rowm"], w=["rowm"])
        p.tt("v", rowm[:, 8:9], rowm[:, 8:9], sc0[:, j:j + 1], ALU.mult,
             r=["rowm", "sc0"], w=["rowm"])
        p.ts("v", rowm[:, 8:9], rowm[:, 8:9], -1.0, None, ALU.mult, r=["rowm"], w=["rowm"])
        p.tt("v", mnmx[:, 0:1], mnmx[:, 0:1], rowm[:, 8:9], ALU.max,
             r=["rowm", mnmx_b], w=[mnmx_b])
        p.tt("v", rowm[:, 9:10], rowm[:, 2:3], rowm[:, 4:5], ALU.subtract, r=["rowm"], w=["rowm"])
        p.tt("v", rowm[:, 9:10], rowm[:, 9:10], sc0[:, j:j + 1], ALU.mult,
             r=["rowm", "sc0"], w=["rowm"])
        p.tt("v", mnmx[:, 1:2], mnmx[:, 1:2], rowm[:, 9:10], ALU.max,
             r=["rowm", mnmx_b], w=[mnmx_b])

        p.act(xnb, xa, AF.Identity, r=[xb, "sc0", "sb0"], w=[xnb_b],
              bias=sb0[:, j:j + 1], scale=sc0[:, j:j + 1])
        p.tr_group([(ptrb[:, dc * 128:(dc + 1) * 128],
                     xnb[:, dc * 128:(dc + 1) * 128], ident[:, :])
                    for dc in range(8)], r=[xnb_b, "ident"], w=["ptrb"])
        p.cp("v", xnf, ptrb[:, :], r=["ptrb"], w=[xnf_b])
        pcs_banks = [(PS0, "PS0"), (PS1, "PS1"), (PS2, "PS2"), (PS3, "PS3"),
                     (PS0, "PS0"), (PS1, "PS1")]
        for pc in range(6):
            ps, psn = pcs_banks[pc]
            p.mm_group([(ps[:, :], xnf[:, dc * 128:(dc + 1) * 128],
                         wq_ap(dc, pc * 512, 512), dc == 0, dc == 7)
                        for dc in range(8)], r=[xnf_b, wqv_b], w=[psn])
            if pc == 0:
                p.act(eqs[:, 0:512], ps[:, :], AF.Exp, r=[psn], w=[eqs_b], scale=ASC)
            elif pc == 1:
                p.act(eqs[:, 512:1024], ps[:, :], AF.Exp, r=[psn], w=[eqs_b], scale=ASC)
            elif pc in (2, 3):
                o = (pc - 2) * 512
                p.act(scC[:, o:o + 512], ps[:, :], AF.Exp, r=[psn], w=[scC_b], scale=ASC)
                p.ts("v", keb[:, o:o + 512], scC[:, o:o + 512], 1e-6, None, ALU.add,
                     r=[scC_b], w=[keb_b])
            else:
                o = (pc - 4) * 512
                p.cp("v", vtm[:, o:o + 512], ps[:, :], r=[psn], w=[vtm_b])
        for hh, (ps, psn) in enumerate([(PS2, "PS2"), (PS3, "PS3")]):
            mms = [(ps[:, :], tri4[:, 0:128], keb[:, hh * 512:(hh + 1) * 512],
                    True, j == 0)]
            if j > 0:
                mms.append((ps[:, :], ones_rb1[0:1, :],
                            carry_bb[0:1, hh * 512:(hh + 1) * 512], False, True))
            p.mm_group(mms, r=[tri4_b, keb_b, "ones_rb1", carry_b_b], w=[psn])
            p.act(lnk[:, hh * 512:(hh + 1) * 512], ps[:, :], AF.Ln, r=[psn], w=[lnk_b])
        if j < NT - 1:
            for hh in range(2):
                p.op("t", lambda E, hh=hh: E.matmul(pb2[0:1, :], ones_cb[:, 0:1],
                     keb[:, hh * 512:(hh + 1) * 512], start=True, stop=True),
                     r=["ones_cb", keb_b], w=["pb2"])
                p.tt("v", carry_f[0:1, hh * 512:(hh + 1) * 512],
                     carry_f[0:1, hh * 512:(hh + 1) * 512], pb2[0:1, :], ALU.add,
                     r=["pb2", carry_f_b], w=[carry_f_b])
            p.cp("v", carry_bb[0:1, :], carry_f[0:1, :], r=[carry_f_b], w=[carry_b_b])
        p.red("v", s16[:, 0:16], eqs[:, :].rearrange("np (h l) -> np h l", l=64),
              ALU.add, r=[eqs_b], w=[s16_b])
        p.act(s16[:, 16:32], s16[:, 0:16], AF.Ln, r=[s16_b], w=[s16_b])
        p.tt("v", lnk[:, :].rearrange("np (h l) -> np h l", l=64),
             lnk[:, :].rearrange("np (h l) -> np h l", l=64),
             s16[:, 16:32].unsqueeze(2).broadcast_to([128, 16, 64]),
             ALU.add, r=[lnk_b, s16_b], w=[lnk_b])
        p.act(scC, lnk[:, :], AF.Exp, r=[lnk_b], w=[scC_b], scale=-1.0)
        p.tt("v", qsbf, eqs, scC, ALU.mult, r=[eqs_b, scC_b], w=[qsbf_b])
        for rnd in range(2):
            p.tr_group([(ptrb[0:64, hi * 128:(hi + 1) * 128],
                         qsbf[:, (rnd * 8 + hi) * 64:(rnd * 8 + hi + 1) * 64],
                         ident[:, :]) for hi in range(8)],
                       r=[qsbf_b, "ident"], w=["ptrb"])
            p.cp("v", qsf[0:64, rnd * 1024:(rnd + 1) * 1024], ptrb[0:64, :],
                 r=["ptrb"], w=[qsf_b])
            p.tr_group([(ptrb[0:64, hi * 128:(hi + 1) * 128],
                         keb[:, (rnd * 8 + hi) * 64:(rnd * 8 + hi + 1) * 64],
                         ident[:, :]) for hi in range(8)],
                       r=[keb_b, "ident"], w=["ptrb"])
            p.cp("v", kef[0:64, rnd * 1024:(rnd + 1) * 1024], ptrb[0:64, :],
                 r=["ptrb"], w=[kef_b])

        def hslice(buf_ap, h):
            return buf_ap[0:64, h * 128:(h + 1) * 128]

        for quad in range(4):
            ps, psn = (PS2, "PS2") if quad % 2 == 0 else (PS3, "PS3")
            ab, abn = (atb, atb_b) if quad % 2 == 0 else (atb2, atb2_b)
            p.mm_group([(ps[:, hi * 128:(hi + 1) * 128], hslice(kef, quad * 4 + hi),
                         hslice(qsf, quad * 4 + hi), True, True)
                        for hi in range(4)], r=[kef_b, qsf_b], w=[psn])
            p.tt("v", ab[:, :], ps[:, :], tri4[:, :], ALU.mult, r=[psn, tri4_b], w=[abn])
        for half in range(2):
            mms = []
            for qd in (half * 2, half * 2 + 1):
                ab, abn = (atb, atb_b) if qd % 2 == 0 else (atb2, atb2_b)
                for hi in range(4):
                    h = qd * 4 + hi
                    mms.append((PSY[:, (h % 8) * 64:(h % 8) * 64 + 64],
                                ab[:, hi * 128:(hi + 1) * 128],
                                vtm[:, h * 64:(h + 1) * 64], True, j == 0))
                    if j > 0:
                        mms.append((PSY[:, (h % 8) * 64:(h % 8) * 64 + 64],
                                    hslice(qsf, h),
                                    S_b_[0:64, h * 64:(h + 1) * 64], False, True))
            p.mm_group(mms, r=[atb_b, atb2_b, vtm_b, qsf_b, S_b_b], w=["PSY"])
            p.cp("v", ytm[:, half * 512:(half + 1) * 512], PSY[:, :],
                 r=["PSY"], w=[ytm_b])
        for half in range(2):
            p.mm_group([(pS[0:64, hi * 64:(hi + 1) * 64],
                         keb[:, (half * 8 + hi) * 64:(half * 8 + hi + 1) * 64],
                         vtm[:, (half * 8 + hi) * 64:(half * 8 + hi + 1) * 64],
                         True, True) for hi in range(8)],
                       r=[keb_b, vtm_b], w=["pS"])
            p.tt("v", S_f[0:64, half * 512:(half + 1) * 512],
                 S_f[0:64, half * 512:(half + 1) * 512], pS[0:64, :], ALU.add,
                 r=["pS", S_f_b], w=[S_f_b])
        p.cp("v", S_b_[0:64, :], S_f[0:64, :], r=[S_f_b], w=[S_b_b])
        p.tr_group([(ptrb[:, dc * 128:(dc + 1) * 128],
                     ytm[:, dc * 128:(dc + 1) * 128], ident[:, :])
                    for dc in range(8)], r=[ytm_b, "ident"], w=["ptrb"])
        p.cp("v", yf, ptrb[:, :], r=["ptrb"], w=[yf_b])
        for hh, (ps, psn) in enumerate([(PS0, "PS0"), (PS1, "PS1")]):
            p.mm_group([(ps[:, :], yf[:, dc * 128:(dc + 1) * 128],
                         wou[:, dc * 1024 + hh * 512:dc * 1024 + hh * 512 + 512],
                         dc == 0, dc == 7) for dc in range(8)],
                       r=[yf_b, wou_b], w=[psn])
            p.tt("v", x1p[:, hh * 512:(hh + 1) * 512], xa[:, hh * 512:(hh + 1) * 512],
                 ps[:, :], ALU.add, r=[xb, psn], w=[x1p_b])
        p.dma("y", (out_t if upto == "attn" else x1_h)[j * 128:(j + 1) * 128, :],
              x1p, "x1hC", in_buf=x1p_b)
        if j == min(8, NT - 2):
            quant_weight(w1T, 8, H0, w1q_h, 0, "w1")
        if j == min(18, NT - 1):
            quant_weight(w2T, 32, D, w2q_h, 1, "w2")

    part_max(mnmx[:, 0:1], mnmx_b, row16[0:1, 0:1], "r16a", 10)
    part_max(mnmx[:, 1:2], mnmx_b, row16[0:1, 1:2], "r16b", 11)
    p.dma("g", cc_in.ap()[0:1, 0:2], row16[0:1, 0:2], "ccinA", in_buf="r16b")
    p.buf("ccinA").readers[("xa",)] = p.buf("r16a").writer
    p.op("g", lambda E: E.collective_compute("AllReduce", ALU.max,
         replica_groups=[list(range(NCORES))], ins=[cc_in.ap()[0:1, 0:2]],
         outs=[cc_out.ap()[0:1, 0:2]]), r=["ccinA"], w=["ccoutA"])
    p.dma("g", row16[0:1, 2:4], cc_out.ap()[0:1, 0:2], "ccresA", in_buf="ccoutA")

    def derive_act_qparams(nmin_ap, gmax_ap, res_buf, sw_ap, hi_c, lo_c, smm_c,
                           inv_c):
        r, rb = row16, "row16"
        p.tt("v", r[0:1, 4:5], gmax_ap, nmin_ap, ALU.add, r=[res_buf, rb], w=[rb])
        p.ts("v", r[0:1, 4:5], r[0:1, 4:5], 1.0 / 255.0, 1e-8, ALU.mult, ALU.max,
             r=[rb], w=[rb])
        p.recip(r[0:1, 5:6], r[0:1, 4:5], r=[rb], w=[rb])
        p.tt("v", r[0:1, 6:7], nmin_ap, r[0:1, 5:6], ALU.mult, r=[rb], w=[rb])
        p.ts("v", r[0:1, 6:7], r[0:1, 6:7], MAGIC, MAGIC, ALU.add, ALU.subtract,
             r=[rb], w=[rb])
        p.ts("v", r[0:1, 6:7], r[0:1, 6:7], 128.0, 127.0, ALU.subtract, ALU.min,
             r=[rb], w=[rb])
        p.ts("v", r[0:1, 6:7], r[0:1, 6:7], -128.0, None, ALU.max, r=[rb], w=[rb])
        p.ts("v", r[0:1, 7:8], r[0:1, 6:7], -1.0, 127.0, ALU.mult, ALU.add, r=[rb], w=[rb])
        p.ts("v", r[0:1, 8:9], r[0:1, 6:7], -1.0, -128.0, ALU.mult, ALU.add, r=[rb], w=[rb])
        p.tt("v", r[0:1, 9:10], r[0:1, 4:5], sw_ap, ALU.mult, r=[rb, "roww"], w=[rb])
        bcast_col(r[0:1, 7:8], qpc(hi_c), [rb], "qp")
        bcast_col(r[0:1, 8:9], qpc(lo_c), [rb], "qp")
        bcast_col(r[0:1, 9:10], qpc(smm_c), [rb], "qp")
        bcast_col(r[0:1, 5:6], qpc(inv_c), [rb], "qp")

    derive_act_qparams(row16[0:1, 2:3], row16[0:1, 3:4], "ccresA",
                       roww[0:1, 4:5], QP_XHI, QP_XLO, QP_SMM1, QP_INVSX)
    p.ts("v", qsc[:, :], sc0[:, :], qpc(QP_INVSX), None, ALU.mult,
         r=["sc0", "qp"], w=["qsc"])
    p.ts("v", qsb[:, :], sb0[:, :], qpc(QP_INVSX), None, ALU.mult,
         r=["sb0", "qp"], w=["qsb"])

    def emit_xq_tile(j, jj, xa, xb, scr, scr_b, xqtm, xqtm_b, xqf3, xqf_b):
        p.act(scr, xa, AF.Identity, r=[xb, "qsc", "qsb"], w=[scr_b],
              bias=qsb[:, j:j + 1], scale=qsc[:, j:j + 1])
        p.ts("v", scr, scr, MAGIC, MAGIC, ALU.add, ALU.subtract, r=[scr_b], w=[scr_b])
        p.ts("v", scr, scr, qpc(QP_XHI), None, ALU.min, r=[scr_b, "qp"], w=[scr_b])
        p.ts("v", xqtm, scr, qpc(QP_XLO), None, ALU.max, r=[scr_b, "qp"], w=[xqtm_b])
        p.tr_group([(ptrb[:, dc * 128:(dc + 1) * 128],
                     xqtm[:, dc * 128:(dc + 1) * 128], ident[:, :])
                    for dc in range(8)], r=[xqtm_b, "ident"], w=["ptrb"])
        p.cp("v", xqf3[:, :, jj * 128:(jj + 1) * 128],
             ptrb[:, :].rearrange("np (dc tc) -> np dc tc", tc=128),
             r=["ptrb"], w=[xqf_b])

    w1r, w1r_b = AR.alloc("w1r", 40 * 1024, 64 * 1024, BF16)
    xqfBs = [AR.alloc(f"xqfB{i}", 104 * 1024 + i * 8 * 1024, 8 * 1024, BF16)
             for i in range(2)]
    scrBs = [AR.alloc(f"scrB{i}", 120 * 1024 + i * 4096, 4096, F32) for i in range(2)]
    xqtmBs = [AR.alloc(f"xqtmB{i}", 128 * 1024 + i * 2048, 2048, BF16) for i in range(2)]
    accst = [AR.alloc(f"accst{i}", 132 * 1024 + 64 + i * 2048, 2048, F32)
             for i in range(2)]
    hmn, hmn_b = AR.alloc("hmn", 132 * 1024, 64, F32)
    rowB, rowB_b = AR.alloc("rowB", 166 * 1024, 1024, F32)
    p.mset(hmn[:, 0:2], -1e38, [hmn_b])
    for dc in range(8):
        p.dma("y", w1r[:, dc * 4096:(dc + 1) * 4096],
              w1q_h[dc * 128:(dc + 1) * 128, :], w1r_b, in_buf="w1h")

    for ch in range(T // 512):
        xqfB, xqfB_b = xqfBs[ch % 2]
        xqfB3 = xqfB.rearrange("np (dc tc) -> np dc tc", tc=512)
        for jj in range(4):
            j = ch * 4 + jj
            xa, xb = xr[j % 4]
            p.dma("y", xa, x_in[j * 128:(j + 1) * 128, :], xb)
            scrB, scrB_b = scrBs[jj % 2]
            xqtmB, xqtmB_b = xqtmBs[jj % 2]
            emit_xq_tile(j, jj, xa, xb, scrB, scrB_b, xqtmB, xqtmB_b, xqfB3, xqfB_b)
        for hg in range(32):
            ps = PS0 if hg % 2 == 0 else PS1
            psn = "PS0" if hg % 2 == 0 else "PS1"
            p.mm_group([(ps[:, :],
                         w1r[:, dc * 4096 + hg * 128:dc * 4096 + (hg + 1) * 128],
                         xqfB[:, dc * 512:(dc + 1) * 512], dc == 0, dc == 7)
                        for dc in range(8)], r=[w1r_b, xqfB_b], w=[psn])
            ac, ac_b = accst[hg % 2]
            p.cp("v", ac, ps[:, :], r=[psn], w=[ac_b])
            p.dma("y", acc_h[hg, :, ch * 512:(ch + 1) * 512], ac, "acch", in_buf=ac_b)
            p.red("v", rowB[:, hg:hg + 1], ac, ALU.max, r=[ac_b], w=[rowB_b])
            p.red("v", rowB[:, 32 + hg:33 + hg], ac, ALU.min, r=[ac_b], w=[rowB_b])
        p.ts("v", rowB[:, 64:128], rowB[:, 0:64], qpc(QP_SMM1), None,
             ALU.mult, r=[rowB_b, "qp"], w=[rowB_b])
        p.tt("v", rowB[:, 64:96], rowB[:, 64:96], b1c[:, 0:32], ALU.add,
             r=[rowB_b, "b1c"], w=[rowB_b])
        p.tt("v", rowB[:, 96:128], rowB[:, 96:128], b1c[:, 0:32], ALU.add,
             r=[rowB_b, "b1c"], w=[rowB_b])
        p.act(rowB[:, 128:192], rowB[:, 64:128], AF.Gelu,
              r=[rowB_b], w=[rowB_b])
        p.ts("v", rowB[:, 192:224], rowB[:, 64:96],
             -0.751791534894327, None, ALU.is_ge, r=[rowB_b], w=[rowB_b])
        p.ts("v", rowB[:, 224:256], rowB[:, 96:128],
             -0.751791534894327, None, ALU.is_le, r=[rowB_b], w=[rowB_b])
        p.tt("v", rowB[:, 192:224], rowB[:, 192:224], rowB[:, 224:256],
             ALU.mult, r=[rowB_b], w=[rowB_b])
        p.ts("v", rowB[:, 224:256], rowB[:, 192:224],
             -0.16997120747990369, None, ALU.mult, r=[rowB_b], w=[rowB_b])
        p.ts("v", rowB[:, 192:224], rowB[:, 192:224], -65536.0, 65536.0,
             ALU.mult, ALU.add, r=[rowB_b], w=[rowB_b])
        p.tt("v", rowB[:, 192:224], rowB[:, 192:224], rowB[:, 224:256],
             ALU.add, r=[rowB_b], w=[rowB_b])
        p.tt("v", rowB[:, 224:256], rowB[:, 128:160], rowB[:, 160:192],
             ALU.min, r=[rowB_b], w=[rowB_b])
        p.tt("v", rowB[:, 224:256], rowB[:, 224:256], rowB[:, 192:224],
             ALU.min, r=[rowB_b], w=[rowB_b])
        p.ts("v", rowB[:, 224:256], rowB[:, 224:256], -1.0, None,
             ALU.mult, r=[rowB_b], w=[rowB_b])
        p.red("v", rowm[:, 0:1], rowB[:, 224:256], ALU.max,
              r=[rowB_b], w=["rowm"])
        p.tt("v", hmn[:, 0:1], hmn[:, 0:1], rowm[:, 0:1], ALU.max,
             r=["rowm", hmn_b], w=[hmn_b])
        p.red("v", rowm[:, 1:2], rowB[:, 128:192], ALU.max,
              r=[rowB_b], w=["rowm"])
        p.tt("v", hmn[:, 1:2], hmn[:, 1:2], rowm[:, 1:2], ALU.max,
             r=["rowm", hmn_b], w=[hmn_b])

    part_max(hmn[:, 0:1], hmn_b, row16[0:1, 0:1], "r16c", 10)
    part_max(hmn[:, 1:2], hmn_b, row16[0:1, 1:2], "r16d", 11)
    p.dma("g", cc_in.ap()[0:1, 4:6], row16[0:1, 0:2], "ccinB", in_buf="r16d")
    p.buf("ccinB").readers[("xc",)] = p.buf("r16c").writer
    p.op("g", lambda E: E.collective_compute("AllReduce", ALU.max,
         replica_groups=[list(range(NCORES))], ins=[cc_in.ap()[0:1, 4:6]],
         outs=[cc_out.ap()[0:1, 4:6]]), r=["ccinB"], w=["ccoutB"])
    p.dma("g", row16[0:1, 2:4], cc_out.ap()[0:1, 4:6], "ccresB", in_buf="ccoutB")
    derive_act_qparams(row16[0:1, 2:3], row16[0:1, 3:4], "ccresB",
                       roww[0:1, 5:6], QP_HHI, QP_HLO, QP_SMM2, QP_INVSH)

    if upto == "attn":
        p.final_wait("y", ["x1hC"])
        return nc

    hqDs = [AR.alloc(f"hqD{i}", 16 * 1024 + i * 32 * 1024, 32 * 1024, BF16)
            for i in range(2)]
    accld = [AR.alloc(f"accld{i}", 80 * 1024 + i * 2048, 2048, F32) for i in range(4)]
    x1r = [AR.alloc(f"x1rD{i}", 88 * 1024 + i * 4096, 4096, F32) for i in range(2)]
    scrDs = [AR.alloc(f"scrD{i}", 96 * 1024 + i * 4096, 4096, F32) for i in range(2)]
    w2s = [AR.alloc(f"w2s{i}", 104 * 1024 + i * 2 * 1024, 2 * 1024, BF16)
           for i in range(4)]
    dst_l0 = out_t if upto == "l0" else x1_h

    for ch in range(T // 512):
        hqD, hqD_b = hqDs[ch % 2]
        for hg in range(32):
            ac, ac_b = accld[hg % 4]
            p.dma("y", ac, acc_h[hg, :, ch * 512:(ch + 1) * 512], ac_b, in_buf="acch")
            scrD, scrD_b = scrDs[hg % 2]
            sD = scrD[:, 0:512]
            p.act(sD, ac, AF.Gelu, r=[ac_b, "qp", "b1c"], w=[scrD_b],
                  bias=b1c[:, hg:hg + 1], scale=qpc(QP_SMM1))
            p.ts("v", sD, sD, qpc(QP_INVSH), MAGIC, ALU.mult, ALU.add,
                 r=[scrD_b, "qp"], w=[scrD_b])
            p.ts("v", sD, sD, MAGIC, qpc(QP_HHI), ALU.subtract, ALU.min,
                 r=[scrD_b, "qp"], w=[scrD_b])
            p.ts("v", hqD[:, hg * 512:(hg + 1) * 512], sD, qpc(QP_HLO), None,
                 ALU.max, r=[scrD_b, "qp"], w=[hqD_b])
        for jp in range(2):
            for jj2 in range(2):
                j = ch * 4 + jp * 2 + jj2
                p.dma("y", x1r[jj2][0], x1_h[j * 128:(j + 1) * 128, :], x1r[jj2][1],
                      in_buf="x1hC")
            banks = [(PS2, "PS2"), (PS3, "PS3"), (PSY, "PSY"), (pb2, "pb2")]
            p.stream_begin(w=[b[1] for b in banks])
            for hg in range(32):
                ws, wsb = w2s[hg % 4]
                p.dma("y", ws, w2q_h[hg * 128:(hg + 1) * 128, :], wsb, in_buf="w2h")
                mms = []
                for jj2 in range(2):
                    jjj = jp * 2 + jj2
                    lhs = hqD[:, hg * 512 + jjj * 128:hg * 512 + (jjj + 1) * 128]
                    mms.append((banks[jj2 * 2][0][:, :], lhs, ws[:, 0:512],
                                hg == 0, hg == 31))
                    mms.append((banks[jj2 * 2 + 1][0][:, :], lhs, ws[:, 512:1024],
                                hg == 0, hg == 31))
                p.stream_step([hqD_b, wsb], mms)
            p.stream_end()
            for jj2 in range(2):
                j = ch * 4 + jp * 2 + jj2
                scrD, scrD_b = scrDs[jj2 % 2]
                for hh in range(2):
                    ps, psn = banks[jj2 * 2 + hh]
                    p.stt("v", scrD[:, hh * 512:(hh + 1) * 512], ps[:, :], qpc(QP_SMM2),
                          b2b[:, hh * 512:(hh + 1) * 512], ALU.mult, ALU.add,
                          r=[psn, "qp", "b2b"], w=[scrD_b])
                    p.tt("v", scrD[:, hh * 512:(hh + 1) * 512],
                         scrD[:, hh * 512:(hh + 1) * 512],
                         x1r[jj2][0][:, hh * 512:(hh + 1) * 512], ALU.add,
                         r=[scrD_b, x1r[jj2][1]], w=[scrD_b])
                p.dma("y", dst_l0[j * 128:(j + 1) * 128, :], scrD, "x1hD", in_buf=scrD_b)

    if upto == "l0":
        p.final_wait("y", ["x1hD"])
        return nc

    x1L = [AR.alloc(f"x1L{i}", i * 4096, 4096, F32) for i in range(4)]
    xn1 = [AR.alloc(f"xn1L{i}", 16 * 1024 + i * 4096, 4096, F32) for i in range(4)]
    xnf1, xnf1_b = AR.alloc("xnf1", 32 * 1024, 8 * 1024, BF16)
    xnf13 = xnf1.rearrange("np (dc tc) -> np dc tc", tc=512)
    hnL, hnL_b = AR.alloc("hnL", 40 * 1024, 4096, F32)
    hfm, hfm_b = AR.alloc("hfm", 44 * 1024, 8 * 2056, F32)
    HS = 514
    dscL, dscL_b = AR.alloc("dscL", 61 * 1024, 2048, F32)
    xkf, xkf_b = AR.alloc("xkf", 63 * 1024, 8 * 1024, BF16)
    xvf, xvf_b = AR.alloc("xvf", 71 * 1024, 8 * 1024, BF16)
    xrf, xrf_b = AR.alloc("xrf", 79 * 1024, 8 * 1024, BF16)
    rfm, rfm_b = AR.alloc("rfm", 87 * 1024, 8 * 1024, BF16)
    ybuf, ybuf_b = AR.alloc("ybuf", 95 * 1024, 8 * 1024, BF16)
    kmS, kmS_b = AR.alloc("kmS", 103 * 1024, 2048, F32)
    KpS, KpS_b = AR.alloc("KpS", 105 * 1024, 2048, F32)
    PbS, PbS_b = AR.alloc("PbS", 107 * 1024, 2056, F32)
    lnS, lnS_b = AR.alloc("lnS", 109 * 1024 + 64, 2056, F32)
    alS, alS_b = AR.alloc("alS", 111 * 1024 + 128, 2048, F32)
    bvS, bvS_b = AR.alloc("bvS", 113 * 1024 + 128, 2048, F32)
    aaS, aaS_b = AR.alloc("aaS", 115 * 1024 + 128, 2056, F32)
    vsb, vsb_b = AR.alloc("vsb", 117 * 1024 + 192, 2048, F32)
    ryf, ryf_b = AR.alloc("ryf", 120 * 1024, 8 * 1024, BF16)
    wsl = [AR.alloc(f"wslL{i}", 174 * 1024 + i * 4096, 4096, BF16) for i in range(4)]

    for ch in range(T // 512):
        for jj in range(4):
            j = ch * 4 + jj
            xa, xab = x1L[jj]
            p.dma("y", xa, x1_h[j * 128:(j + 1) * 128, :], xab, in_buf="x1hD")
            p.red("v", rowm[:, 0:1], xa, ALU.add, r=[xab], w=["rowm"])
            p.act(hnL, xa, AF.Square, r=[xab], w=[hnL_b, "rowm"], accum=rowm[:, 1:2])
            p.ts("v", rowm[:, 4:5], rowm[:, 0:1], 1.0 / D, None, ALU.mult, r=["rowm"], w=["rowm"])
            p.ts("v", rowm[:, 5:6], rowm[:, 1:2], 1.0 / D, None, ALU.mult, r=["rowm"], w=["rowm"])
            p.tt("v", rowm[:, 6:7], rowm[:, 4:5], rowm[:, 4:5], ALU.mult, r=["rowm"], w=["rowm"])
            p.tt("v", rowm[:, 6:7], rowm[:, 5:6], rowm[:, 6:7], ALU.subtract, r=["rowm"], w=["rowm"])
            p.act(rowm[:, 7:8], rowm[:, 6:7], AF.Ln, r=["rowm"], w=["rowm"], bias=LN_EPS)
            p.act(rowm[:, 7:8], rowm[:, 7:8], AF.Exp, r=["rowm"], w=["rowm"], scale=-0.5)
            p.tt("v", rowm[:, 8:9], rowm[:, 4:5], rowm[:, 7:8], ALU.mult, r=["rowm"], w=["rowm"])
            p.ts("v", rowm[:, 8:9], rowm[:, 8:9], -1.0, None, ALU.mult, r=["rowm"], w=["rowm"])
            p.act(xn1[jj][0], xa, AF.Identity, r=[xab, "rowm"], w=[xn1[jj][1]],
                  bias=rowm[:, 8:9], scale=rowm[:, 7:8])
            xna, xnab = xn1[jj]
            p.red("v", rowm[:, 0:1], xna, ALU.add, r=[xnab], w=["rowm"])
            p.act(hnL, xna, AF.Square, r=[xnab], w=[hnL_b, "rowm"], accum=rowm[:, 1:2])
            p.ts("v", rowm[:, 4:5], rowm[:, 0:1], 1.0 / D, None, ALU.mult, r=["rowm"], w=["rowm"])
            p.ts("v", rowm[:, 5:6], rowm[:, 1:2], 1.0 / D, None, ALU.mult, r=["rowm"], w=["rowm"])
            p.tt("v", rowm[:, 6:7], rowm[:, 4:5], rowm[:, 4:5], ALU.mult, r=["rowm"], w=["rowm"])
            p.tt("v", rowm[:, 6:7], rowm[:, 5:6], rowm[:, 6:7], ALU.subtract, r=["rowm"], w=["rowm"])
            p.act(rowm[:, 7:8], rowm[:, 6:7], AF.Ln, r=["rowm"], w=["rowm"], bias=LN_EPS)
            p.act(rowm[:, 7:8], rowm[:, 7:8], AF.Exp, r=["rowm"], w=["rowm"], scale=-0.5)
            p.tt("v", rowm[:, 8:9], rowm[:, 4:5], rowm[:, 7:8], ALU.mult, r=["rowm"], w=["rowm"])
            p.ts("v", rowm[:, 8:9], rowm[:, 8:9], -1.0, None, ALU.mult, r=["rowm"], w=["rowm"])
            p.act(hnL, xna, AF.Identity, r=[xnab, "rowm"], w=[hnL_b],
                  bias=rowm[:, 8:9], scale=rowm[:, 7:8])
            for half in range(2):
                ps, psn = (PS3, "PS3") if half == 0 else (PSY, "PSY")
                p.tr_group([(ps[:, q4 * 128:(q4 + 1) * 128],
                             hnL[:, (half * 4 + q4) * 128:(half * 4 + q4 + 1) * 128],
                             identf[:, :]) for q4 in range(4)],
                           r=[hnL_b, "identf"], w=[psn])
                for q4 in range(4):
                    dc = half * 4 + q4
                    p.act(hfm[:, dc * HS + 1 + jj * 128:dc * HS + 1 + (jj + 1) * 128],
                          ps[:, q4 * 128:(q4 + 1) * 128], AF.Identity,
                          r=[psn, "chp"], w=[hfm_b],
                          bias=chcol(NP_B1, dc), scale=chcol(NP_G1, dc))
            for half in range(2):
                ps, psn = (PS3, "PS3") if half == 0 else (PSY, "PSY")
                p.tr_group([(ps[:, q4 * 128:(q4 + 1) * 128],
                             xna[:, (half * 4 + q4) * 128:(half * 4 + q4 + 1) * 128],
                             identf[:, :]) for q4 in range(4)],
                           r=[xnab, "identf"], w=[psn])
                for q4 in range(4):
                    dc = half * 4 + q4
                    p.cp("v", xnf13[:, dc, jj * 128:(jj + 1) * 128],
                         ps[:, q4 * 128:(q4 + 1) * 128], r=[psn], w=[xnf1_b])
        for g in range(8):
            p.cp("v", hfm[:, g * HS:g * HS + 1], chcol(NP_HB, g), r=["chp"], w=[hfm_b])
        for g in range(8):
            h_g = hfm[:, g * HS + 1:g * HS + 513]
            hp_g = hfm[:, g * HS:g * HS + 512]
            p.tt("v", dscL, h_g, hp_g, ALU.subtract, r=[hfm_b], w=[dscL_b])
            p.ts("v", kmS, dscL, chcol(NP_MK, g), None, ALU.mult,
                 r=[dscL_b, "chp"], w=[kmS_b])
            p.tt("g", xkf[:, g * 512:(g + 1) * 512], kmS, hp_g, ALU.add,
                 r=[kmS_b, hfm_b], w=[xkf_b])
            p.stt("v", xvf[:, g * 512:(g + 1) * 512], dscL, chcol(NP_MV, g), hp_g,
                  ALU.mult, ALU.add, r=[dscL_b, "chp", hfm_b], w=[xvf_b])
            p.ts("v", KpS, dscL, chcol(NP_MR, g), None, ALU.mult,
                 r=[dscL_b, "chp"], w=[KpS_b])
            p.tt("g", xrf[:, g * 512:(g + 1) * 512], KpS, hp_g, ALU.add,
                 r=[KpS_b, hfm_b], w=[xrf_b])
        for g in range(8):
            p.cp("v", chcol(NP_HB, g), hfm[:, g * HS + 512:g * HS + 513],
                 r=[hfm_b], w=["chp"])
        for g in range(8):
            for wi, (wsrc, wsrcb, psj, psnj, xf, xfb) in enumerate([
                    (wtk_b, "wtkb", PS0, "PS0", xkf, xkf_b),
                    (wtv_b, "wtvb", PS1, "PS1", xvf, xvf_b),
                    (wtr_b, "wtrb", PS2, "PS2", xrf, xrf_b)]):
                ws, wsb = wsl[(g * 3 + wi) % 4]
                p.dma("y", ws[:, 0:1024], wsrc[g * 128:(g + 1) * 128, :], wsb,
                      in_buf=wsrcb)
                p.mm_group([(psj[:, :], ws[:, dc * 128:(dc + 1) * 128],
                             xf[:, dc * 512:(dc + 1) * 512], dc == 0, dc == 7)
                            for dc in range(8)], r=[wsb, xfb], w=[psnj])
            p.act(rfm[:, g * 512:(g + 1) * 512], PS2[:, :], AF.Identity,
                  r=["PS2"], w=[rfm_b])
            p.cp("v", vsb, PS1[:, :], r=["PS1"], w=[vsb_b])
            p.red("v", rowm[:, 0:1], PS0[:, :], ALU.max, r=["PS0"], w=["rowm"])
            p.tt("v", rowm[:, 0:1], rowm[:, 0:1], ppe(g), ALU.max, r=["rowm", "wkvs"], w=["rowm"])
            p.ts("v", rowm[:, 1:2], rowm[:, 0:1], -1.0, None, ALU.mult, r=["rowm"], w=["rowm"])
            p.act(kmS, PS0[:, :], AF.Identity, r=["PS0", "rowm"], w=[kmS_b],
                  bias=rowm[:, 1:2])
            p.act(KpS, kmS, AF.Exp, r=[kmS_b], w=[KpS_b])
            p.ts("v", bvS, KpS, chcol(NP_EW, g), None, ALU.mult, r=[KpS_b, "chp"], w=[bvS_b])
            p.act(PbS[:, 0:1], ppe(g), AF.Exp, r=["wkvs", "rowm"], w=[PbS_b],
                  bias=rowm[:, 1:2])
            p.scan("v", PbS[:, 1:513], chcol(NP_EW, g).broadcast_to([128, 512]),
                   bvS, PbS[:, 0:1], r=[bvS_b, "chp", PbS_b], w=[PbS_b])
            p.act(lnS[:, 0:513], PbS[:, 0:513], AF.Ln, r=[PbS_b], w=[lnS_b],
                  bias=1e-38)
            p.tt("v", ppe(g), lnS[:, 512:513], rowm[:, 0:1], ALU.add,
                 r=[lnS_b, "rowm"], w=["wkvs"])
            p.tt("v", dscL, lnS[:, 0:512], kmS, ALU.subtract, r=[lnS_b, kmS_b], w=[dscL_b])
            p.ts("v", alS, dscL, 0.0, None, ALU.min, r=[dscL_b], w=[alS_b])
            p.act(alS, alS, AF.Exp, r=[alS_b], w=[alS_b])
            p.ts("v", bvS, dscL, -1.0, 0.0, ALU.mult, ALU.min, r=[dscL_b], w=[bvS_b])
            p.act(bvS, bvS, AF.Exp, r=[bvS_b], w=[bvS_b])
            p.tt("v", bvS, bvS, vsb, ALU.mult, r=[bvS_b, vsb_b], w=[bvS_b])
            p.cp("v", aaS[:, 0:1], aae(g), r=["wkvs"], w=[aaS_b])
            p.scan("v", aaS[:, 1:513], alS, bvS, aaS[:, 0:1], r=[alS_b, bvS_b, aaS_b],
                   w=[aaS_b])
            p.cp("v", aae(g), aaS[:, 512:513], r=[aaS_b], w=["wkvs"])
            p.stt("v", kmS, KpS, chcol(NP_EU, g), vsb, ALU.mult, ALU.mult,
                  r=[KpS_b, "chp", vsb_b], w=[kmS_b])
            p.tt("v", alS, PbS[:, 0:512], aaS[:, 0:512], ALU.mult,
                 r=[PbS_b, aaS_b], w=[alS_b])
            p.tt("v", alS, alS, kmS, ALU.add, r=[alS_b, kmS_b], w=[alS_b])
            p.stt("v", bvS, KpS, chcol(NP_EU, g), PbS[:, 0:512], ALU.mult, ALU.add,
                  r=[KpS_b, "chp", PbS_b], w=[bvS_b])
            p.act(bvS, bvS, AF.Ln, r=[bvS_b], w=[bvS_b])
            p.act(bvS, bvS, AF.Exp, r=[bvS_b], w=[bvS_b], scale=-1.0)
            p.tt("v", ybuf[:, g * 512:(g + 1) * 512], alS, bvS, ALU.mult,
                 r=[alS_b, bvS_b], w=[ybuf_b])
        for g in range(8):
            p.act(dscL, rfm[:, g * 512:(g + 1) * 512], AF.Sigmoid, r=[rfm_b], w=[dscL_b])
            p.tt("v", ryf[:, g * 512:(g + 1) * 512], dscL, ybuf[:, g * 512:(g + 1) * 512],
                 ALU.mult, r=[dscL_b, ybuf_b], w=[ryf_b])
        for jj in range(4):
            p.tr_group([(ptrb[:, g * 128:(g + 1) * 128],
                         ryf[:, g * 512 + jj * 128:g * 512 + (jj + 1) * 128],
                         ident[:, :]) for g in range(8)],
                       r=[ryf_b, "ident"], w=["ptrb"])
            p.tt("v", xn1[jj][0], xn1[jj][0], ptrb[:, :], ALU.add,
                 r=["ptrb", xn1[jj][1]], w=[xn1[jj][1]])
            p.red("v", lnrow[:, jj:jj + 1], xn1[jj][0], ALU.add, r=[xn1[jj][1]], w=["lnrow"])
            p.act(hnL, xn1[jj][0], AF.Square, r=[xn1[jj][1]], w=[hnL_b, "lnrow"],
                  accum=lnrow[:, 4 + jj:5 + jj])
        hs, hs_b = AR.alloc(f"hs{ch}", 95 * 1024, 24 * 1024, BF16)
        for hg in range(24):
            if hg % 2 == 0:
                hg2 = hg // 2
                wsg, wsgb = wsl[hg2 % 2]
                wsu, wsub = wsl[2 + hg2 % 2]
                p.dma("y", wsg[:, 0:2048].rearrange("np (two h) -> np two h", two=2),
                      wg_b[hg2 * 256:(hg2 + 1) * 256, :].rearrange(
                          "(two np) h -> np two h", np=128), wsgb, in_buf="wgb")
                p.dma("y", wsu[:, 0:2048].rearrange("np (two h) -> np two h", two=2),
                      wu_b[hg2 * 256:(hg2 + 1) * 256, :].rearrange(
                          "(two np) h -> np two h", np=128), wsub, in_buf="wub")
            wo = (hg % 2) * 1024
            psg, psgn = (PS0, "PS0") if hg % 2 == 0 else (PS2, "PS2")
            psu, psun = (PS1, "PS1") if hg % 2 == 0 else (PS3, "PS3")
            p.mm_group([(psg[:, :], wsg[:, wo + dc * 128:wo + (dc + 1) * 128],
                         xnf1[:, dc * 512:(dc + 1) * 512], dc == 0, dc == 7)
                        for dc in range(8)], r=[wsgb, xnf1_b], w=[psgn])
            p.mm_group([(psu[:, :], wsu[:, wo + dc * 128:wo + (dc + 1) * 128],
                         xnf1[:, dc * 512:(dc + 1) * 512], dc == 0, dc == 7)
                        for dc in range(8)], r=[wsub, xnf1_b], w=[psun])
            p.act(hnL[:, 0:512], psg[:, :], AF.Sigmoid, r=[psgn], w=[hnL_b],
                  scale=1.0 / 127.0)
            p.tt("v", hnL[:, 512:1024], hnL[:, 0:512], psg[:, :], ALU.mult,
                 r=[hnL_b, psgn], w=[hnL_b])
            p.tt("v", hs[:, hg * 512:(hg + 1) * 512], hnL[:, 512:1024], psu[:, :],
                 ALU.mult, r=[hnL_b, psun], w=[hs_b])
        kk, kk_b = AR.alloc(f"kk{ch}", 61 * 1024, 32 * 1024, BF16)
        ckf, ckf_b = AR.alloc(f"ckf{ch}", 126 * 1024, 8 * 1024, BF16)
        crf, crf_b = AR.alloc(f"crf{ch}", 150 * 1024, 8 * 1024, BF16)
        d3s = [AR.alloc(f"d3{ch}_{i}", 158 * 1024 + i * 4096, 4096, F32)
               for i in range(4)]
        dall = [(PS0, "PS0"), (PS1, "PS1"), (PS2, "PS2"), (PS3, "PS3")]
        for jp in range(2):
            jjs = (2 * jp, 2 * jp + 1)
            p.stream_begin(w=["PS0", "PS1", "PS2", "PS3"])
            for hg in range(24):
                if hg % 2 == 0:
                    hg2 = hg // 2
                    ws, wsb = wsl[hg2 % 4]
                    p.dma("y", ws[:, 0:2048].rearrange("np (two h) -> np two h", two=2),
                          wd_b[hg2 * 256:(hg2 + 1) * 256, :].rearrange(
                              "(two np) h -> np two h", np=128), wsb, in_buf="wdb")
                wo = (hg % 2) * 1024
                mms = []
                for ji, jj in enumerate(jjs):
                    lhs = hs[:, hg * 512 + jj * 128:hg * 512 + (jj + 1) * 128]
                    for hh in range(2):
                        mms.append((dall[ji * 2 + hh][0][:, :], lhs,
                                    ws[:, wo + hh * 512:wo + hh * 512 + 512],
                                    hg == 0, hg == 23))
                p.stream_step([hs_b, wsb], mms)
            p.stream_end()
            for ji, jj in enumerate(jjs):
                d3, d3_b = d3s[jp * 2 + ji]
                for hh in range(2):
                    ps, psn = dall[ji * 2 + hh]
                    p.ts("v", d3[:, hh * 512:(hh + 1) * 512], ps[:, :], 127.0 ** -3,
                         None, ALU.mult, r=[psn], w=[d3_b])
        u3s = [AR.alloc(f"u3{ch}_{i}", 120 * 1024 + i * 4096, 4096, F32)
               for i in range(2)]
        for jj in range(4):
            p.ts("v", rowm[:, 4:5], lnrow[:, jj:jj + 1], 1.0 / D, None, ALU.mult,
                 r=["lnrow"], w=["rowm"])
            p.ts("v", rowm[:, 5:6], lnrow[:, 4 + jj:5 + jj], 1.0 / D, None, ALU.mult,
                 r=["lnrow"], w=["rowm"])
            p.tt("v", rowm[:, 6:7], rowm[:, 4:5], rowm[:, 4:5], ALU.mult, r=["rowm"], w=["rowm"])
            p.tt("v", rowm[:, 6:7], rowm[:, 5:6], rowm[:, 6:7], ALU.subtract, r=["rowm"], w=["rowm"])
            p.act(rowm[:, 7:8], rowm[:, 6:7], AF.Ln, r=["rowm"], w=["rowm"], bias=LN_EPS)
            p.act(rowm[:, 7:8], rowm[:, 7:8], AF.Exp, r=["rowm"], w=["rowm"], scale=-0.5)
            p.tt("v", rowm[:, 8:9], rowm[:, 4:5], rowm[:, 7:8], ALU.mult, r=["rowm"], w=["rowm"])
            p.ts("v", rowm[:, 8:9], rowm[:, 8:9], -1.0, None, ALU.mult, r=["rowm"], w=["rowm"])
            p.act(hnL, xn1[jj][0], AF.Identity, r=[xn1[jj][1], "rowm"], w=[hnL_b],
                  bias=rowm[:, 8:9], scale=rowm[:, 7:8])
            for half in range(2):
                ps, psn = (PS3, "PS3") if half == 0 else (PSY, "PSY")
                p.tr_group([(ps[:, q4 * 128:(q4 + 1) * 128],
                             hnL[:, (half * 4 + q4) * 128:(half * 4 + q4 + 1) * 128],
                             identf[:, :]) for q4 in range(4)],
                           r=[hnL_b, "identf"], w=[psn])
                for q4 in range(4):
                    dc = half * 4 + q4
                    p.act(hfm[:, dc * HS + 1 + jj * 128:dc * HS + 1 + (jj + 1) * 128],
                          ps[:, q4 * 128:(q4 + 1) * 128], AF.Identity,
                          r=[psn, "chp"], w=[hfm_b],
                          bias=chcol(NP_B2, dc), scale=chcol(NP_G2, dc))
        for g in range(8):
            p.cp("v", hfm[:, g * HS:g * HS + 1], chcol(NP_H2B, g), r=["chp"], w=[hfm_b])
        for g in range(8):
            h_g = hfm[:, g * HS + 1:g * HS + 513]
            hp_g = hfm[:, g * HS:g * HS + 512]
            p.tt("v", hnL[:, 0:512], h_g, hp_g, ALU.subtract, r=[hfm_b], w=[hnL_b])
            p.ts("v", hnL[:, 512:1024], hnL[:, 0:512], chcol(NP_CK, g), None,
                 ALU.mult, r=[hnL_b, "chp"], w=[hnL_b])
            p.tt("g", ckf[:, g * 512:(g + 1) * 512], hnL[:, 512:1024], hp_g, ALU.add,
                 r=[hnL_b, hfm_b], w=[ckf_b])
            p.stt("v", crf[:, g * 512:(g + 1) * 512], hnL[:, 0:512], chcol(NP_CR, g),
                  hp_g, ALU.mult, ALU.add, r=[hnL_b, "chp", hfm_b], w=[crf_b])
        for g in range(8):
            p.cp("v", chcol(NP_H2B, g), hfm[:, g * HS + 512:g * HS + 513],
                 r=[hfm_b], w=["chp"])
        for hg in range(32):
            if hg % 2 == 0:
                hg2 = hg // 2
                ws, wsb = wsl[hg2 % 4]
                p.dma("y", ws[:, 0:2048].rearrange("np (two h) -> np two h", two=2),
                      wck_b[hg2 * 256:(hg2 + 1) * 256, :].rearrange(
                          "(two np) h -> np two h", np=128), wsb, in_buf="wckb")
            wo = (hg % 2) * 1024
            ps, psn = (PS0, "PS0") if hg % 2 == 0 else (PS1, "PS1")
            p.mm_group([(ps[:, :], ws[:, wo + dc * 128:wo + (dc + 1) * 128],
                         ckf[:, dc * 512:(dc + 1) * 512], dc == 0, dc == 7)
                        for dc in range(8)], r=[wsb, ckf_b], w=[psn])
            p.ts("v", hnL[:, 0:512], ps[:, :], 0.0, None, ALU.max, r=[psn], w=[hnL_b])
            p.tt("v", kk[:, hg * 512:(hg + 1) * 512], hnL[:, 0:512], hnL[:, 0:512],
                 ALU.mult, r=[hnL_b], w=[kk_b])
        for jp in range(2):
            jjs = (2 * jp, 2 * jp + 1)
            cvbank = {jjs[0]: [(PS0, "PS0"), (PS1, "PS1")],
                      jjs[1]: [(PS2, "PS2"), (PS3, "PS3")]}
            p.stream_begin(w=["PS0", "PS1", "PS2", "PS3"])
            for hg in range(32):
                if hg % 2 == 0:
                    hg2 = hg // 2
                    ws, wsb = wsl[hg2 % 4]
                    p.dma("y", ws[:, 0:2048].rearrange("np (two h) -> np two h", two=2),
                          wcv_b[hg2 * 256:(hg2 + 1) * 256, :].rearrange(
                              "(two np) h -> np two h", np=128), wsb, in_buf="wcvb")
                wo = (hg % 2) * 1024
                mms = []
                for jj in jjs:
                    lhs = kk[:, hg * 512 + jj * 128:hg * 512 + (jj + 1) * 128]
                    mms.append((cvbank[jj][0][0][:, :], lhs, ws[:, wo:wo + 512],
                                hg == 0, hg == 31))
                    mms.append((cvbank[jj][1][0][:, :], lhs, ws[:, wo + 512:wo + 1024],
                                hg == 0, hg == 31))
                p.stream_step([kk_b, wsb], mms)
            p.stream_end()
            for ji, jj in enumerate(jjs):
                j = ch * 4 + jj
                u3, u3_b = u3s[ji]
                p.stream_begin(w=["PSY", "pb2"])
                for dc in range(8):
                    ws, wsb = wsl[dc % 4]
                    p.dma("y", ws[:, 0:1024], wcr_b[dc * 128:(dc + 1) * 128, :], wsb,
                          in_buf="wcrb")
                    lhs = crf[:, dc * 512 + jj * 128:dc * 512 + (jj + 1) * 128]
                    p.stream_step([crf_b, wsb],
                                  [(PSY[:, :], lhs, ws[:, 0:512], dc == 0, dc == 7),
                                   (pb2[:, :], lhs, ws[:, 512:1024], dc == 0, dc == 7)])
                p.stream_end()
                for hh, psn in enumerate(["PSY", "pb2"]):
                    ps = PSY if hh == 0 else pb2
                    o = hh * 512
                    p.act(hnL[:, o:o + 512], ps[:, :], AF.Exp, r=[psn], w=[hnL_b],
                          scale=-1.0)
                    p.act(hnL[:, o:o + 512], hnL[:, o:o + 512], AF.Ln, r=[hnL_b],
                          w=[hnL_b], bias=1.0)
                    p.act(hnL[:, o:o + 512], hnL[:, o:o + 512], AF.Exp, r=[hnL_b],
                          w=[hnL_b], scale=-1.0)
                for hh in range(2):
                    ps, psn = cvbank[jj][hh]
                    o = hh * 512
                    p.tt("v", u3[:, o:o + 512], hnL[:, o:o + 512], ps[:, :], ALU.mult,
                         r=[hnL_b, psn], w=[u3_b])
                    p.tt("v", u3[:, o:o + 512], u3[:, o:o + 512],
                         xn1[jj][0][:, o:o + 512], ALU.add,
                         r=[u3_b, xn1[jj][1]], w=[u3_b])
                    p.tt("v", u3[:, o:o + 512], u3[:, o:o + 512],
                         x1L[jj][0][:, o:o + 512], ALU.add,
                         r=[u3_b, x1L[jj][1]], w=[u3_b])
            for ji, jj in enumerate(jjs):
                j = ch * 4 + jj
                u3, u3_b = u3s[ji]
                d3l, d3l_b = d3s[jp * 2 + ji]
                p.tt("v", u3, u3, d3l, ALU.add, r=[u3_b, d3l_b], w=[u3_b])
                p.dma("y", out_t[j * 128:(j + 1) * 128, :], u3, "out", in_buf=u3_b)

    p.final_wait("y", ["out"])
    return nc


def kernel(**inputs):
    f32 = np.float32

    def c(a, dt=np.float32):
        return np.ascontiguousarray(np.asarray(a), dtype=dt)

    x = c(inputs["L0_attn_x"])
    wqkvT = c(inputs["L0_attn_w"].T)
    w_out = c(inputs["L0_attn_w_out"])
    w1T = c(inputs["L0_ffn_W1"].T)
    b1v = c(inputs["L0_ffn_b1"]).reshape(1, H0)
    w2T = c(inputs["L0_ffn_W2"].T)
    b2v = c(inputs["L0_ffn_b2"]).reshape(1, D)
    tdv = c(inputs["L1_attn_time_decay"]).reshape(1, D)
    tfv = c(inputs["L1_attn_time_first"]).reshape(1, D)

    def fmH(w, ngrp):
        din, dout = w.shape
        ndc = din // 128
        r = w.reshape(ndc, 128, ngrp, 128)
        return np.ascontiguousarray(r.transpose(2, 1, 0, 3).reshape(ngrp * 128, ndc * 128))

    wtkH = fmH(c(inputs["L1_attn_w_tk"]), 8)
    wtvH = fmH(c(inputs["L1_attn_w_tv"]), 8)
    wtrH = fmH(c(inputs["L1_attn_w_tr"]), 8)
    wckH = fmH(c(inputs["L1_attn_w_ck"]), 32)
    wcv = c(inputs["L1_attn_w_cv"])
    wcr = c(inputs["L1_attn_w_cr"])
    l1g = c(inputs["L1_attn_ln1_g"]).reshape(1, D)
    l1b = c(inputs["L1_attn_ln1_b"]).reshape(1, D)
    l2g = c(inputs["L1_attn_ln2_g"]).reshape(1, D)
    l2b = c(inputs["L1_attn_ln2_b"]).reshape(1, D)
    mxk = c(inputs["L1_attn_mix_k"]).reshape(1, D)
    mxv = c(inputs["L1_attn_mix_v"]).reshape(1, D)
    mxr = c(inputs["L1_attn_mix_r"]).reshape(1, D)
    cmk = c(inputs["L1_attn_cmix_k"]).reshape(1, D)
    cmr = c(inputs["L1_attn_cmix_r"]).reshape(1, D)
    wgH = fmH(np.ascontiguousarray(inputs["L1_ffn_w_gate"].T), 24).astype(np.int32)
    wuH = fmH(np.ascontiguousarray(inputs["L1_ffn_w_up"].T), 24).astype(np.int32)
    wdT = c(inputs["L1_ffn_w_down"].T, np.int32)

    nc = build_program(BUILD_UPTO)
    shared = dict(wqkvT=wqkvT, w_out=w_out, w1T=w1T, b1v=b1v, w2T=w2T, b2v=b2v,
                  tdv=tdv, tfv=tfv, wtkH=wtkH, wtvH=wtvH, wtrH=wtrH, wckH=wckH,
                  wcv=wcv, wcr=wcr, l1g=l1g, l1b=l1b, l2g=l2g, l2b=l2b,
                  mxk=mxk, mxv=mxv, mxr=mxr, cmk=cmk, cmr=cmr,
                  wgH=wgH, wuH=wuH, wdT=wdT)
    in_maps = [dict(shared, x=np.ascontiguousarray(x[b])) for b in range(x.shape[0])]
    res = run_bass_kernel_spmd(nc, in_maps, core_ids=list(range(x.shape[0])))
    return np.stack([res.results[b]["o"] for b in range(x.shape[0])], axis=0)


if __name__ == "__main__":
    import reference
    ins = reference.setup_inputs()
    ins = {k: np.asarray(v) for k, v in ins.items()}
    out = kernel(**ins)
    print(out.shape, out.dtype)


# version 16
# speedup vs baseline: 1.1485x; 1.0079x over previous
import numpy as np
import concourse.bass as bass
import concourse.mybir as mybir
from concourse.bass_utils import run_bass_kernel_spmd

F32 = mybir.dt.float32
BF16 = mybir.dt.bfloat16
I32 = mybir.dt.int32
AF = mybir.ActivationFunctionType
ALU = mybir.AluOpType
AX = mybir.AxisListType

MAGIC = 1.5 * 2.0 ** 23
T, D = 4096, 1024
NT = T // 128
NCORES = 8
H0, H1 = 4096, 3072
NHEAD, HL = 16, 64
LN_EPS = 1e-5
ASC = 0.125

BUILD_UPTO = "full"


def set_shape(t, ncores=8):
    global T, NT, NCORES
    T = t
    NT = t // 128
    NCORES = ncores


class Buf:
    __slots__ = ("name", "writer", "readers", "sem", "fills")

    def __init__(self, name):
        self.name = name
        self.writer = None
        self.readers = {}
        self.sem = None
        self.fills = 0


class Prog:
    def __init__(self, nc):
        self.nc = nc
        self.eng = {"t": nc.tensor, "v": nc.vector, "s": nc.scalar,
                    "g": nc.gpsimd, "y": nc.sync}
        self.psem = {e: nc.alloc_semaphore(f"ps_{e}") for e in "tvsg"}
        self.pcnt = {e: 0 for e in "tvsg"}
        self.obs = {e: {} for e in "tvsgy"}
        self.bufs = {}
        self.nsem = 0

    def buf(self, n):
        if isinstance(n, Buf):
            return n
        b = self.bufs.get(n)
        if b is None:
            b = Buf(n)
            self.bufs[n] = b
        return b

    def _wait(self, e, sem, val, clock=None):
        k = sem.num
        if self.obs[e].get(k, 0) >= val:
            return
        self.eng[e].wait_ge(sem, val)
        obs = self.obs[e]
        obs[k] = max(obs.get(k, 0), val)
        if clock:
            for k2, v2 in clock.items():
                if obs.get(k2, 0) < v2:
                    obs[k2] = v2

    def _gather(self, e, reads, writes):
        need = {}

        def add(evt):
            if evt is None:
                return
            sem, val = evt[0], evt[1]
            clk = evt[2] if len(evt) > 2 else None
            cur = need.get(sem.num)
            if cur is None or cur[1] < val:
                need[sem.num] = (sem, val, clk)

        for b in reads:
            add(b.writer)
        for b in writes:
            add(b.writer)
            for evt in b.readers.values():
                add(evt)
        for sem, val, clk in need.values():
            self._wait(e, sem, val, clk)

    def op(self, e, fn, r=(), w=()):
        reads = [self.buf(x) for x in r]
        writes = [self.buf(x) for x in w]
        self._gather(e, reads, writes)
        ins = fn(self.eng[e])
        self.pcnt[e] += 1
        ins.then_inc(self.psem[e], 1)
        evt = (self.psem[e], self.pcnt[e], dict(self.obs[e]))
        for b in writes:
            b.writer = evt
            b.readers = {}
        for b in reads:
            b.readers[("e", e)] = evt
        return ins

    def tt(self, e, out, a, b, alu, r=(), w=()):
        self.op(e, lambda E: E.tensor_tensor(out, a, b, alu), r, w)

    def ts(self, e, out, a, s1, s2, op0, op1=None, r=(), w=()):
        if s2 is None and op1 is None:
            self.op(e, lambda E: E.tensor_single_scalar(out, a, s1, op0), r, w)
        else:
            self.op(e, lambda E: E.tensor_scalar(out, a, s1, s2, op0, op1), r, w)

    def stt(self, e, out, a, s, b, op0, op1, r=(), w=()):
        self.op(e, lambda E: E.scalar_tensor_tensor(out, a, s, b, op0, op1), r, w)

    def cp(self, e, out, a, r=(), w=()):
        self.op(e, lambda E: E.tensor_copy(out, a), r, w)

    def recip(self, out, a, r=(), w=()):
        self.op("v", lambda E: E.reciprocal(out, a), r, w)

    def red(self, e, out, a, alu, r=(), w=(), absval=None):
        self.op(e, lambda E: E.tensor_reduce(out, a, AX.X, alu,
                apply_absolute_value=absval), r, w)

    def act(self, out, a, func, r=(), w=(), bias=0.0, scale=1.0, accum=None):
        if isinstance(bias, float) and bias != 0.0:
            bias = self.constc(bias)
            r = list(r) + ["cvals"]
        self.op("s", lambda E: E.activation(out, a, func, bias=bias, scale=scale,
                accum_out=accum), r, w)

    def mm(self, out, lhsT, rhs, start, stop, r=(), w=()):
        self.op("t", lambda E: E.matmul(out, lhsT, rhs, start=start, stop=stop), r, w)

    def mm_group(self, mms, r=(), w=()):
        reads = [self.buf(x) for x in r]
        writes = [self.buf(x) for x in w]
        self._gather("t", reads, writes)
        E = self.eng["t"]
        for (out, lhsT, rhs, start, stop) in mms:
            ins = E.matmul(out, lhsT, rhs, start=start, stop=stop)
        self.pcnt["t"] += 1
        ins.then_inc(self.psem["t"], 1)
        evt = (self.psem["t"], self.pcnt["t"], dict(self.obs["t"]))
        for b in writes:
            b.writer = evt
            b.readers = {}
        for b in reads:
            b.readers[("e", "t")] = evt
        return ins

    def stream_begin(self, w=()):
        self._stream_w = [self.buf(x) for x in w]
        self._gather("t", [], self._stream_w)
        self._stream_evt = None

    def stream_step(self, rbufs, mms):
        rb = [self.buf(x) for x in rbufs]
        self._gather("t", rb, [])
        E = self.eng["t"]
        ins = None
        for (out, lhsT, rhs, start, stop) in mms:
            ins = E.matmul(out, lhsT, rhs, start=start, stop=stop)
        self.pcnt["t"] += 1
        ins.then_inc(self.psem["t"], 1)
        evt = (self.psem["t"], self.pcnt["t"], dict(self.obs["t"]))
        for b in rb:
            b.readers[("e", "t")] = evt
        self._stream_evt = evt

    def stream_end(self):
        for b in self._stream_w:
            b.writer = self._stream_evt
            b.readers = {}

    def tr_group(self, trs, r=(), w=()):
        reads = [self.buf(x) for x in r]
        writes = [self.buf(x) for x in w]
        self._gather("t", reads, writes)
        E = self.eng["t"]
        for (out, a, idt) in trs:
            ins = E.transpose(out, a, idt)
        self.pcnt["t"] += 1
        ins.then_inc(self.psem["t"], 1)
        evt = (self.psem["t"], self.pcnt["t"], dict(self.obs["t"]))
        for b in writes:
            b.writer = evt
            b.readers = {}
        for b in reads:
            b.readers[("e", "t")] = evt
        return ins

    def trans(self, out, a, idt, r=(), w=()):
        self.op("t", lambda E: E.transpose(out, a, idt), r, w)

    def scan(self, e, out, d0, d1, init, r=(), w=()):
        self.op(e, lambda E: E.tensor_tensor_scan(out, d0, d1, init,
                ALU.mult, ALU.add), r, w)

    def mset(self, ap, val, w, r=()):
        self.op("g", lambda E: E.memset(ap, val), r, w)

    def afsel(self, ap, cmp, fill, pattern, chmul, r=(), w=()):
        self.op("g", lambda E: E.affine_select(out=ap, in_=ap, compare_op=cmp,
                fill=fill, base=0, pattern=pattern, channel_multiplier=chmul), r, w)

    def dma(self, e, out_ap, in_ap, out_buf, in_buf=None, slow=False):
        ob = self.buf(out_buf)
        rb = [self.buf(in_buf)] if in_buf is not None else []
        self._gather(e, rb, [ob])
        if ob.sem is None:
            self.nsem += 1
            ob.sem = self.nc.alloc_semaphore(f"dm{self.nsem}")
        ins = self.eng[e].dma_start(out=out_ap, in_=in_ap,
                                    allow_slow_non_contiguous=slow)
        ob.fills += 1
        ins.then_inc(ob.sem, 16)
        evt = (ob.sem, 16 * ob.fills, dict(self.obs[e]))
        ob.writer = evt
        ob.readers = {}
        for b in rb:
            b.readers[("d", ob.name, ob.fills)] = evt
        return ins

    def final_wait(self, e, names):
        for n in names:
            b = self.buf(n)
            if b.writer is not None:
                self._wait(e, b.writer[0], b.writer[1])


class Arena:
    def __init__(self, prog, name, nbytes):
        self.p = prog
        self.t = prog.nc.alloc_sbuf_tensor(name, [128, nbytes], mybir.dt.uint8)
        self.nbytes = nbytes
        self.live = []
        self.gen = 0

    def alloc(self, name, lo, nbytes, dtype):
        hi = lo + nbytes
        assert hi <= self.nbytes, (name, lo, nbytes)
        self.gen += 1
        b = Buf(f"{name}#{self.gen}")
        keep = []
        for (l2, h2, b2) in self.live:
            if l2 < hi and lo < h2:
                if b2.writer is not None:
                    b.readers[("w", b2.name)] = b2.writer
                for k, evt in b2.readers.items():
                    b.readers[(b2.name,) + (k if isinstance(k, tuple) else (k,))] = evt
                if lo <= l2 and h2 <= hi:
                    continue
            keep.append((l2, h2, b2))
        keep.append((lo, hi, b))
        self.live = keep
        self.p.bufs[b.name] = b
        es = mybir.dt.size(dtype)
        ap = self.t[:, lo:hi].bitcast(dtype)
        return ap, b


def build_program(upto="full"):
    nc = bass.Bass("TRN2", target_bir_lowering=False, debug=False)
    p = Prog(nc)

    def din(name, shape, dt=F32):
        return nc.dram_tensor(name, shape, dt, kind="ExternalInput").ap()

    x_in = din("x", [T, D])
    wqkvT = din("wqkvT", [D, 3 * D])
    w_out = din("w_out", [D, D])
    w1T = din("w1T", [D, H0])
    b1v = din("b1v", [1, H0])
    w2T = din("w2T", [H0, D])
    b2v = din("b2v", [1, D])
    tdv = din("tdv", [1, D]); tfv = din("tfv", [1, D])
    wtkH = din("wtkH", [8 * 128, 1024])
    wtvH = din("wtvH", [8 * 128, 1024])
    wtrH = din("wtrH", [8 * 128, 1024])
    wckH = din("wckH", [32 * 128, 1024])
    wcv = din("wcv", [H0, D])
    wcr = din("wcr", [D, D])
    l1g = din("l1g", [1, D]); l1b = din("l1b", [1, D])
    l2g = din("l2g", [1, D]); l2b = din("l2b", [1, D])
    mxk = din("mxk", [1, D]); mxv = din("mxv", [1, D]); mxr = din("mxr", [1, D])
    cmk = din("cmk", [1, D]); cmr = din("cmr", [1, D])
    wgH = din("wgH", [24 * 128, 1024], I32)
    wuH = din("wuH", [24 * 128, 1024], I32)
    wdT = din("wdT", [H1, D], I32)
    out_t = nc.dram_tensor("o", [T, D], F32, kind="ExternalOutput").ap()

    x1_h = nc.dram_tensor("x1_h", [T, D], F32).ap()
    w1q_h = nc.dram_tensor("w1q_h", [D, H0], BF16).ap()
    acc_h = nc.dram_tensor("acc_h", [32, 128, T], F32).ap()
    w2q_h = nc.dram_tensor("w2q_h", [H0, D], BF16).ap()
    cc_in = nc.dram_tensor("cc_in", [1, 16], F32)
    cc_out = nc.dram_tensor("cc_out", [1, 16], F32)
    wtk_b = nc.dram_tensor("wtk_b", [8 * 128, 1024], BF16).ap()
    wtv_b = nc.dram_tensor("wtv_b", [8 * 128, 1024], BF16).ap()
    wtr_b = nc.dram_tensor("wtr_b", [8 * 128, 1024], BF16).ap()
    wck_b = nc.dram_tensor("wck_b", [32 * 128, 1024], BF16).ap()
    wcv_b = nc.dram_tensor("wcv_b", [H0, D], BF16).ap()
    wcr_b = nc.dram_tensor("wcr_b", [D, D], BF16).ap()
    wg_b = nc.dram_tensor("wg_b", [24 * 128, 1024], BF16).ap()
    wu_b = nc.dram_tensor("wu_b", [24 * 128, 1024], BF16).ap()
    wd_b = nc.dram_tensor("wd_b", [H1, D], BF16).ap()

    ident = nc.alloc_sbuf_tensor("ident", [128, 128], BF16)
    identf = nc.alloc_sbuf_tensor("identf", [128, 128], F32)
    ones_r = nc.alloc_sbuf_tensor("ones_r", [1, 128], F32)
    ones_cb = nc.alloc_sbuf_tensor("ones_cb", [128, 1], BF16)
    ones_rb1 = nc.alloc_sbuf_tensor("ones_rb1", [1, 128], BF16)
    sc0 = nc.alloc_sbuf_tensor("sc0", [128, NT], F32)
    sb0 = nc.alloc_sbuf_tensor("sb0", [128, NT], F32)
    qsc = nc.alloc_sbuf_tensor("qsc", [128, NT], F32)
    qsb = nc.alloc_sbuf_tensor("qsb", [128, NT], F32)
    qp = nc.alloc_sbuf_tensor("qp", [128, 12], F32)
    (QP_XHI, QP_XLO, QP_SMM1, QP_INVSH, QP_HHI, QP_HLO, QP_SMM2,
     QP_IW1, QP_IW2, QP_INVSX) = range(10)

    def qpc(i):
        return qp[:, i:i + 1]

    rowm = nc.alloc_sbuf_tensor("rowm", [128, 16], F32)
    row16 = nc.alloc_sbuf_tensor("row16", [1, 16], F32)
    roww = nc.alloc_sbuf_tensor("roww", [1, 16], F32)
    b1c = nc.alloc_sbuf_tensor("b1c", [128, 32], F32)
    b2b = nc.alloc_sbuf_tensor("b2b", [128, 1024], F32)
    chp = nc.alloc_sbuf_tensor("chp", [128, 8 * 16], F32)
    (NP_W, NP_EW, NP_U, NP_G1, NP_B1, NP_G2, NP_B2, NP_MK, NP_MV, NP_MR,
     NP_CK, NP_CR, NP_HB, NP_H2B, NP_EU) = range(15)

    def chcol(i, g):
        return chp[:, i * 8 + g:i * 8 + g + 1]

    wkvs = nc.alloc_sbuf_tensor("wkvs", [128, 32], F32)

    def ppe(g):
        return wkvs[:, g:g + 1]

    def aae(g):
        return wkvs[:, 8 + g:8 + g + 1]

    lnrow = nc.alloc_sbuf_tensor("lnrow", [128, 16], F32)

    PS0 = nc.alloc_psum_tensor("PS0", [128, 512], F32)
    PS1 = nc.alloc_psum_tensor("PS1", [128, 512], F32)
    PS2 = nc.alloc_psum_tensor("PS2", [128, 512], F32)
    PS3 = nc.alloc_psum_tensor("PS3", [128, 512], F32)
    PSY = nc.alloc_psum_tensor("PSY", [128, 512], F32)
    pS = nc.alloc_psum_tensor("pS", [64, 512], F32)
    ptrb = nc.alloc_psum_tensor("ptrb", [128, 1024], BF16)
    pb2 = nc.alloc_psum_tensor("pb2", [128, 512], F32)

    AR = Arena(p, "arena", 200 * 1024)

    cvals = nc.alloc_sbuf_tensor("cvals", [128, 8], F32)
    _creg = {}

    def constc(v):
        if v not in _creg:
            i = len(_creg)
            assert i < 8
            p.mset(cvals[:, i:i + 1], v, ["cvals"], r=["cvals"])
            _creg[v] = cvals[:, i:i + 1]
        return _creg[v]

    p.constc = constc

    p.mset(ident[:, :], 1.0, ["ident"])
    p.afsel(ident[:, :], ALU.is_equal, 0.0, [[1, 128]], -1, ["ident"], ["ident"])
    p.mset(identf[:, :], 1.0, ["identf"])
    p.afsel(identf[:, :], ALU.is_equal, 0.0, [[1, 128]], -1, ["identf"], ["identf"])
    p.mset(ones_r[:, :], 1.0, ["ones_r"])
    p.mset(ones_cb[:, :], 1.0, ["ones_cb"])
    p.mset(ones_rb1[:, :], 1.0, ["ones_rb1"])
    p.mset(wkvs[:, :], 0.0, ["wkvs"])
    for g in range(8):
        p.mset(ppe(g), -1e38, ["wkvs"], r=["wkvs"])

    for src, idx in [(tdv, NP_W), (tfv, NP_U), (l1g, NP_G1), (l1b, NP_B1),
                     (l2g, NP_G2), (l2b, NP_B2), (mxk, NP_MK), (mxv, NP_MV),
                     (mxr, NP_MR), (cmk, NP_CK), (cmr, NP_CR)]:
        p.dma("y", chp[:, idx * 8:(idx + 1) * 8],
              src.rearrange("one (g np) -> (one np) g", np=128), "chp", slow=True)
    p.act(chp[:, NP_EW * 8:(NP_EW + 1) * 8], chp[:, NP_W * 8:(NP_W + 1) * 8],
          AF.Exp, r=["chp"], w=["chp"])
    p.act(chp[:, NP_EU * 8:(NP_EU + 1) * 8], chp[:, NP_U * 8:(NP_U + 1) * 8],
          AF.Exp, r=["chp"], w=["chp"])
    p.mset(chp[:, NP_HB * 8:(NP_H2B + 1) * 8], 0.0, ["chp"], r=["chp"])
    p.dma("y", b1c[:, :], b1v.rearrange("one (c np) -> (one np) c", np=128), "b1c",
          slow=True)

    b2row, b2row_b = AR.alloc("b2row", 175 * 1024, 4096, F32)
    p.dma("y", b2row[0:1, :], b2v[0:1, :], b2row_b)
    for hh in range(2):
        p.op("t", lambda E, hh=hh: E.matmul(pb2[:, :], ones_r[0:1, :],
             b2row[0:1, hh * 512:(hh + 1) * 512], start=True, stop=True),
             r=[b2row_b, "ones_r"], w=["pb2"])
        p.cp("v", b2b[:, hh * 512:(hh + 1) * 512], pb2[:, :], r=["pb2"], w=["b2b"])

    def bcast_col(src_ap, dst_ap, rbufs, wbuf):
        p.op("t", lambda E: E.matmul(pb2[:, 0:1], ones_r[0:1, :], src_ap,
             start=True, stop=True), r=list(rbufs) + ["ones_r"], w=["pb2"])
        p.cp("v", dst_ap, pb2[:, 0:1], r=["pb2"], w=[wbuf])

    def part_max(src_ap, src_buf, dst_row_ap, dst_buf, tmpcol):
        p.trans(pb2[0:1, 0:128], src_ap, identf[:, :], r=[src_buf, "identf"], w=["pb2"])
        p.red("v", rowm[0:1, tmpcol:tmpcol + 1], pb2[0:1, 0:128], ALU.max,
              r=["pb2"], w=["rowm"])
        p.dma("g", dst_row_ap, rowm[0:1, tmpcol:tmpcol + 1], dst_buf, in_buf="rowm")

    xr = [AR.alloc(f"xrA{i}", i * 4096, 4096, F32) for i in range(4)]
    scrA, scrA_b = AR.alloc("scrA", 175 * 1024, 4096, F32)
    mnmx, mnmx_b = AR.alloc("mnmxA", 179 * 1024, 64, F32)
    p.mset(mnmx[:, 0:2], -1e38, [mnmx_b])

    wst = [AR.alloc(f"wstg{i}", 180 * 1024 + i * 4 * 1024, 4 * 1024, F32)
           for i in range(2)]
    wqo = [AR.alloc(f"wqo{i}", 188 * 1024 + i * 2 * 1024, 2 * 1024, BF16)
           for i in range(2)]
    wmx, wmx_b = AR.alloc("wmx", 192 * 1024, 64, F32)

    def quant_weight(wsrc, nchunks, X, hbm_dst, iw_slot, tag):
        iw_col = QP_IW1 if tag == "w1" else QP_IW2
        pieces = []
        nh = max(1, X // 1024)
        Xp = X // nh
        for ci in range(nchunks):
            for hswp in range(nh):
                pieces.append((ci, hswp * Xp, Xp))
        p.mset(wmx[:, 0:1], 0.0, [wmx_b])
        for pi, (ci, c0, Xp_) in enumerate(pieces):
            a, b = wst[pi % 2]
            p.dma("y", a[:, 0:Xp_], wsrc[ci * 128:(ci + 1) * 128, c0:c0 + Xp_], b)
            p.red("v", wmx[:, 1:2], a[:, 0:Xp_], ALU.max, r=[b], w=[wmx_b], absval=True)
            p.tt("v", wmx[:, 0:1], wmx[:, 0:1], wmx[:, 1:2], ALU.max, r=[wmx_b], w=[wmx_b])
        part_max(wmx[:, 0:1], wmx_b, roww[0:1, iw_slot:iw_slot + 1], f"sw{tag}", 12)
        p.ts("v", roww[0:1, 4 + iw_slot:5 + iw_slot], roww[0:1, iw_slot:iw_slot + 1],
             1.0 / 127.0, 1e-8, ALU.mult, ALU.max, r=[f"sw{tag}", "roww"], w=["roww"])
        p.recip(roww[0:1, 6:7], roww[0:1, 4 + iw_slot:5 + iw_slot], r=["roww"], w=["roww"])
        bcast_col(roww[0:1, 6:7], qpc(iw_col), ["roww"], "qp")
        for pi, (ci, c0, Xp_) in enumerate(pieces):
            a, b = wst[pi % 2]
            qa, qb = wqo[pi % 2]
            p.dma("y", a[:, 0:Xp_], wsrc[ci * 128:(ci + 1) * 128, c0:c0 + Xp_], b)
            p.ts("v", a[:, 0:Xp_], a[:, 0:Xp_], qpc(iw_col), MAGIC, ALU.mult, ALU.add,
                 r=[b, "qp"], w=[b])
            p.ts("v", a[:, 0:Xp_], a[:, 0:Xp_], MAGIC, 127.0, ALU.subtract, ALU.min,
                 r=[b], w=[b])
            p.ts("v", qa[:, 0:Xp_], a[:, 0:Xp_], -127.0, None, ALU.max, r=[b], w=[qb])
            p.dma("g", hbm_dst[ci * 128:(ci + 1) * 128, c0:c0 + Xp_], qa[:, 0:Xp_],
                  f"{tag}h", in_buf=qb)


    wqv, wqv_b = AR.alloc("wqv", 16 * 1024, 48 * 1024, BF16)
    wou, wou_b = AR.alloc("wou", 64 * 1024, 16 * 1024, BF16)
    tri4, tri4_b = AR.alloc("tri4", 80 * 1024, 1024, BF16)
    S_f, S_f_b = AR.alloc("S_f", 81 * 1024, 4096, F32)
    S_b_, S_b_b = AR.alloc("S_b", 85 * 1024, 2048, BF16)
    carry_f, carry_f_b = AR.alloc("carry_f", 87 * 1024, 4096, F32)
    carry_bb, carry_b_b = AR.alloc("carry_bb", 91 * 1024, 2048, BF16)
    def _cset(base, tag):
        ns = {}
        off = [("xnb", 0, 2048, BF16), ("xnf", 2048, 2048, BF16),
               ("eqs", 4096, 4096, F32), ("keb", 8192, 2048, BF16),
               ("lnk", 10240, 4096, F32), ("scC", 14336, 4096, F32),
               ("qsbf", 18432, 2048, BF16), ("qsf", 20480, 4096, BF16),
               ("kef", 24576, 4096, BF16), ("vtm", 28672, 2048, BF16),
               ("atb", 30720, 1024, BF16), ("atb2", 31744, 1024, BF16),
               ("ytm", 32768, 2048, BF16), ("yf", 34816, 2048, BF16),
               ("s16", 36864, 128, F32), ("x1p", 37376, 4096, F32)]
        for nm, o, nb, dt in off:
            ns[nm] = AR.alloc(f"{nm}{tag}", base + o, nb, dt)
        return ns

    csets = [_cset(93 * 1024, "0"), _cset(134 * 1024, "1")]

    p.mset(tri4[:, :], 1.0, [tri4_b])
    for i in range(4):
        p.afsel(tri4[:, i * 128:(i + 1) * 128], ALU.is_ge, 0.0, [[1, 128]], -1,
                [tri4_b], [tri4_b])
    p.mset(S_f[:, :], 0.0, [S_f_b])
    p.mset(S_b_[:, :], 0.0, [S_b_b])
    p.mset(carry_f[:, :], 0.0, [carry_f_b])
    p.mset(carry_bb[:, :], 0.0, [carry_b_b])

    for dc in range(8):
        p.dma("g", wqv[:, dc * 3072:(dc + 1) * 3072],
              wqkvT[dc * 128:(dc + 1) * 128, :], wqv_b)
        p.dma("g", wou[:, dc * 1024:(dc + 1) * 1024],
              w_out[dc * 128:(dc + 1) * 128, :], wou_b)
    for dst, srcw, nm in [(wtk_b, wtkH, "wtkb"), (wtv_b, wtvH, "wtvb"),
                          (wtr_b, wtrH, "wtrb"), (wck_b, wckH, "wckb"),
                          (wcv_b, wcv, "wcvb"), (wcr_b, wcr, "wcrb"),
                          (wg_b, wgH, "wgb"), (wu_b, wuH, "wub"),
                          (wd_b, wdT, "wdb")]:
        p.dma("g", dst, srcw, nm)

    def wq_ap(dc, lo, n):
        return wqv[:, dc * 3072 + lo:dc * 3072 + lo + n]

    for j in range(NT):
        cs = csets[j % 2]
        (xnb, xnb_b) = cs["xnb"]; (xnf, xnf_b) = cs["xnf"]
        (eqs, eqs_b) = cs["eqs"]; (keb, keb_b) = cs["keb"]
        (lnk, lnk_b) = cs["lnk"]; (scC, scC_b) = cs["scC"]
        (qsbf, qsbf_b) = cs["qsbf"]; (qsf, qsf_b) = cs["qsf"]
        (kef, kef_b) = cs["kef"]; (vtm, vtm_b) = cs["vtm"]
        (atb, atb_b) = cs["atb"]; (atb2, atb2_b) = cs["atb2"]
        (ytm, ytm_b) = cs["ytm"]; (yf, yf_b) = cs["yf"]
        (s16, s16_b) = cs["s16"]; (x1p, x1p_b) = cs["x1p"]
        xa, xb = xr[j % 4]
        p.dma("y", xa, x_in[j * 128:(j + 1) * 128, :], xb)
        p.red("v", rowm[:, 0:1], xa, ALU.add, r=[xb], w=["rowm"])
        p.act(scrA, xa, AF.Square, r=[xb], w=[scrA_b, "rowm"], accum=rowm[:, 1:2])
        p.red("v", rowm[:, 2:3], xa, ALU.max, r=[xb], w=["rowm"])
        p.red("v", rowm[:, 3:4], xa, ALU.min, r=[xb], w=["rowm"])
        p.ts("v", rowm[:, 4:5], rowm[:, 0:1], 1.0 / D, None, ALU.mult, r=["rowm"], w=["rowm"])
        p.ts("v", rowm[:, 5:6], rowm[:, 1:2], 1.0 / D, None, ALU.mult, r=["rowm"], w=["rowm"])
        p.tt("v", rowm[:, 6:7], rowm[:, 4:5], rowm[:, 4:5], ALU.mult, r=["rowm"], w=["rowm"])
        p.tt("v", rowm[:, 6:7], rowm[:, 5:6], rowm[:, 6:7], ALU.subtract, r=["rowm"], w=["rowm"])
        p.act(rowm[:, 7:8], rowm[:, 6:7], AF.Ln, r=["rowm"], w=["rowm"], bias=LN_EPS)
        p.act(sc0[:, j:j + 1], rowm[:, 7:8], AF.Exp, r=["rowm"], w=["sc0"], scale=-0.5)
        p.tt("v", sb0[:, j:j + 1], rowm[:, 4:5], sc0[:, j:j + 1], ALU.mult,
             r=["rowm", "sc0"], w=["sb0"])
        p.ts("v", sb0[:, j:j + 1], sb0[:, j:j + 1], -1.0, None, ALU.mult, r=["sb0"], w=["sb0"])
        p.tt("v", rowm[:, 8:9], rowm[:, 3:4], rowm[:, 4:5], ALU.subtract, r=["rowm"], w=["rowm"])
        p.tt("v", rowm[:, 8:9], rowm[:, 8:9], sc0[:, j:j + 1], ALU.mult,
             r=["rowm", "sc0"], w=["rowm"])
        p.ts("v", rowm[:, 8:9], rowm[:, 8:9], -1.0, None, ALU.mult, r=["rowm"], w=["rowm"])
        p.tt("v", mnmx[:, 0:1], mnmx[:, 0:1], rowm[:, 8:9], ALU.max,
             r=["rowm", mnmx_b], w=[mnmx_b])
        p.tt("v", rowm[:, 9:10], rowm[:, 2:3], rowm[:, 4:5], ALU.subtract, r=["rowm"], w=["rowm"])
        p.tt("v", rowm[:, 9:10], rowm[:, 9:10], sc0[:, j:j + 1], ALU.mult,
             r=["rowm", "sc0"], w=["rowm"])
        p.tt("v", mnmx[:, 1:2], mnmx[:, 1:2], rowm[:, 9:10], ALU.max,
             r=["rowm", mnmx_b], w=[mnmx_b])

        p.act(xnb, xa, AF.Identity, r=[xb, "sc0", "sb0"], w=[xnb_b],
              bias=sb0[:, j:j + 1], scale=sc0[:, j:j + 1])
        p.tr_group([(ptrb[:, dc * 128:(dc + 1) * 128],
                     xnb[:, dc * 128:(dc + 1) * 128], ident[:, :])
                    for dc in range(8)], r=[xnb_b, "ident"], w=["ptrb"])
        p.cp("v", xnf, ptrb[:, :], r=["ptrb"], w=[xnf_b])
        pcs_banks = [(PS0, "PS0"), (PS1, "PS1"), (PS2, "PS2"), (PS3, "PS3"),
                     (PS0, "PS0"), (PS1, "PS1")]
        for pc in range(6):
            ps, psn = pcs_banks[pc]
            p.mm_group([(ps[:, :], xnf[:, dc * 128:(dc + 1) * 128],
                         wq_ap(dc, pc * 512, 512), dc == 0, dc == 7)
                        for dc in range(8)], r=[xnf_b, wqv_b], w=[psn])
            if pc == 0:
                p.act(eqs[:, 0:512], ps[:, :], AF.Exp, r=[psn], w=[eqs_b], scale=ASC)
            elif pc == 1:
                p.act(eqs[:, 512:1024], ps[:, :], AF.Exp, r=[psn], w=[eqs_b], scale=ASC)
            elif pc in (2, 3):
                o = (pc - 2) * 512
                p.act(scC[:, o:o + 512], ps[:, :], AF.Exp, r=[psn], w=[scC_b], scale=ASC)
                p.ts("v", keb[:, o:o + 512], scC[:, o:o + 512], 1e-6, None, ALU.add,
                     r=[scC_b], w=[keb_b])
            else:
                o = (pc - 4) * 512
                p.cp("v", vtm[:, o:o + 512], ps[:, :], r=[psn], w=[vtm_b])
        for hh, (ps, psn) in enumerate([(PS2, "PS2"), (PS3, "PS3")]):
            mms = [(ps[:, :], tri4[:, 0:128], keb[:, hh * 512:(hh + 1) * 512],
                    True, j == 0)]
            if j > 0:
                mms.append((ps[:, :], ones_rb1[0:1, :],
                            carry_bb[0:1, hh * 512:(hh + 1) * 512], False, True))
            p.mm_group(mms, r=[tri4_b, keb_b, "ones_rb1", carry_b_b], w=[psn])
            p.act(lnk[:, hh * 512:(hh + 1) * 512], ps[:, :], AF.Ln, r=[psn], w=[lnk_b])
        if j < NT - 1:
            for hh in range(2):
                p.op("t", lambda E, hh=hh: E.matmul(pb2[0:1, :], ones_cb[:, 0:1],
                     keb[:, hh * 512:(hh + 1) * 512], start=True, stop=True),
                     r=["ones_cb", keb_b], w=["pb2"])
                p.tt("v", carry_f[0:1, hh * 512:(hh + 1) * 512],
                     carry_f[0:1, hh * 512:(hh + 1) * 512], pb2[0:1, :], ALU.add,
                     r=["pb2", carry_f_b], w=[carry_f_b])
            p.cp("v", carry_bb[0:1, :], carry_f[0:1, :], r=[carry_f_b], w=[carry_b_b])
        p.red("v", s16[:, 0:16], eqs[:, :].rearrange("np (h l) -> np h l", l=64),
              ALU.add, r=[eqs_b], w=[s16_b])
        p.act(s16[:, 16:32], s16[:, 0:16], AF.Ln, r=[s16_b], w=[s16_b])
        p.tt("v", lnk[:, :].rearrange("np (h l) -> np h l", l=64),
             lnk[:, :].rearrange("np (h l) -> np h l", l=64),
             s16[:, 16:32].unsqueeze(2).broadcast_to([128, 16, 64]),
             ALU.add, r=[lnk_b, s16_b], w=[lnk_b])
        p.act(scC, lnk[:, :], AF.Exp, r=[lnk_b], w=[scC_b], scale=-1.0)
        p.tt("v", qsbf, eqs, scC, ALU.mult, r=[eqs_b, scC_b], w=[qsbf_b])
        for rnd in range(2):
            p.tr_group([(ptrb[0:64, hi * 128:(hi + 1) * 128],
                         qsbf[:, (rnd * 8 + hi) * 64:(rnd * 8 + hi + 1) * 64],
                         ident[:, :]) for hi in range(8)],
                       r=[qsbf_b, "ident"], w=["ptrb"])
            p.cp("v", qsf[0:64, rnd * 1024:(rnd + 1) * 1024], ptrb[0:64, :],
                 r=["ptrb"], w=[qsf_b])
            p.tr_group([(ptrb[0:64, hi * 128:(hi + 1) * 128],
                         keb[:, (rnd * 8 + hi) * 64:(rnd * 8 + hi + 1) * 64],
                         ident[:, :]) for hi in range(8)],
                       r=[keb_b, "ident"], w=["ptrb"])
            p.cp("v", kef[0:64, rnd * 1024:(rnd + 1) * 1024], ptrb[0:64, :],
                 r=["ptrb"], w=[kef_b])

        def hslice(buf_ap, h):
            return buf_ap[0:64, h * 128:(h + 1) * 128]

        for quad in range(4):
            ps, psn = (PS2, "PS2") if quad % 2 == 0 else (PS3, "PS3")
            ab, abn = (atb, atb_b) if quad % 2 == 0 else (atb2, atb2_b)
            p.mm_group([(ps[:, hi * 128:(hi + 1) * 128], hslice(kef, quad * 4 + hi),
                         hslice(qsf, quad * 4 + hi), True, True)
                        for hi in range(4)], r=[kef_b, qsf_b], w=[psn])
            p.tt("v", ab[:, :], ps[:, :], tri4[:, :], ALU.mult, r=[psn, tri4_b], w=[abn])
        for half in range(2):
            mms = []
            for qd in (half * 2, half * 2 + 1):
                ab, abn = (atb, atb_b) if qd % 2 == 0 else (atb2, atb2_b)
                for hi in range(4):
                    h = qd * 4 + hi
                    mms.append((PSY[:, (h % 8) * 64:(h % 8) * 64 + 64],
                                ab[:, hi * 128:(hi + 1) * 128],
                                vtm[:, h * 64:(h + 1) * 64], True, j == 0))
                    if j > 0:
                        mms.append((PSY[:, (h % 8) * 64:(h % 8) * 64 + 64],
                                    hslice(qsf, h),
                                    S_b_[0:64, h * 64:(h + 1) * 64], False, True))
            p.mm_group(mms, r=[atb_b, atb2_b, vtm_b, qsf_b, S_b_b], w=["PSY"])
            p.cp("v", ytm[:, half * 512:(half + 1) * 512], PSY[:, :],
                 r=["PSY"], w=[ytm_b])
        for half in range(2):
            p.mm_group([(pS[0:64, hi * 64:(hi + 1) * 64],
                         keb[:, (half * 8 + hi) * 64:(half * 8 + hi + 1) * 64],
                         vtm[:, (half * 8 + hi) * 64:(half * 8 + hi + 1) * 64],
                         True, True) for hi in range(8)],
                       r=[keb_b, vtm_b], w=["pS"])
            p.tt("v", S_f[0:64, half * 512:(half + 1) * 512],
                 S_f[0:64, half * 512:(half + 1) * 512], pS[0:64, :], ALU.add,
                 r=["pS", S_f_b], w=[S_f_b])
        p.cp("v", S_b_[0:64, :], S_f[0:64, :], r=[S_f_b], w=[S_b_b])
        p.tr_group([(ptrb[:, dc * 128:(dc + 1) * 128],
                     ytm[:, dc * 128:(dc + 1) * 128], ident[:, :])
                    for dc in range(8)], r=[ytm_b, "ident"], w=["ptrb"])
        p.cp("v", yf, ptrb[:, :], r=["ptrb"], w=[yf_b])
        for hh, (ps, psn) in enumerate([(PS0, "PS0"), (PS1, "PS1")]):
            p.mm_group([(ps[:, :], yf[:, dc * 128:(dc + 1) * 128],
                         wou[:, dc * 1024 + hh * 512:dc * 1024 + hh * 512 + 512],
                         dc == 0, dc == 7) for dc in range(8)],
                       r=[yf_b, wou_b], w=[psn])
            p.tt("v", x1p[:, hh * 512:(hh + 1) * 512], xa[:, hh * 512:(hh + 1) * 512],
                 ps[:, :], ALU.add, r=[xb, psn], w=[x1p_b])
        p.dma("y", (out_t if upto == "attn" else x1_h)[j * 128:(j + 1) * 128, :],
              x1p, "x1hC", in_buf=x1p_b)
        if j == min(8, NT - 2):
            quant_weight(w1T, 8, H0, w1q_h, 0, "w1")
        if j == min(18, NT - 1):
            quant_weight(w2T, 32, D, w2q_h, 1, "w2")

    part_max(mnmx[:, 0:1], mnmx_b, row16[0:1, 0:1], "r16a", 10)
    part_max(mnmx[:, 1:2], mnmx_b, row16[0:1, 1:2], "r16b", 11)
    p.dma("g", cc_in.ap()[0:1, 0:2], row16[0:1, 0:2], "ccinA", in_buf="r16b")
    p.buf("ccinA").readers[("xa",)] = p.buf("r16a").writer
    p.op("g", lambda E: E.collective_compute("AllReduce", ALU.max,
         replica_groups=[list(range(NCORES))], ins=[cc_in.ap()[0:1, 0:2]],
         outs=[cc_out.ap()[0:1, 0:2]]), r=["ccinA"], w=["ccoutA"])
    p.dma("g", row16[0:1, 2:4], cc_out.ap()[0:1, 0:2], "ccresA", in_buf="ccoutA")

    def derive_act_qparams(nmin_ap, gmax_ap, res_buf, sw_ap, hi_c, lo_c, smm_c,
                           inv_c):
        r, rb = row16, "row16"
        p.tt("v", r[0:1, 4:5], gmax_ap, nmin_ap, ALU.add, r=[res_buf, rb], w=[rb])
        p.ts("v", r[0:1, 4:5], r[0:1, 4:5], 1.0 / 255.0, 1e-8, ALU.mult, ALU.max,
             r=[rb], w=[rb])
        p.recip(r[0:1, 5:6], r[0:1, 4:5], r=[rb], w=[rb])
        p.tt("v", r[0:1, 6:7], nmin_ap, r[0:1, 5:6], ALU.mult, r=[rb], w=[rb])
        p.ts("v", r[0:1, 6:7], r[0:1, 6:7], MAGIC, MAGIC, ALU.add, ALU.subtract,
             r=[rb], w=[rb])
        p.ts("v", r[0:1, 6:7], r[0:1, 6:7], 128.0, 127.0, ALU.subtract, ALU.min,
             r=[rb], w=[rb])
        p.ts("v", r[0:1, 6:7], r[0:1, 6:7], -128.0, None, ALU.max, r=[rb], w=[rb])
        p.ts("v", r[0:1, 7:8], r[0:1, 6:7], -1.0, 127.0, ALU.mult, ALU.add, r=[rb], w=[rb])
        p.ts("v", r[0:1, 8:9], r[0:1, 6:7], -1.0, -128.0, ALU.mult, ALU.add, r=[rb], w=[rb])
        p.tt("v", r[0:1, 9:10], r[0:1, 4:5], sw_ap, ALU.mult, r=[rb, "roww"], w=[rb])
        bcast_col(r[0:1, 7:8], qpc(hi_c), [rb], "qp")
        bcast_col(r[0:1, 8:9], qpc(lo_c), [rb], "qp")
        bcast_col(r[0:1, 9:10], qpc(smm_c), [rb], "qp")
        bcast_col(r[0:1, 5:6], qpc(inv_c), [rb], "qp")

    derive_act_qparams(row16[0:1, 2:3], row16[0:1, 3:4], "ccresA",
                       roww[0:1, 4:5], QP_XHI, QP_XLO, QP_SMM1, QP_INVSX)
    p.ts("v", qsc[:, :], sc0[:, :], qpc(QP_INVSX), None, ALU.mult,
         r=["sc0", "qp"], w=["qsc"])
    p.ts("v", qsb[:, :], sb0[:, :], qpc(QP_INVSX), None, ALU.mult,
         r=["sb0", "qp"], w=["qsb"])

    def emit_xq_tile(j, jj, xa, xb, scr, scr_b, xqtm, xqtm_b, xqf3, xqf_b):
        p.act(scr, xa, AF.Identity, r=[xb, "qsc", "qsb"], w=[scr_b],
              bias=qsb[:, j:j + 1], scale=qsc[:, j:j + 1])
        p.ts("v", scr, scr, MAGIC, MAGIC, ALU.add, ALU.subtract, r=[scr_b], w=[scr_b])
        p.ts("v", scr, scr, qpc(QP_XHI), None, ALU.min, r=[scr_b, "qp"], w=[scr_b])
        p.ts("v", xqtm, scr, qpc(QP_XLO), None, ALU.max, r=[scr_b, "qp"], w=[xqtm_b])
        p.tr_group([(ptrb[:, dc * 128:(dc + 1) * 128],
                     xqtm[:, dc * 128:(dc + 1) * 128], ident[:, :])
                    for dc in range(8)], r=[xqtm_b, "ident"], w=["ptrb"])
        p.cp("v", xqf3[:, :, jj * 128:(jj + 1) * 128],
             ptrb[:, :].rearrange("np (dc tc) -> np dc tc", tc=128),
             r=["ptrb"], w=[xqf_b])

    w1r, w1r_b = AR.alloc("w1r", 40 * 1024, 64 * 1024, BF16)
    xqfBs = [AR.alloc(f"xqfB{i}", 104 * 1024 + i * 8 * 1024, 8 * 1024, BF16)
             for i in range(2)]
    scrBs = [AR.alloc(f"scrB{i}", 120 * 1024 + i * 4096, 4096, F32) for i in range(2)]
    xqtmBs = [AR.alloc(f"xqtmB{i}", 128 * 1024 + i * 2048, 2048, BF16) for i in range(2)]
    accst = [AR.alloc(f"accst{i}", 132 * 1024 + 64 + i * 2048, 2048, F32)
             for i in range(2)]
    hmn, hmn_b = AR.alloc("hmn", 132 * 1024, 64, F32)
    rowB, rowB_b = AR.alloc("rowB", 166 * 1024, 1024, F32)
    p.mset(hmn[:, 0:2], -1e38, [hmn_b])
    for dc in range(8):
        p.dma("y", w1r[:, dc * 4096:(dc + 1) * 4096],
              w1q_h[dc * 128:(dc + 1) * 128, :], w1r_b, in_buf="w1h")

    for ch in range(T // 512):
        xqfB, xqfB_b = xqfBs[ch % 2]
        xqfB3 = xqfB.rearrange("np (dc tc) -> np dc tc", tc=512)
        for jj in range(4):
            j = ch * 4 + jj
            xa, xb = xr[j % 4]
            p.dma("y", xa, x_in[j * 128:(j + 1) * 128, :], xb)
            scrB, scrB_b = scrBs[jj % 2]
            xqtmB, xqtmB_b = xqtmBs[jj % 2]
            emit_xq_tile(j, jj, xa, xb, scrB, scrB_b, xqtmB, xqtmB_b, xqfB3, xqfB_b)
        for hg in range(32):
            ps = PS0 if hg % 2 == 0 else PS1
            psn = "PS0" if hg % 2 == 0 else "PS1"
            p.mm_group([(ps[:, :],
                         w1r[:, dc * 4096 + hg * 128:dc * 4096 + (hg + 1) * 128],
                         xqfB[:, dc * 512:(dc + 1) * 512], dc == 0, dc == 7)
                        for dc in range(8)], r=[w1r_b, xqfB_b], w=[psn])
            ac, ac_b = accst[hg % 2]
            p.act(ac, ps[:, :], AF.Identity, r=[psn], w=[ac_b])
            p.dma("y", acc_h[hg, :, ch * 512:(ch + 1) * 512], ac, "acch",
                  in_buf=ac_b)
            p.red("v", rowB[:, hg:hg + 1], ac, ALU.max, r=[ac_b], w=[rowB_b])
            p.red("v", rowB[:, 32 + hg:33 + hg], ac, ALU.min, r=[ac_b],
                  w=[rowB_b])
        p.ts("v", rowB[:, 64:128], rowB[:, 0:64], qpc(QP_SMM1), None,
             ALU.mult, r=[rowB_b, "qp"], w=[rowB_b])
        p.tt("v", rowB[:, 64:96], rowB[:, 64:96], b1c[:, 0:32], ALU.add,
             r=[rowB_b, "b1c"], w=[rowB_b])
        p.tt("v", rowB[:, 96:128], rowB[:, 96:128], b1c[:, 0:32], ALU.add,
             r=[rowB_b, "b1c"], w=[rowB_b])
        p.act(rowB[:, 128:192], rowB[:, 64:128], AF.Gelu,
              r=[rowB_b], w=[rowB_b])
        p.ts("v", rowB[:, 192:224], rowB[:, 64:96],
             -0.751791534894327, None, ALU.is_ge, r=[rowB_b], w=[rowB_b])
        p.ts("v", rowB[:, 224:256], rowB[:, 96:128],
             -0.751791534894327, None, ALU.is_le, r=[rowB_b], w=[rowB_b])
        p.tt("v", rowB[:, 192:224], rowB[:, 192:224], rowB[:, 224:256],
             ALU.mult, r=[rowB_b], w=[rowB_b])
        p.ts("v", rowB[:, 224:256], rowB[:, 192:224],
             -0.16997120747990369, None, ALU.mult, r=[rowB_b], w=[rowB_b])
        p.ts("v", rowB[:, 192:224], rowB[:, 192:224], -65536.0, 65536.0,
             ALU.mult, ALU.add, r=[rowB_b], w=[rowB_b])
        p.tt("v", rowB[:, 192:224], rowB[:, 192:224], rowB[:, 224:256],
             ALU.add, r=[rowB_b], w=[rowB_b])
        p.tt("v", rowB[:, 224:256], rowB[:, 128:160], rowB[:, 160:192],
             ALU.min, r=[rowB_b], w=[rowB_b])
        p.tt("v", rowB[:, 224:256], rowB[:, 224:256], rowB[:, 192:224],
             ALU.min, r=[rowB_b], w=[rowB_b])
        p.ts("v", rowB[:, 224:256], rowB[:, 224:256], -1.0, None,
             ALU.mult, r=[rowB_b], w=[rowB_b])
        p.red("v", rowm[:, 0:1], rowB[:, 224:256], ALU.max,
              r=[rowB_b], w=["rowm"])
        p.tt("v", hmn[:, 0:1], hmn[:, 0:1], rowm[:, 0:1], ALU.max,
             r=["rowm", hmn_b], w=[hmn_b])
        p.red("v", rowm[:, 1:2], rowB[:, 128:192], ALU.max,
              r=[rowB_b], w=["rowm"])
        p.tt("v", hmn[:, 1:2], hmn[:, 1:2], rowm[:, 1:2], ALU.max,
             r=["rowm", hmn_b], w=[hmn_b])

    part_max(hmn[:, 0:1], hmn_b, row16[0:1, 0:1], "r16c", 10)
    part_max(hmn[:, 1:2], hmn_b, row16[0:1, 1:2], "r16d", 11)
    p.dma("g", cc_in.ap()[0:1, 4:6], row16[0:1, 0:2], "ccinB", in_buf="r16d")
    p.buf("ccinB").readers[("xc",)] = p.buf("r16c").writer
    p.op("g", lambda E: E.collective_compute("AllReduce", ALU.max,
         replica_groups=[list(range(NCORES))], ins=[cc_in.ap()[0:1, 4:6]],
         outs=[cc_out.ap()[0:1, 4:6]]), r=["ccinB"], w=["ccoutB"])
    p.dma("g", row16[0:1, 2:4], cc_out.ap()[0:1, 4:6], "ccresB", in_buf="ccoutB")
    derive_act_qparams(row16[0:1, 2:3], row16[0:1, 3:4], "ccresB",
                       roww[0:1, 5:6], QP_HHI, QP_HLO, QP_SMM2, QP_INVSH)

    if upto == "attn":
        p.final_wait("y", ["x1hC"])
        return nc

    hqDs = [AR.alloc(f"hqD{i}", 16 * 1024 + i * 32 * 1024, 32 * 1024, BF16)
            for i in range(2)]
    accld = [AR.alloc(f"accld{i}", 80 * 1024 + i * 2048, 2048, F32) for i in range(4)]
    x1r = [AR.alloc(f"x1rD{i}", 88 * 1024 + i * 4096, 4096, F32) for i in range(2)]
    scrDs = [AR.alloc(f"scrD{i}", 96 * 1024 + i * 4096, 4096, F32) for i in range(2)]
    w2s = [AR.alloc(f"w2s{i}", 104 * 1024 + i * 2 * 1024, 2 * 1024, BF16)
           for i in range(4)]
    dst_l0 = out_t if upto == "l0" else x1_h

    for ch in range(T // 512):
        hqD, hqD_b = hqDs[ch % 2]
        for hg in range(32):
            ac, ac_b = accld[hg % 4]
            p.dma("y", ac, acc_h[hg, :, ch * 512:(ch + 1) * 512], ac_b, in_buf="acch")
            scrD, scrD_b = scrDs[hg % 2]
            sD = scrD[:, 0:512]
            p.act(sD, ac, AF.Gelu, r=[ac_b, "qp", "b1c"], w=[scrD_b],
                  bias=b1c[:, hg:hg + 1], scale=qpc(QP_SMM1))
            p.ts("v", sD, sD, qpc(QP_INVSH), MAGIC, ALU.mult, ALU.add,
                 r=[scrD_b, "qp"], w=[scrD_b])
            p.ts("v", sD, sD, MAGIC, qpc(QP_HHI), ALU.subtract, ALU.min,
                 r=[scrD_b, "qp"], w=[scrD_b])
            p.ts("v", hqD[:, hg * 512:(hg + 1) * 512], sD, qpc(QP_HLO), None,
                 ALU.max, r=[scrD_b, "qp"], w=[hqD_b])
        for jp in range(2):
            for jj2 in range(2):
                j = ch * 4 + jp * 2 + jj2
                p.dma("y", x1r[jj2][0], x1_h[j * 128:(j + 1) * 128, :], x1r[jj2][1],
                      in_buf="x1hC")
            banks = [(PS2, "PS2"), (PS3, "PS3"), (PSY, "PSY"), (pb2, "pb2")]
            p.stream_begin(w=[b[1] for b in banks])
            for hg in range(32):
                ws, wsb = w2s[hg % 4]
                p.dma("y", ws, w2q_h[hg * 128:(hg + 1) * 128, :], wsb, in_buf="w2h")
                mms = []
                for jj2 in range(2):
                    jjj = jp * 2 + jj2
                    lhs = hqD[:, hg * 512 + jjj * 128:hg * 512 + (jjj + 1) * 128]
                    mms.append((banks[jj2 * 2][0][:, :], lhs, ws[:, 0:512],
                                hg == 0, hg == 31))
                    mms.append((banks[jj2 * 2 + 1][0][:, :], lhs, ws[:, 512:1024],
                                hg == 0, hg == 31))
                p.stream_step([hqD_b, wsb], mms)
            p.stream_end()
            for jj2 in range(2):
                j = ch * 4 + jp * 2 + jj2
                scrD, scrD_b = scrDs[jj2 % 2]
                for hh in range(2):
                    ps, psn = banks[jj2 * 2 + hh]
                    p.stt("v", scrD[:, hh * 512:(hh + 1) * 512], ps[:, :], qpc(QP_SMM2),
                          b2b[:, hh * 512:(hh + 1) * 512], ALU.mult, ALU.add,
                          r=[psn, "qp", "b2b"], w=[scrD_b])
                    p.tt("v", scrD[:, hh * 512:(hh + 1) * 512],
                         scrD[:, hh * 512:(hh + 1) * 512],
                         x1r[jj2][0][:, hh * 512:(hh + 1) * 512], ALU.add,
                         r=[scrD_b, x1r[jj2][1]], w=[scrD_b])
                p.dma("y", dst_l0[j * 128:(j + 1) * 128, :], scrD, "x1hD", in_buf=scrD_b)

    if upto == "l0":
        p.final_wait("y", ["x1hD"])
        return nc

    x1L = [AR.alloc(f"x1L{i}", i * 4096, 4096, F32) for i in range(4)]
    xn1 = [AR.alloc(f"xn1L{i}", 16 * 1024 + i * 4096, 4096, F32) for i in range(4)]
    xnf1, xnf1_b = AR.alloc("xnf1", 32 * 1024, 8 * 1024, BF16)
    xnf13 = xnf1.rearrange("np (dc tc) -> np dc tc", tc=512)
    hnL, hnL_b = AR.alloc("hnL", 40 * 1024, 4096, F32)
    hfm, hfm_b = AR.alloc("hfm", 44 * 1024, 8 * 2056, F32)
    HS = 514
    dscL, dscL_b = AR.alloc("dscL", 61 * 1024, 2048, F32)
    xkf, xkf_b = AR.alloc("xkf", 63 * 1024, 8 * 1024, BF16)
    xvf, xvf_b = AR.alloc("xvf", 71 * 1024, 8 * 1024, BF16)
    xrf, xrf_b = AR.alloc("xrf", 79 * 1024, 8 * 1024, BF16)
    rfm, rfm_b = AR.alloc("rfm", 87 * 1024, 8 * 1024, BF16)
    ybuf, ybuf_b = AR.alloc("ybuf", 95 * 1024, 8 * 1024, BF16)
    kmS, kmS_b = AR.alloc("kmS", 103 * 1024, 2048, F32)
    KpS, KpS_b = AR.alloc("KpS", 105 * 1024, 2048, F32)
    PbS, PbS_b = AR.alloc("PbS", 107 * 1024, 2056, F32)
    lnS, lnS_b = AR.alloc("lnS", 109 * 1024 + 64, 2056, F32)
    alS, alS_b = AR.alloc("alS", 111 * 1024 + 128, 2048, F32)
    bvS, bvS_b = AR.alloc("bvS", 113 * 1024 + 128, 2048, F32)
    aaS, aaS_b = AR.alloc("aaS", 115 * 1024 + 128, 2056, F32)
    vsb, vsb_b = AR.alloc("vsb", 117 * 1024 + 192, 2048, F32)
    ryf, ryf_b = AR.alloc("ryf", 120 * 1024, 8 * 1024, BF16)
    wsl = [AR.alloc(f"wslL{i}", 174 * 1024 + i * 4096, 4096, BF16) for i in range(4)]

    for ch in range(T // 512):
        for jj in range(4):
            j = ch * 4 + jj
            xa, xab = x1L[jj]
            p.dma("y", xa, x1_h[j * 128:(j + 1) * 128, :], xab, in_buf="x1hD")
            p.red("v", rowm[:, 0:1], xa, ALU.add, r=[xab], w=["rowm"])
            p.act(hnL, xa, AF.Square, r=[xab], w=[hnL_b, "rowm"], accum=rowm[:, 1:2])
            p.ts("v", rowm[:, 4:5], rowm[:, 0:1], 1.0 / D, None, ALU.mult, r=["rowm"], w=["rowm"])
            p.ts("v", rowm[:, 5:6], rowm[:, 1:2], 1.0 / D, None, ALU.mult, r=["rowm"], w=["rowm"])
            p.tt("v", rowm[:, 6:7], rowm[:, 4:5], rowm[:, 4:5], ALU.mult, r=["rowm"], w=["rowm"])
            p.tt("v", rowm[:, 6:7], rowm[:, 5:6], rowm[:, 6:7], ALU.subtract, r=["rowm"], w=["rowm"])
            p.act(rowm[:, 7:8], rowm[:, 6:7], AF.Ln, r=["rowm"], w=["rowm"], bias=LN_EPS)
            p.act(rowm[:, 7:8], rowm[:, 7:8], AF.Exp, r=["rowm"], w=["rowm"], scale=-0.5)
            p.tt("v", rowm[:, 8:9], rowm[:, 4:5], rowm[:, 7:8], ALU.mult, r=["rowm"], w=["rowm"])
            p.ts("v", rowm[:, 8:9], rowm[:, 8:9], -1.0, None, ALU.mult, r=["rowm"], w=["rowm"])
            p.act(xn1[jj][0], xa, AF.Identity, r=[xab, "rowm"], w=[xn1[jj][1]],
                  bias=rowm[:, 8:9], scale=rowm[:, 7:8])
            xna, xnab = xn1[jj]
            p.red("v", rowm[:, 0:1], xna, ALU.add, r=[xnab], w=["rowm"])
            p.act(hnL, xna, AF.Square, r=[xnab], w=[hnL_b, "rowm"], accum=rowm[:, 1:2])
            p.ts("v", rowm[:, 4:5], rowm[:, 0:1], 1.0 / D, None, ALU.mult, r=["rowm"], w=["rowm"])
            p.ts("v", rowm[:, 5:6], rowm[:, 1:2], 1.0 / D, None, ALU.mult, r=["rowm"], w=["rowm"])
            p.tt("v", rowm[:, 6:7], rowm[:, 4:5], rowm[:, 4:5], ALU.mult, r=["rowm"], w=["rowm"])
            p.tt("v", rowm[:, 6:7], rowm[:, 5:6], rowm[:, 6:7], ALU.subtract, r=["rowm"], w=["rowm"])
            p.act(rowm[:, 7:8], rowm[:, 6:7], AF.Ln, r=["rowm"], w=["rowm"], bias=LN_EPS)
            p.act(rowm[:, 7:8], rowm[:, 7:8], AF.Exp, r=["rowm"], w=["rowm"], scale=-0.5)
            p.tt("v", rowm[:, 8:9], rowm[:, 4:5], rowm[:, 7:8], ALU.mult, r=["rowm"], w=["rowm"])
            p.ts("v", rowm[:, 8:9], rowm[:, 8:9], -1.0, None, ALU.mult, r=["rowm"], w=["rowm"])
            p.act(hnL, xna, AF.Identity, r=[xnab, "rowm"], w=[hnL_b],
                  bias=rowm[:, 8:9], scale=rowm[:, 7:8])
            for half in range(2):
                ps, psn = (PS3, "PS3") if half == 0 else (PSY, "PSY")
                p.tr_group([(ps[:, q4 * 128:(q4 + 1) * 128],
                             hnL[:, (half * 4 + q4) * 128:(half * 4 + q4 + 1) * 128],
                             identf[:, :]) for q4 in range(4)],
                           r=[hnL_b, "identf"], w=[psn])
                for q4 in range(4):
                    dc = half * 4 + q4
                    p.act(hfm[:, dc * HS + 1 + jj * 128:dc * HS + 1 + (jj + 1) * 128],
                          ps[:, q4 * 128:(q4 + 1) * 128], AF.Identity,
                          r=[psn, "chp"], w=[hfm_b],
                          bias=chcol(NP_B1, dc), scale=chcol(NP_G1, dc))
            for half in range(2):
                ps, psn = (PS3, "PS3") if half == 0 else (PSY, "PSY")
                p.tr_group([(ps[:, q4 * 128:(q4 + 1) * 128],
                             xna[:, (half * 4 + q4) * 128:(half * 4 + q4 + 1) * 128],
                             identf[:, :]) for q4 in range(4)],
                           r=[xnab, "identf"], w=[psn])
                for q4 in range(4):
                    dc = half * 4 + q4
                    p.cp("v", xnf13[:, dc, jj * 128:(jj + 1) * 128],
                         ps[:, q4 * 128:(q4 + 1) * 128], r=[psn], w=[xnf1_b])
        for g in range(8):
            p.cp("v", hfm[:, g * HS:g * HS + 1], chcol(NP_HB, g), r=["chp"], w=[hfm_b])
        for g in range(8):
            h_g = hfm[:, g * HS + 1:g * HS + 513]
            hp_g = hfm[:, g * HS:g * HS + 512]
            p.tt("v", dscL, h_g, hp_g, ALU.subtract, r=[hfm_b], w=[dscL_b])
            p.ts("v", kmS, dscL, chcol(NP_MK, g), None, ALU.mult,
                 r=[dscL_b, "chp"], w=[kmS_b])
            p.tt("g", xkf[:, g * 512:(g + 1) * 512], kmS, hp_g, ALU.add,
                 r=[kmS_b, hfm_b], w=[xkf_b])
            p.stt("v", xvf[:, g * 512:(g + 1) * 512], dscL, chcol(NP_MV, g), hp_g,
                  ALU.mult, ALU.add, r=[dscL_b, "chp", hfm_b], w=[xvf_b])
            p.ts("v", KpS, dscL, chcol(NP_MR, g), None, ALU.mult,
                 r=[dscL_b, "chp"], w=[KpS_b])
            p.tt("g", xrf[:, g * 512:(g + 1) * 512], KpS, hp_g, ALU.add,
                 r=[KpS_b, hfm_b], w=[xrf_b])
        for g in range(8):
            p.cp("v", chcol(NP_HB, g), hfm[:, g * HS + 512:g * HS + 513],
                 r=[hfm_b], w=["chp"])
        for g in range(8):
            for wi, (wsrc, wsrcb, psj, psnj, xf, xfb) in enumerate([
                    (wtk_b, "wtkb", PS0, "PS0", xkf, xkf_b),
                    (wtv_b, "wtvb", PS1, "PS1", xvf, xvf_b),
                    (wtr_b, "wtrb", PS2, "PS2", xrf, xrf_b)]):
                ws, wsb = wsl[(g * 3 + wi) % 4]
                p.dma("y", ws[:, 0:1024], wsrc[g * 128:(g + 1) * 128, :], wsb,
                      in_buf=wsrcb)
                p.mm_group([(psj[:, :], ws[:, dc * 128:(dc + 1) * 128],
                             xf[:, dc * 512:(dc + 1) * 512], dc == 0, dc == 7)
                            for dc in range(8)], r=[wsb, xfb], w=[psnj])
            p.act(rfm[:, g * 512:(g + 1) * 512], PS2[:, :], AF.Identity,
                  r=["PS2"], w=[rfm_b])
            p.cp("v", vsb, PS1[:, :], r=["PS1"], w=[vsb_b])
            p.red("v", rowm[:, 0:1], PS0[:, :], ALU.max, r=["PS0"], w=["rowm"])
            p.tt("v", rowm[:, 0:1], rowm[:, 0:1], ppe(g), ALU.max, r=["rowm", "wkvs"], w=["rowm"])
            p.ts("v", rowm[:, 1:2], rowm[:, 0:1], -1.0, None, ALU.mult, r=["rowm"], w=["rowm"])
            p.act(kmS, PS0[:, :], AF.Identity, r=["PS0", "rowm"], w=[kmS_b],
                  bias=rowm[:, 1:2])
            p.act(KpS, kmS, AF.Exp, r=[kmS_b], w=[KpS_b])
            p.ts("v", bvS, KpS, chcol(NP_EW, g), None, ALU.mult, r=[KpS_b, "chp"], w=[bvS_b])
            p.act(PbS[:, 0:1], ppe(g), AF.Exp, r=["wkvs", "rowm"], w=[PbS_b],
                  bias=rowm[:, 1:2])
            p.scan("v", PbS[:, 1:513], chcol(NP_EW, g).broadcast_to([128, 512]),
                   bvS, PbS[:, 0:1], r=[bvS_b, "chp", PbS_b], w=[PbS_b])
            p.act(lnS[:, 0:513], PbS[:, 0:513], AF.Ln, r=[PbS_b], w=[lnS_b],
                  bias=1e-38)
            p.tt("v", ppe(g), lnS[:, 512:513], rowm[:, 0:1], ALU.add,
                 r=[lnS_b, "rowm"], w=["wkvs"])
            p.tt("v", dscL, lnS[:, 0:512], kmS, ALU.subtract, r=[lnS_b, kmS_b], w=[dscL_b])
            p.ts("v", alS, dscL, 0.0, None, ALU.min, r=[dscL_b], w=[alS_b])
            p.act(alS, alS, AF.Exp, r=[alS_b], w=[alS_b])
            p.ts("v", bvS, dscL, -1.0, 0.0, ALU.mult, ALU.min, r=[dscL_b], w=[bvS_b])
            p.act(bvS, bvS, AF.Exp, r=[bvS_b], w=[bvS_b])
            p.tt("v", bvS, bvS, vsb, ALU.mult, r=[bvS_b, vsb_b], w=[bvS_b])
            p.cp("v", aaS[:, 0:1], aae(g), r=["wkvs"], w=[aaS_b])
            p.scan("v", aaS[:, 1:513], alS, bvS, aaS[:, 0:1], r=[alS_b, bvS_b, aaS_b],
                   w=[aaS_b])
            p.cp("v", aae(g), aaS[:, 512:513], r=[aaS_b], w=["wkvs"])
            p.stt("v", kmS, KpS, chcol(NP_EU, g), vsb, ALU.mult, ALU.mult,
                  r=[KpS_b, "chp", vsb_b], w=[kmS_b])
            p.tt("v", alS, PbS[:, 0:512], aaS[:, 0:512], ALU.mult,
                 r=[PbS_b, aaS_b], w=[alS_b])
            p.tt("v", alS, alS, kmS, ALU.add, r=[alS_b, kmS_b], w=[alS_b])
            p.stt("v", bvS, KpS, chcol(NP_EU, g), PbS[:, 0:512], ALU.mult, ALU.add,
                  r=[KpS_b, "chp", PbS_b], w=[bvS_b])
            p.act(bvS, bvS, AF.Ln, r=[bvS_b], w=[bvS_b])
            p.act(bvS, bvS, AF.Exp, r=[bvS_b], w=[bvS_b], scale=-1.0)
            p.tt("v", ybuf[:, g * 512:(g + 1) * 512], alS, bvS, ALU.mult,
                 r=[alS_b, bvS_b], w=[ybuf_b])
        for g in range(8):
            p.act(dscL, rfm[:, g * 512:(g + 1) * 512], AF.Sigmoid, r=[rfm_b], w=[dscL_b])
            p.tt("v", ryf[:, g * 512:(g + 1) * 512], dscL, ybuf[:, g * 512:(g + 1) * 512],
                 ALU.mult, r=[dscL_b, ybuf_b], w=[ryf_b])
        for jj in range(4):
            p.tr_group([(ptrb[:, g * 128:(g + 1) * 128],
                         ryf[:, g * 512 + jj * 128:g * 512 + (jj + 1) * 128],
                         ident[:, :]) for g in range(8)],
                       r=[ryf_b, "ident"], w=["ptrb"])
            p.tt("v", xn1[jj][0], xn1[jj][0], ptrb[:, :], ALU.add,
                 r=["ptrb", xn1[jj][1]], w=[xn1[jj][1]])
            p.red("v", lnrow[:, jj:jj + 1], xn1[jj][0], ALU.add, r=[xn1[jj][1]], w=["lnrow"])
            p.act(hnL, xn1[jj][0], AF.Square, r=[xn1[jj][1]], w=[hnL_b, "lnrow"],
                  accum=lnrow[:, 4 + jj:5 + jj])
        hs, hs_b = AR.alloc(f"hs{ch}", 95 * 1024, 24 * 1024, BF16)
        for hg in range(24):
            if hg % 2 == 0:
                hg2 = hg // 2
                wsg, wsgb = wsl[hg2 % 2]
                wsu, wsub = wsl[2 + hg2 % 2]
                p.dma("y", wsg[:, 0:2048].rearrange("np (two h) -> np two h", two=2),
                      wg_b[hg2 * 256:(hg2 + 1) * 256, :].rearrange(
                          "(two np) h -> np two h", np=128), wsgb, in_buf="wgb")
                p.dma("y", wsu[:, 0:2048].rearrange("np (two h) -> np two h", two=2),
                      wu_b[hg2 * 256:(hg2 + 1) * 256, :].rearrange(
                          "(two np) h -> np two h", np=128), wsub, in_buf="wub")
            wo = (hg % 2) * 1024
            psg, psgn = (PS0, "PS0") if hg % 2 == 0 else (PS2, "PS2")
            psu, psun = (PS1, "PS1") if hg % 2 == 0 else (PS3, "PS3")
            p.mm_group([(psg[:, :], wsg[:, wo + dc * 128:wo + (dc + 1) * 128],
                         xnf1[:, dc * 512:(dc + 1) * 512], dc == 0, dc == 7)
                        for dc in range(8)], r=[wsgb, xnf1_b], w=[psgn])
            p.mm_group([(psu[:, :], wsu[:, wo + dc * 128:wo + (dc + 1) * 128],
                         xnf1[:, dc * 512:(dc + 1) * 512], dc == 0, dc == 7)
                        for dc in range(8)], r=[wsub, xnf1_b], w=[psun])
            p.act(hnL[:, 0:512], psg[:, :], AF.Sigmoid, r=[psgn], w=[hnL_b],
                  scale=1.0 / 127.0)
            p.tt("v", hnL[:, 512:1024], hnL[:, 0:512], psg[:, :], ALU.mult,
                 r=[hnL_b, psgn], w=[hnL_b])
            p.tt("v", hs[:, hg * 512:(hg + 1) * 512], hnL[:, 512:1024], psu[:, :],
                 ALU.mult, r=[hnL_b, psun], w=[hs_b])
        kk, kk_b = AR.alloc(f"kk{ch}", 61 * 1024, 32 * 1024, BF16)
        ckf, ckf_b = AR.alloc(f"ckf{ch}", 126 * 1024, 8 * 1024, BF16)
        crf, crf_b = AR.alloc(f"crf{ch}", 150 * 1024, 8 * 1024, BF16)
        d3s = [AR.alloc(f"d3{ch}_{i}", 158 * 1024 + i * 4096, 4096, F32)
               for i in range(4)]
        dall = [(PS0, "PS0"), (PS1, "PS1"), (PS2, "PS2"), (PS3, "PS3")]
        for jp in range(2):
            jjs = (2 * jp, 2 * jp + 1)
            p.stream_begin(w=["PS0", "PS1", "PS2", "PS3"])
            for hg in range(24):
                if hg % 2 == 0:
                    hg2 = hg // 2
                    ws, wsb = wsl[hg2 % 4]
                    p.dma("y", ws[:, 0:2048].rearrange("np (two h) -> np two h", two=2),
                          wd_b[hg2 * 256:(hg2 + 1) * 256, :].rearrange(
                              "(two np) h -> np two h", np=128), wsb, in_buf="wdb")
                wo = (hg % 2) * 1024
                mms = []
                for ji, jj in enumerate(jjs):
                    lhs = hs[:, hg * 512 + jj * 128:hg * 512 + (jj + 1) * 128]
                    for hh in range(2):
                        mms.append((dall[ji * 2 + hh][0][:, :], lhs,
                                    ws[:, wo + hh * 512:wo + hh * 512 + 512],
                                    hg == 0, hg == 23))
                p.stream_step([hs_b, wsb], mms)
            p.stream_end()
            for ji, jj in enumerate(jjs):
                d3, d3_b = d3s[jp * 2 + ji]
                for hh in range(2):
                    ps, psn = dall[ji * 2 + hh]
                    p.ts("v", d3[:, hh * 512:(hh + 1) * 512], ps[:, :], 127.0 ** -3,
                         None, ALU.mult, r=[psn], w=[d3_b])
        u3s = [AR.alloc(f"u3{ch}_{i}", 120 * 1024 + i * 4096, 4096, F32)
               for i in range(2)]
        for jj in range(4):
            p.ts("v", rowm[:, 4:5], lnrow[:, jj:jj + 1], 1.0 / D, None, ALU.mult,
                 r=["lnrow"], w=["rowm"])
            p.ts("v", rowm[:, 5:6], lnrow[:, 4 + jj:5 + jj], 1.0 / D, None, ALU.mult,
                 r=["lnrow"], w=["rowm"])
            p.tt("v", rowm[:, 6:7], rowm[:, 4:5], rowm[:, 4:5], ALU.mult, r=["rowm"], w=["rowm"])
            p.tt("v", rowm[:, 6:7], rowm[:, 5:6], rowm[:, 6:7], ALU.subtract, r=["rowm"], w=["rowm"])
            p.act(rowm[:, 7:8], rowm[:, 6:7], AF.Ln, r=["rowm"], w=["rowm"], bias=LN_EPS)
            p.act(rowm[:, 7:8], rowm[:, 7:8], AF.Exp, r=["rowm"], w=["rowm"], scale=-0.5)
            p.tt("v", rowm[:, 8:9], rowm[:, 4:5], rowm[:, 7:8], ALU.mult, r=["rowm"], w=["rowm"])
            p.ts("v", rowm[:, 8:9], rowm[:, 8:9], -1.0, None, ALU.mult, r=["rowm"], w=["rowm"])
            p.act(hnL, xn1[jj][0], AF.Identity, r=[xn1[jj][1], "rowm"], w=[hnL_b],
                  bias=rowm[:, 8:9], scale=rowm[:, 7:8])
            for half in range(2):
                ps, psn = (PS3, "PS3") if half == 0 else (PSY, "PSY")
                p.tr_group([(ps[:, q4 * 128:(q4 + 1) * 128],
                             hnL[:, (half * 4 + q4) * 128:(half * 4 + q4 + 1) * 128],
                             identf[:, :]) for q4 in range(4)],
                           r=[hnL_b, "identf"], w=[psn])
                for q4 in range(4):
                    dc = half * 4 + q4
                    p.act(hfm[:, dc * HS + 1 + jj * 128:dc * HS + 1 + (jj + 1) * 128],
                          ps[:, q4 * 128:(q4 + 1) * 128], AF.Identity,
                          r=[psn, "chp"], w=[hfm_b],
                          bias=chcol(NP_B2, dc), scale=chcol(NP_G2, dc))
        for g in range(8):
            p.cp("v", hfm[:, g * HS:g * HS + 1], chcol(NP_H2B, g), r=["chp"], w=[hfm_b])
        for g in range(8):
            h_g = hfm[:, g * HS + 1:g * HS + 513]
            hp_g = hfm[:, g * HS:g * HS + 512]
            p.tt("v", hnL[:, 0:512], h_g, hp_g, ALU.subtract, r=[hfm_b], w=[hnL_b])
            p.ts("v", hnL[:, 512:1024], hnL[:, 0:512], chcol(NP_CK, g), None,
                 ALU.mult, r=[hnL_b, "chp"], w=[hnL_b])
            p.tt("g", ckf[:, g * 512:(g + 1) * 512], hnL[:, 512:1024], hp_g, ALU.add,
                 r=[hnL_b, hfm_b], w=[ckf_b])
            p.stt("v", crf[:, g * 512:(g + 1) * 512], hnL[:, 0:512], chcol(NP_CR, g),
                  hp_g, ALU.mult, ALU.add, r=[hnL_b, "chp", hfm_b], w=[crf_b])
        for g in range(8):
            p.cp("v", chcol(NP_H2B, g), hfm[:, g * HS + 512:g * HS + 513],
                 r=[hfm_b], w=["chp"])
        for hg in range(32):
            if hg % 2 == 0:
                hg2 = hg // 2
                ws, wsb = wsl[hg2 % 4]
                p.dma("y", ws[:, 0:2048].rearrange("np (two h) -> np two h", two=2),
                      wck_b[hg2 * 256:(hg2 + 1) * 256, :].rearrange(
                          "(two np) h -> np two h", np=128), wsb, in_buf="wckb")
            wo = (hg % 2) * 1024
            ps, psn = (PS0, "PS0") if hg % 2 == 0 else (PS1, "PS1")
            p.mm_group([(ps[:, :], ws[:, wo + dc * 128:wo + (dc + 1) * 128],
                         ckf[:, dc * 512:(dc + 1) * 512], dc == 0, dc == 7)
                        for dc in range(8)], r=[wsb, ckf_b], w=[psn])
            p.ts("v", hnL[:, 0:512], ps[:, :], 0.0, None, ALU.max, r=[psn], w=[hnL_b])
            p.tt("v", kk[:, hg * 512:(hg + 1) * 512], hnL[:, 0:512], hnL[:, 0:512],
                 ALU.mult, r=[hnL_b], w=[kk_b])
        for jp in range(2):
            jjs = (2 * jp, 2 * jp + 1)
            cvbank = {jjs[0]: [(PS0, "PS0"), (PS1, "PS1")],
                      jjs[1]: [(PS2, "PS2"), (PS3, "PS3")]}
            p.stream_begin(w=["PS0", "PS1", "PS2", "PS3"])
            for hg in range(32):
                if hg % 2 == 0:
                    hg2 = hg // 2
                    ws, wsb = wsl[hg2 % 4]
                    p.dma("y", ws[:, 0:2048].rearrange("np (two h) -> np two h", two=2),
                          wcv_b[hg2 * 256:(hg2 + 1) * 256, :].rearrange(
                              "(two np) h -> np two h", np=128), wsb, in_buf="wcvb")
                wo = (hg % 2) * 1024
                mms = []
                for jj in jjs:
                    lhs = kk[:, hg * 512 + jj * 128:hg * 512 + (jj + 1) * 128]
                    mms.append((cvbank[jj][0][0][:, :], lhs, ws[:, wo:wo + 512],
                                hg == 0, hg == 31))
                    mms.append((cvbank[jj][1][0][:, :], lhs, ws[:, wo + 512:wo + 1024],
                                hg == 0, hg == 31))
                p.stream_step([kk_b, wsb], mms)
            p.stream_end()
            for ji, jj in enumerate(jjs):
                j = ch * 4 + jj
                u3, u3_b = u3s[ji]
                p.stream_begin(w=["PSY", "pb2"])
                for dc in range(8):
                    ws, wsb = wsl[dc % 4]
                    p.dma("y", ws[:, 0:1024], wcr_b[dc * 128:(dc + 1) * 128, :], wsb,
                          in_buf="wcrb")
                    lhs = crf[:, dc * 512 + jj * 128:dc * 512 + (jj + 1) * 128]
                    p.stream_step([crf_b, wsb],
                                  [(PSY[:, :], lhs, ws[:, 0:512], dc == 0, dc == 7),
                                   (pb2[:, :], lhs, ws[:, 512:1024], dc == 0, dc == 7)])
                p.stream_end()
                for hh, psn in enumerate(["PSY", "pb2"]):
                    ps = PSY if hh == 0 else pb2
                    o = hh * 512
                    p.act(hnL[:, o:o + 512], ps[:, :], AF.Exp, r=[psn], w=[hnL_b],
                          scale=-1.0)
                    p.act(hnL[:, o:o + 512], hnL[:, o:o + 512], AF.Ln, r=[hnL_b],
                          w=[hnL_b], bias=1.0)
                    p.act(hnL[:, o:o + 512], hnL[:, o:o + 512], AF.Exp, r=[hnL_b],
                          w=[hnL_b], scale=-1.0)
                for hh in range(2):
                    ps, psn = cvbank[jj][hh]
                    o = hh * 512
                    p.tt("v", u3[:, o:o + 512], hnL[:, o:o + 512], ps[:, :], ALU.mult,
                         r=[hnL_b, psn], w=[u3_b])
                    p.tt("v", u3[:, o:o + 512], u3[:, o:o + 512],
                         xn1[jj][0][:, o:o + 512], ALU.add,
                         r=[u3_b, xn1[jj][1]], w=[u3_b])
                    p.tt("v", u3[:, o:o + 512], u3[:, o:o + 512],
                         x1L[jj][0][:, o:o + 512], ALU.add,
                         r=[u3_b, x1L[jj][1]], w=[u3_b])
            for ji, jj in enumerate(jjs):
                j = ch * 4 + jj
                u3, u3_b = u3s[ji]
                d3l, d3l_b = d3s[jp * 2 + ji]
                p.tt("v", u3, u3, d3l, ALU.add, r=[u3_b, d3l_b], w=[u3_b])
                p.dma("y", out_t[j * 128:(j + 1) * 128, :], u3, "out", in_buf=u3_b)

    p.final_wait("y", ["out"])
    return nc


def kernel(**inputs):
    f32 = np.float32

    def c(a, dt=np.float32):
        return np.ascontiguousarray(np.asarray(a), dtype=dt)

    x = c(inputs["L0_attn_x"])
    wqkvT = c(inputs["L0_attn_w"].T)
    w_out = c(inputs["L0_attn_w_out"])
    w1T = c(inputs["L0_ffn_W1"].T)
    b1v = c(inputs["L0_ffn_b1"]).reshape(1, H0)
    w2T = c(inputs["L0_ffn_W2"].T)
    b2v = c(inputs["L0_ffn_b2"]).reshape(1, D)
    tdv = c(inputs["L1_attn_time_decay"]).reshape(1, D)
    tfv = c(inputs["L1_attn_time_first"]).reshape(1, D)

    def fmH(w, ngrp):
        din, dout = w.shape
        ndc = din // 128
        r = w.reshape(ndc, 128, ngrp, 128)
        return np.ascontiguousarray(r.transpose(2, 1, 0, 3).reshape(ngrp * 128, ndc * 128))

    wtkH = fmH(c(inputs["L1_attn_w_tk"]), 8)
    wtvH = fmH(c(inputs["L1_attn_w_tv"]), 8)
    wtrH = fmH(c(inputs["L1_attn_w_tr"]), 8)
    wckH = fmH(c(inputs["L1_attn_w_ck"]), 32)
    wcv = c(inputs["L1_attn_w_cv"])
    wcr = c(inputs["L1_attn_w_cr"])
    l1g = c(inputs["L1_attn_ln1_g"]).reshape(1, D)
    l1b = c(inputs["L1_attn_ln1_b"]).reshape(1, D)
    l2g = c(inputs["L1_attn_ln2_g"]).reshape(1, D)
    l2b = c(inputs["L1_attn_ln2_b"]).reshape(1, D)
    mxk = c(inputs["L1_attn_mix_k"]).reshape(1, D)
    mxv = c(inputs["L1_attn_mix_v"]).reshape(1, D)
    mxr = c(inputs["L1_attn_mix_r"]).reshape(1, D)
    cmk = c(inputs["L1_attn_cmix_k"]).reshape(1, D)
    cmr = c(inputs["L1_attn_cmix_r"]).reshape(1, D)
    wgH = fmH(np.ascontiguousarray(inputs["L1_ffn_w_gate"].T), 24).astype(np.int32)
    wuH = fmH(np.ascontiguousarray(inputs["L1_ffn_w_up"].T), 24).astype(np.int32)
    wdT = c(inputs["L1_ffn_w_down"].T, np.int32)

    nc = build_program(BUILD_UPTO)
    shared = dict(wqkvT=wqkvT, w_out=w_out, w1T=w1T, b1v=b1v, w2T=w2T, b2v=b2v,
                  tdv=tdv, tfv=tfv, wtkH=wtkH, wtvH=wtvH, wtrH=wtrH, wckH=wckH,
                  wcv=wcv, wcr=wcr, l1g=l1g, l1b=l1b, l2g=l2g, l2b=l2b,
                  mxk=mxk, mxv=mxv, mxr=mxr, cmk=cmk, cmr=cmr,
                  wgH=wgH, wuH=wuH, wdT=wdT)
    in_maps = [dict(shared, x=np.ascontiguousarray(x[b])) for b in range(x.shape[0])]
    res = run_bass_kernel_spmd(nc, in_maps, core_ids=list(range(x.shape[0])))
    return np.stack([res.results[b]["o"] for b in range(x.shape[0])], axis=0)


if __name__ == "__main__":
    import reference
    ins = reference.setup_inputs()
    ins = {k: np.asarray(v) for k, v in ins.items()}
    out = kernel(**ins)
    print(out.shape, out.dtype)
